# version 01
# baseline (speedup 1.0000x reference)
import numpy as np
import ml_dtypes
import concourse.bass as bass
import concourse.mybir as mybir
from concourse.bass_utils import run_bass_kernel_spmd
from contextlib import ExitStack

AF = mybir.ActivationFunctionType
ALU = mybir.AluOpType
AX = mybir.AxisListType
F32, BF16, I32 = mybir.dt.float32, mybir.dt.bfloat16, mybir.dt.int32

NCORES = 8
BPC = 4
L = 2048
T = BPC * L
D = 1024
DI = 2048
NH = 32
Q = 128
NCH = L // Q
GRP = 4
NG = NCH // GRP
HEADS = 16
HD = 64
MZ = 256
HID = 2048
NT = T // 128
EPS_RMS = 1.1920929e-07
EPS_LN = 1e-5
TWO_PI = float(2.0 * np.pi)
BIG = 1e30


class CT:
    def __init__(self, nc, es, name):
        self.h = es.enter_context(nc.semaphore(name))
        self.n = 0
    def inc(self, inst, k=1):
        inst.then_inc(self.h, k)
        self.n += k
        return self.n


def _host_consts():
    c = {}
    le = np.where(np.arange(128)[:, None] <= np.arange(128)[None, :], 1.0, 0.0)
    c["c_cum"] = le.astype(np.float32)
    c["c_wmask"] = le.astype(ml_dtypes.bfloat16)
    c["c_idt16"] = np.eye(128, dtype=np.float32).astype(ml_dtypes.bfloat16)
    c["c_idt32"] = np.eye(128, dtype=np.float32)
    oh = np.zeros((32, 32 * 128), np.float32)
    for h in range(32):
        oh[h, h * 128:(h + 1) * 128] = 1.0
    c["c_onehot"] = oh
    inv2pi = (1.0 / 10000.0 ** (np.arange(16, dtype=np.float64) / 16.0)) / (2 * np.pi)
    c["c_if2pi"] = np.broadcast_to(inv2pi.astype(np.float32), (128, 32, 16)).reshape(128, 512).copy()
    am = np.zeros((4, 128, 512), np.float32)
    for i, dd in enumerate((0, 128, 256, 384)):
        am[i] = np.where(np.arange(128)[:, None] + dd <= np.arange(512)[None, :], 0.0, -30000.0)
    c["c_amask"] = np.ascontiguousarray(am.reshape(4 * 128, 512))
    ltm = np.zeros((4, 4), np.float32)
    for e in range(4):
        ltm[e, :e] = 1.0
    c["c_ltm"] = np.broadcast_to(ltm.reshape(1, 16), (128, 16)).copy()
    c["c_ones_f"] = np.ones((1, 128), np.float32)
    c["c_onescol"] = np.ones((128, 1), np.float32)
    return c


def _host_layout(inputs, core):
    g = lambda k: np.asarray(inputs[k])
    tr = lambda a: np.ascontiguousarray(np.asarray(a).T)
    d = {}
    d["x"] = np.ascontiguousarray(g("L0_attn_x")[core * BPC:(core + 1) * BPC].reshape(T, D))
    W_in = g("L0_attn_W_in")
    d["w_z"] = np.ascontiguousarray(W_in[:, :DI])
    d["w_x"] = np.ascontiguousarray(W_in[:, DI:2 * DI])
    d["w_bcd"] = np.ascontiguousarray(W_in[:, 2 * DI:])
    d["w_out"] = np.ascontiguousarray(g("L0_attn_W_out"))
    d["wq_t"] = tr(g("L0_ffn_wq")); d["wk_t"] = tr(g("L0_ffn_wk"))
    d["wv_t"] = tr(g("L0_ffn_wv")); d["wo_t"] = tr(g("L0_ffn_wo"))
    d["gw_t"] = tr(g("L0_ffn_gate_w"))
    d["e0_up_t"] = tr(g("L0_ffn_e0_up")); d["e0_dn_t"] = tr(g("L0_ffn_e0_down"))
    d["e1_w1_t"] = tr(g("L0_ffn_e1_w1")); d["e1_w2_t"] = tr(g("L0_ffn_e1_w2"))
    d["e1_dn_t"] = tr(g("L0_ffn_e1_down"))
    d["e2_l1_t"] = tr(g("L0_ffn_e2_l1")); d["e2_l2_t"] = tr(g("L0_ffn_e2_l2"))
    d["e2_l3_t"] = tr(g("L0_ffn_e2_l3")); d["e2_l4_t"] = tr(g("L0_ffn_e2_l4"))
    d["e3_dn_t"] = tr(g("L0_ffn_e3_down")); d["e3_up_t"] = tr(g("L0_ffn_e3_up"))
    d["e3_out_t"] = tr(g("L0_ffn_e3_out"))
    d["a_log_r"] = np.ascontiguousarray(g("L0_attn_A_log").reshape(1, NH))
    d["qn_r"] = np.ascontiguousarray(g("L0_ffn_qn_w").reshape(1, HD))
    d["kn_r"] = np.ascontiguousarray(g("L0_ffn_kn_w").reshape(1, HD))
    d["dtb_bc"] = np.ascontiguousarray(np.broadcast_to(g("L0_attn_dt_bias"), (128, NH)))
    d["dsk_bc"] = np.ascontiguousarray(np.broadcast_to(g("L0_attn_D_skip"), (128, NH)))
    d["n1t"] = np.ascontiguousarray(np.broadcast_to(g("L0_ffn_n1_w"), (128, D)))
    d["n2t"] = np.ascontiguousarray(np.broadcast_to(g("L0_ffn_n2_w"), (128, D)))
    d["qnt"] = np.ascontiguousarray(np.broadcast_to(np.tile(g("L0_ffn_qn_w"), HEADS), (128, D)))
    d["knt"] = np.ascontiguousarray(np.broadcast_to(np.tile(g("L0_ffn_kn_w"), HEADS), (128, D)))
    d["nw_c"] = np.ascontiguousarray(g("L0_attn_norm_weight").reshape(16, 128).T)
    return d


class Ctx:
    pass


def _declare(nc, dbg):
    cx = Ctx()
    cx.nc = nc
    cx.es = ExitStack()
    cx.SY, cx.TE, cx.VE, cx.SC, cx.GP = nc.sync, nc.tensor, nc.vector, nc.scalar, nc.gpsimd
    def din(name, shape, dt=F32):
        ap = nc.dram_tensor(name, list(shape), dt, kind="ExternalInput").ap()
        setattr(cx, name, ap)
    din("x", [T, D])
    din("w_z", [D, DI]); din("w_x", [D, DI]); din("w_bcd", [D, 160])
    din("w_out", [DI, D])
    din("wq_t", [D, D]); din("wk_t", [D, D]); din("wv_t", [D, D])
    din("wo_t", [D, D]); din("gw_t", [D, 4])
    din("e0_up_t", [D, HID]); din("e0_dn_t", [HID, D])
    din("e1_w1_t", [D, HID]); din("e1_w2_t", [D, HID]); din("e1_dn_t", [HID, D])
    din("e2_l1_t", [D, HID]); din("e2_l2_t", [HID, HID])
    din("e2_l3_t", [HID, HID]); din("e2_l4_t", [HID, D])
    din("e3_dn_t", [D, D]); din("e3_up_t", [D, HID]); din("e3_out_t", [HID, D])
    din("a_log_r", [1, NH]); din("qn_r", [1, HD]); din("kn_r", [1, HD])
    din("dtb_bc", [128, NH]); din("dsk_bc", [128, NH])
    din("n1t", [128, D]); din("n2t", [128, D]); din("qnt", [128, D]); din("knt", [128, D])
    din("nw_c", [128, 16])
    din("c_cum", [128, 128]); din("c_wmask", [128, 128], BF16)
    din("c_idt16", [128, 128], BF16); din("c_idt32", [128, 128])
    din("c_onehot", [32, 32 * 128]); din("c_if2pi", [128, 512])
    din("c_amask", [4 * 128, 512]); din("c_ltm", [128, 16])
    din("c_ones_f", [1, 128]); din("c_onescol", [128, 1])
    skind = "ExternalOutput" if dbg else "Internal"
    def dscr(name, shape, dt):
        ap = nc.dram_tensor(name, list(shape), dt, kind=skind).ap()
        setattr(cx, name, ap)
    dscr("xn_tm", [T, D], BF16)
    dscr("xn_tm32", [T, D], F32)
    dscr("xn_dm", [D, T], BF16)
    dscr("sz_tm", [T, DI], BF16)
    dscr("xs_tm", [T, DI], BF16)
    dscr("bc_tm", [T, 128], BF16)
    dscr("dtr_tm", [T, NH], F32)
    dscr("mamba_tm", [T, D], F32)
    dscr("q_dm", [D, T], BF16)
    dscr("k_dm", [D, T], BF16)
    dscr("v_tm", [T, D], BF16)
    dscr("ao_dm", [D, T], BF16)
    dscr("acc_tm", [T, D - MZ], F32)
    dscr("xf_dm", [D, T], BF16)
    dscr("wts_tm", [T, 4], F32)
    dscr("h2_dm", [HID, T], BF16)
    cx.out = nc.dram_tensor("out", [T, D], F32, kind="ExternalOutput").ap()
    def psb(name, shape, dt):
        setattr(cx, name, cx.es.enter_context(nc.sbuf_tensor(name, list(shape), dt)))
    psb("P_idt16", [128, 128], BF16)
    psb("P_idt32", [128, 128], F32)
    psb("P_cum", [128, 128], F32)
    psb("P_wmask", [128, 128], BF16)
    psb("P_onehot", [32, 32 * 128], F32)
    psb("P_if2pi", [128, 512], F32)
    psb("P_amask", [128, 4 * 512], F32)
    psb("P_ltm", [128, 16], F32)
    psb("P_ones_f", [1, 128], F32)
    psb("P_onescol", [128, 1], F32)
    psb("P_dtb", [128, NH], F32)
    psb("P_dsk", [128, NH], F32)
    psb("P_n1t", [128, D], BF16)
    psb("P_n2t", [128, D], BF16)
    psb("P_qnt", [128, D], BF16)
    psb("P_knt", [128, D], BF16)
    psb("P_nwc", [128, 16], F32)
    psb("P_A", [128, NH], F32)
    psb("P_negA", [128, NH], F32)
    psb("P_negc", [128, 1], F32)
    psb("P_tiny", [1, 512], F32)
    psb("P_rowA", [1, 128], F32)
    psb("P_rowNegA", [1, 128], F32)
    psb("P_rowNegc", [1, 128], F32)
    psb("P_const", [128, 2], F32)
    nc.const_aps.register(F32, EPS_LN, cx.P_const[:, 0:1])
    nc.const_aps.register(F32, EPS_RMS, cx.P_const[:, 1:2])
    return cx


def _init_stage(cx):
    nc = cx.nc
    SY, TE, VE, SC, GP = cx.SY, cx.TE, cx.VE, cx.SC, cx.GP
    with nc.Block("init") as blk:
        def emit(_):
            ies = ExitStack()
            dL = CT(nc, ies, "i_dL"); vC = CT(nc, ies, "i_vC")
            sC = CT(nc, ies, "i_sC"); tC = CT(nc, ies, "i_tC")
            pb = ies.enter_context(nc.psum_tensor("i_pb", [128, 512], F32))
            for dst, src in [(cx.P_cum, cx.c_cum), (cx.P_wmask, cx.c_wmask),
                             (cx.P_idt16, cx.c_idt16), (cx.P_idt32, cx.c_idt32),
                             (cx.P_onehot, cx.c_onehot), (cx.P_if2pi, cx.c_if2pi),
                             (cx.P_ltm, cx.c_ltm), (cx.P_ones_f, cx.c_ones_f),
                             (cx.P_onescol, cx.c_onescol), (cx.P_dtb, cx.dtb_bc),
                             (cx.P_dsk, cx.dsk_bc), (cx.P_nwc, cx.nw_c)]:
                dL.inc(SY.dma_start(out=dst[:], in_=src[:]), 16)
            dL.inc(SY.dma_start(out=cx.P_amask[:].rearrange("p (d n) -> p d n", n=512),
                                in_=cx.c_amask.rearrange("(d p) n -> p d n", p=128)), 16)
            for dst, src in [(cx.P_n1t, cx.n1t), (cx.P_n2t, cx.n2t),
                             (cx.P_qnt, cx.qnt), (cx.P_knt, cx.knt)]:
                dL.inc(GP.dma_start(out=dst[:], in_=src[:]), 16)
            vC.inc(VE.memset(cx.P_const[:, 0:1], EPS_LN))
            vC.inc(VE.memset(cx.P_const[:, 1:2], EPS_RMS))
            dL.inc(SY.dma_start(out=cx.P_tiny[0:1, 0:NH], in_=cx.a_log_r[:]), 16)
            dL.inc(SY.dma_start(out=cx.P_tiny[0:1, 32:96], in_=cx.qn_r[:]), 16)
            dL.inc(SY.dma_start(out=cx.P_tiny[0:1, 96:160], in_=cx.kn_r[:]), 16)
            VE.wait_ge(dL.h, dL.n); SC.wait_ge(dL.h, dL.n)
            sC.inc(SC.activation(cx.P_rowA[0:1, 0:NH], cx.P_tiny[0:1, 0:NH], AF.Exp))
            vC.inc(VE.reduce_max(cx.P_tiny[0:1, 160:161], cx.P_tiny[0:1, 32:96], axis=AX.X,
                                 apply_absolute_value=True))
            vC.inc(VE.reduce_max(cx.P_tiny[0:1, 161:162], cx.P_tiny[0:1, 96:160], axis=AX.X,
                                 apply_absolute_value=True))
            VE.wait_ge(vC.h, vC.n)
            vC.inc(VE.tensor_tensor(cx.P_tiny[0:1, 162:163], cx.P_tiny[0:1, 160:161],
                                    cx.P_tiny[0:1, 161:162], ALU.mult))
            VE.wait_ge(vC.h, vC.n)
            vC.inc(VE.tensor_scalar(cx.P_rowNegc[0:1, 0:1], cx.P_tiny[0:1, 162:163],
                                    -8.0, None, ALU.mult))
            SC.wait_ge(sC.h, sC.n)
            sC.inc(SC.mul(cx.P_rowNegA[0:1, 0:NH], cx.P_rowA[0:1, 0:NH], -1.0))
            TE.wait_ge(sC.h, sC.n); TE.wait_ge(vC.h, vC.n)
            tC.inc(TE.matmul(pb[:, 0:NH], cx.P_ones_f[:], cx.P_rowA[0:1, 0:NH],
                             start=True, stop=True))
            VE.wait_ge(tC.h, tC.n)
            vC.inc(VE.tensor_copy(cx.P_A[:], pb[:, 0:NH]))
            VE.wait_ge(vC.h, vC.n)
            TE.wait_ge(vC.h, vC.n)
            tC.inc(TE.matmul(pb[:, 0:NH], cx.P_ones_f[:], cx.P_rowNegA[0:1, 0:NH],
                             start=True, stop=True))
            VE.wait_ge(tC.h, tC.n)
            vC.inc(VE.tensor_copy(cx.P_negA[:], pb[:, 0:NH]))
            VE.wait_ge(vC.h, vC.n)
            TE.wait_ge(vC.h, vC.n)
            tC.inc(TE.matmul(pb[:, 0:1], cx.P_ones_f[:], cx.P_rowNegc[0:1, 0:1],
                             start=True, stop=True))
            VE.wait_ge(tC.h, tC.n)
            vC.inc(VE.tensor_copy(cx.P_negc[:], pb[:, 0:1]))
            SY.wait_ge(vC.h, vC.n)
            ies.close()
        blk.sync(emit)


def _s0(cx):
    nc = cx.nc
    SY, TE, VE, SC = cx.SY, cx.TE, cx.VE, cx.SC
    with nc.Block("s0") as blk:
        def emit(_):
            ies = ExitStack()
            dL = CT(nc, ies, "s0_dL"); dS = CT(nc, ies, "s0_dS")
            tC = CT(nc, ies, "s0_tC"); vC = CT(nc, ies, "s0_vC"); sC = CT(nc, ies, "s0_sC")
            xbuf = [ies.enter_context(nc.sbuf_tensor(f"s0_x{i}", [128, D], F32)) for i in range(2)]
            junk = ies.enter_context(nc.sbuf_tensor("s0_junk", [128, D], F32))
            st = ies.enter_context(nc.sbuf_tensor("s0_st", [128, 8], F32))
            xn16 = [ies.enter_context(nc.sbuf_tensor(f"s0_a{i}", [128, D], BF16)) for i in range(2)]
            xn32 = [ies.enter_context(nc.sbuf_tensor(f"s0_b{i}", [128, D], F32)) for i in range(2)]
            dmb = [ies.enter_context(nc.sbuf_tensor(f"s0_d{i}", [128, 8, 128], BF16)) for i in range(2)]
            ptp = [ies.enter_context(nc.psum_tensor(f"s0_p{i}", [128, 1024], BF16)) for i in range(2)]
            iter_v, iter_s = [], []
            tp_evac = [0, 0]
            for t in range(NT):
                bi = t % 2
                if t >= 2:
                    SY.wait_ge(vC.h, iter_v[t - 2])
                    SY.wait_ge(sC.h, iter_s[t - 2])
                dL.inc(SY.dma_start(out=xbuf[bi][:], in_=cx.x[t * 128:(t + 1) * 128, :]), 16)
                wL = dL.n
                VE.wait_ge(dL.h, wL); SC.wait_ge(dL.h, wL)
                vC.inc(VE.reduce_sum(st[:, 0:1], xbuf[bi][:], axis=AX.X))
                sC.inc(SC.activation(junk[:], xbuf[bi][:], AF.Square, accum_out=st[:, 1:2]))
                VE.wait_ge(sC.h, sC.n); VE.wait_ge(vC.h, vC.n)
                vC.inc(VE.tensor_scalar(st[:, 2:3], st[:, 0:1], 1.0 / D, None, ALU.mult))
                VE.wait_ge(vC.h, vC.n)
                vC.inc(VE.tensor_tensor(st[:, 3:4], st[:, 2:3], st[:, 2:3], ALU.mult))
                VE.wait_ge(vC.h, vC.n)
                vC.inc(VE.tensor_scalar(st[:, 4:5], st[:, 1:2], 1.0 / D, st[:, 3:4],
                                        ALU.mult, ALU.subtract))
                SC.wait_ge(vC.h, vC.n)
                sC.inc(SC.activation(st[:, 5:6], st[:, 4:5], AF.Ln, bias=EPS_LN))
                SC.wait_ge(sC.h, sC.n)
                sC.inc(SC.activation(st[:, 6:7], st[:, 5:6], AF.Exp, scale=-0.5))
                VE.wait_ge(sC.h, sC.n)
                vC.inc(VE.scalar_tensor_tensor(st[:, 7:8], st[:, 2:3], -1.0,
                                               st[:, 6:7], ALU.mult, ALU.mult))
                VE.wait_ge(vC.h, vC.n); SC.wait_ge(vC.h, vC.n)
                sC.inc(SC.activation(xn16[bi][:], xbuf[bi][:], AF.Identity,
                                     bias=st[:, 7:8], scale=st[:, 6:7]))
                vC.inc(VE.scalar_tensor_tensor(xn32[bi][:], xbuf[bi][:], st[:, 6:7],
                                               st[:, 7:8].broadcast_to([128, D]),
                                               ALU.mult, ALU.add))
                wS, wV = sC.n, vC.n
                TE.wait_ge(sC.h, wS)
                for k in range(8):
                    ks = k % 2
                    TE.wait_ge(vC.h, tp_evac[ks])
                    tC.inc(TE.matmul(ptp[ks][:, 0:128], xn16[bi][:, k * 128:(k + 1) * 128],
                                     cx.P_idt16[:], is_transpose=True, start=True, stop=True))
                    VE.wait_ge(tC.h, tC.n)
                    vC.inc(VE.tensor_copy(dmb[bi][:, k, :], ptp[ks][:, 0:128]))
                    tp_evac[ks] = vC.n
                SY.wait_ge(sC.h, wS)
                dS.inc(SY.dma_start(out=cx.xn_tm[t * 128:(t + 1) * 128, :], in_=xn16[bi][:]), 16)
                SY.wait_ge(vC.h, wV)
                dS.inc(SY.dma_start(out=cx.xn_tm32[t * 128:(t + 1) * 128, :], in_=xn32[bi][:]), 16)
                SY.wait_ge(vC.h, vC.n)
                dS.inc(SY.dma_start(
                    out=cx.xn_dm.rearrange("(k p) T2 -> p k T2", p=128)[:, :, t * 128:(t + 1) * 128],
                    in_=dmb[bi][:]), 16)
                iter_v.append(vC.n); iter_s.append(sC.n)
            SY.wait_ge(dS.h, dS.n)
            ies.close()
        blk.sync(emit)


def _s1(cx):
    nc = cx.nc
    SY, TE, VE, SC, GP = cx.SY, cx.TE, cx.VE, cx.SC, cx.GP
    with nc.Block("s1") as blk:
        def emit(_):
            ies = ExitStack()
            dW = CT(nc, ies, "s1_dW"); dL = CT(nc, ies, "s1_dL"); dS = CT(nc, ies, "s1_dS")
            tC = CT(nc, ies, "s1_tC"); vC = CT(nc, ies, "s1_vC"); sC = CT(nc, ies, "s1_sC")
            wz_sb = ies.enter_context(nc.sbuf_tensor("s1_wz", [128, 8, DI], BF16))
            wx_sb = ies.enter_context(nc.sbuf_tensor("s1_wx", [128, 8, DI], BF16))
            wb_sb = ies.enter_context(nc.sbuf_tensor("s1_wb", [128, 8, 160], BF16))
            xb = [ies.enter_context(nc.sbuf_tensor(f"s1_x{i}", [128, 8, 128], BF16)) for i in range(2)]
            zb = [ies.enter_context(nc.sbuf_tensor(f"s1_z{i}", [128, DI], BF16)) for i in range(2)]
            sxb = [ies.enter_context(nc.sbuf_tensor(f"s1_s{i}", [128, DI], BF16)) for i in range(2)]
            bcb = [ies.enter_context(nc.sbuf_tensor(f"s1_b{i}", [128, 128], BF16)) for i in range(2)]
            dtb = [ies.enter_context(nc.sbuf_tensor(f"s1_t{i}", [128, 32], F32)) for i in range(2)]
            pz = [ies.enter_context(nc.psum_tensor(f"s1_p{i}", [128, 512], F32)) for i in range(8)]
            dW.inc(GP.dma_start(out=wz_sb[:], in_=cx.w_z.rearrange("(k p) n -> p k n", p=128)), 16)
            dW.inc(GP.dma_start(out=wx_sb[:], in_=cx.w_x.rearrange("(k p) n -> p k n", p=128)), 16)
            dW.inc(GP.dma_start(out=wb_sb[:], in_=cx.w_bcd.rearrange("(k p) n -> p k n", p=128)), 16)
            iter_v, iter_s = [], []
            for t in range(NT):
                bi = t % 2
                if t >= 2:
                    SY.wait_ge(vC.h, iter_v[t - 2])
                    SY.wait_ge(sC.h, iter_s[t - 2])
                dL.inc(SY.dma_start(
                    out=xb[bi][:],
                    in_=cx.xn_dm.rearrange("(k p) T2 -> p k T2", p=128)[:, :, t * 128:(t + 1) * 128]), 16)
                wL = dL.n
                TE.wait_ge(dL.h, wL)
                if t == 0:
                    TE.wait_ge(dW.h, dW.n)
                if t >= 2:
                    TE.wait_ge(sC.h, iter_s[t - 2])
                    TE.wait_ge(vC.h, iter_v[t - 2])
                for k in range(8):
                    for n in range(8):
                        src = wz_sb[:, k, (n % 4) * 512:(n % 4 + 1) * 512] if n < 4 else \
                              wx_sb[:, k, (n - 4) * 512:(n - 3) * 512]
                        tC.inc(TE.matmul(pz[n][:], xb[bi][:, k, :], src,
                                         start=(k == 0), stop=(k == 7)))
                wT = tC.n
                SC.wait_ge(tC.h, wT); VE.wait_ge(tC.h, wT)
                for n in range(4):
                    sC.inc(SC.activation(zb[bi][:, n * 512:(n + 1) * 512], pz[n][:], AF.Silu))
                for n in range(4):
                    vC.inc(VE.tensor_copy(sxb[bi][:, n * 512:(n + 1) * 512], pz[4 + n][:]))
                wE_s, wE_v = sC.n, vC.n
                TE.wait_ge(sC.h, wE_s)
                for k in range(8):
                    tC.inc(TE.matmul(pz[0][:, 0:160], xb[bi][:, k, :], wb_sb[:, k, :],
                                     start=(k == 0), stop=(k == 7)))
                VE.wait_ge(tC.h, tC.n)
                vC.inc(VE.tensor_copy(bcb[bi][:], pz[0][:, 0:128]))
                vC.inc(VE.tensor_copy(dtb[bi][:], pz[0][:, 128:160]))
                wB = vC.n
                SY.wait_ge(sC.h, wE_s)
                dS.inc(SY.dma_start(out=cx.sz_tm[t * 128:(t + 1) * 128, :], in_=zb[bi][:]), 16)
                SY.wait_ge(vC.h, wE_v)
                dS.inc(SY.dma_start(out=cx.xs_tm[t * 128:(t + 1) * 128, :], in_=sxb[bi][:]), 16)
                SY.wait_ge(vC.h, wB)
                dS.inc(SY.dma_start(out=cx.bc_tm[t * 128:(t + 1) * 128, :], in_=bcb[bi][:]), 16)
                dS.inc(SY.dma_start(out=cx.dtr_tm[t * 128:(t + 1) * 128, :], in_=dtb[bi][:]), 16)
                iter_v.append(vC.n); iter_s.append(sC.n)
            SY.wait_ge(dS.h, dS.n)
            ies.close()
        blk.sync(emit)


S2DBG = False

def _s2(cx):
    nc = cx.nc
    SY, TE, VE, SC, GP = cx.SY, cx.TE, cx.VE, cx.SC, cx.GP
    dbgt = {}
    if S2DBG:
        for nm, sh, dt in [("dbg_dt4", [128, GRP * 32], F32), ("dbg_cumT", [128, GRP * 32], F32),
                           ("dbg_gcum", [128, GRP * 32], F32), ("dbg_biasT", [128, GRP * 32], F32),
                           ("dbg_lamT", [128, GRP * 32], F32), ("dbg_gdtT", [128, GRP * 32], F32),
                           ("dbg_sdup", [128, GRP * 1024], BF16), ("dbg_cdup", [128, GRP * 1024], BF16),
                           ("dbg_cnm", [32, GRP * 128], F32), ("dbg_dec", [64, 128], F32),
                           ("dbg_rot", [128, GRP * 128], BF16), ("dbg_bgdt", [128, GRP * 64], BF16),
                           ("dbg_cltm", [128, GRP * 64], BF16),
                           ("dbg_bnm", [64, 128], BF16), ("dbg_cnm2", [64, 128], BF16),
                           ("dbg_E", [128, 128], F32), ("dbg_Gm", [128, 128], BF16),
                           ("dbg_Wt", [128, 128], BF16), ("dbg_clnm", [64, 128], BF16),
                           ("dbg_H", [64, DI], BF16), ("dbg_y", [128, GRP * DI], BF16)]:
            dbgt[nm] = nc.dram_tensor(nm, sh, dt, kind="ExternalOutput").ap()
    with nc.Block("s2") as blk:
        def emit(_):
            ies = ExitStack()
            dW = CT(nc, ies, "s2_dW"); dL = CT(nc, ies, "s2_dL"); dS = CT(nc, ies, "s2_dS")
            tC = CT(nc, ies, "s2_tC"); vC = CT(nc, ies, "s2_vC"); sC = CT(nc, ies, "s2_sC")
            sbt = lambda n_, s, d: ies.enter_context(nc.sbuf_tensor(n_, list(s), d))
            wout_sb = sbt("s2_wo", [128, 16, D], BF16)
            dtr4 = sbt("s2_dtr", [128, GRP, 32], F32)
            bc4 = sbt("s2_bc", [128, GRP, 128], BF16)
            xs4 = sbt("s2_xs", [128, GRP, DI], BF16)
            sz4 = sbt("s2_sz", [128, GRP, DI], BF16)
            dt4 = sbt("s2_dt", [128, GRP, 32], F32)
            tmpa = sbt("s2_tmpa", [128, GRP, 32], F32)
            cumT = sbt("s2_cumT", [128, GRP * 32], F32)
            gcum = sbt("s2_gcum", [128, GRP * 32], F32)
            biasT = sbt("s2_biasT", [128, GRP * 32], F32)
            lamT = sbt("s2_lamT", [128, GRP * 32], F32)
            gdtT = sbt("s2_gdtT", [128, GRP * 32], F32)
            lndt = sbt("s2_lndt", [128, GRP * 32], F32)
            qa = sbt("s2_qa", [128, 2 * 512], F32)
            qi = sbt("s2_qi", [128, 2 * 512], I32)
            qf = sbt("s2_qf", [128, 2 * 512], F32)
            fr = sbt("s2_fr", [128, 2 * 512], F32)
            sdup = sbt("s2_sd", [128, GRP * 1024], BF16)
            cdup = sbt("s2_cd", [128, GRP * 1024], BF16)
            cnm = sbt("s2_cnm", [32, GRP * 128], F32)
            csrow = sbt("s2_cs", [1, 128], F32)
            offr = sbt("s2_off", [1, 160], F32)
            acdr = sbt("s2_acd", [1, 128], F32)
            decr = sbt("s2_dec", [1, 128], F32)
            dec_bc = sbt("s2_decb", [64, 128], F32)
            acd_bc = sbt("s2_acdb", [128, 128], F32)
            H = [sbt(f"s2_H{i}", [64, DI], BF16) for i in range(2)]
            rot = [sbt(f"s2_rot{i}", [128, GRP, 128], BF16) for i in range(2)]
            rtmp = sbt("s2_rtmp", [128, GRP, 32], F32)
            bgdt = [sbt(f"s2_bg{i}", [128, GRP, 64], BF16) for i in range(2)]
            cltm = [sbt(f"s2_cl{i}", [128, GRP, 64], BF16) for i in range(2)]
            bnm = [sbt(f"s2_bn{i}", [64, 128], BF16) for i in range(2)]
            cnm2 = [sbt(f"s2_cm{i}", [64, 128], BF16) for i in range(2)]
            clnm = [sbt(f"s2_cn{i}", [64, 128], BF16) for i in range(2)]
            Etile = [sbt(f"s2_E{i}", [128, 128], F32) for i in range(2)]
            Gm = [sbt(f"s2_Gm{i}", [128, 128], BF16) for i in range(2)]
            Wt = [sbt(f"s2_W{i}", [128, 128], BF16) for i in range(2)]
            ybuf = sbt("s2_y", [128, GRP, DI], BF16)
            yst = sbt("s2_yst", [128, GRP, 4], F32)
            ydmb = sbt("s2_ydm", [128, 16, 128], BF16)
            mo = [sbt(f"s2_mo{i}", [128, D], F32) for i in range(2)]
            pst = lambda n_, s, d=F32: ies.enter_context(nc.psum_tensor(n_, list(s), d))
            p_tpA = pst("s2_pta", [128, 1024], BF16)
            p_tpB = pst("s2_ptb", [128, 1024], BF16)
            p_R = [pst(f"s2_pr{i}", [128, 512]) for i in range(2)]
            p_G = [pst(f"s2_pg{i}", [128, 512]) for i in range(2)]
            p_Y = pst("s2_py", [128, 512])
            p_Sc = pst("s2_psc", [64, 512])
            p_sm = p_G[0]
            p_W = p_R
            dW.inc(GP.dma_start(out=wout_sb[:], in_=cx.w_out.rearrange("(k p) n -> p k n", p=128)), 16)
            VE.wait_ge(dW.h, dW.n)
            for k in range(16):
                vC.inc(VE.tensor_scalar(wout_sb[:, k, :], wout_sb[:, k, :],
                                        cx.P_nwc[:, k:k + 1], None, ALU.mult))
            wout_ready = vC.n
            mk_tpA = 0; mk_tpB = 0
            mk_R = [0, 0]; mk_G = [0, 0]; mk_Y = 0; mk_Sc = 0
            mk_rot = [0, 0]
            mk_nm = [0, 0]
            mk_W = [0, 0]
            mk_E = [0, 0]
            mk_mo = [0, 0]
            hcur = 1
            for b in range(BPC):
                VE.wait_ge(tC.h, tC.n)
                vC.inc(VE.memset(H[0][:], 0.0))
                vC.inc(VE.memset(H[1][:], 0.0))
                vC.inc(VE.memset(offr[:], 0.0))
                hinit = vC.n
                for g in range(NG):
                    g0 = b * L + g * GRP * Q
                    SY.wait_ge(vC.h, vC.n)
                    SY.wait_ge(sC.h, sC.n)
                    SY.wait_ge(tC.h, tC.n)
                    dL.inc(SY.dma_start(out=dtr4[:], in_=cx.dtr_tm[g0:g0 + 512, :]
                                        .rearrange("(c p) h -> p c h", p=128)), 16)
                    dL.inc(SY.dma_start(out=bc4[:], in_=cx.bc_tm[g0:g0 + 512, :]
                                        .rearrange("(c p) n -> p c n", p=128)), 16)
                    dL.inc(SY.dma_start(out=xs4[:], in_=cx.xs_tm[g0:g0 + 512, :]
                                        .rearrange("(c p) n -> p c n", p=128)), 16)
                    dL.inc(SY.dma_start(out=sz4[:], in_=cx.sz_tm[g0:g0 + 512, :]
                                        .rearrange("(c p) n -> p c n", p=128)), 16)
                    wL = dL.n
                    VE.wait_ge(dL.h, wL); SC.wait_ge(dL.h, wL)
                    VE.wait_ge(vC.h, hinit)
                    vC.inc(VE.tensor_tensor(dt4[:], dtr4[:],
                                            cx.P_dtb[:].unsqueeze(1).broadcast_to([128, GRP, 32]),
                                            ALU.add))
                    VE.wait_ge(vC.h, vC.n); SC.wait_ge(vC.h, vC.n)
                    sC.inc(SC.activation(tmpa[:], dt4[:], AF.Abs))
                    SC.wait_ge(sC.h, sC.n)
                    sC.inc(SC.activation(tmpa[:], tmpa[:], AF.Exp, scale=-1.0))
                    SC.wait_ge(sC.h, sC.n)
                    sC.inc(SC.activation(tmpa[:], tmpa[:], AF.Ln, bias=1.0))
                    VE.wait_ge(sC.h, sC.n)
                    vC.inc(VE.tensor_scalar(dt4[:], dt4[:], 0.0, None, ALU.max))
                    VE.wait_ge(vC.h, vC.n)
                    vC.inc(VE.tensor_tensor(dt4[:], dt4[:], tmpa[:], ALU.add))
                    VE.wait_ge(vC.h, vC.n)
                    wDT = vC.n
                    SC.wait_ge(vC.h, wDT)
                    sC.inc(SC.activation(lndt[:].rearrange("p (c h) -> p c h", h=32),
                                         dt4[:], AF.Ln))
                    TE.wait_ge(vC.h, max(wDT, mk_G[0], mk_W[0], mk_W[1]))
                    for c in range(GRP):
                        tC.inc(TE.matmul(p_sm[:, c * 32:(c + 1) * 32], cx.P_cum[:],
                                         dt4[:, c, :], start=(c == 0), stop=(c == GRP - 1)))
                    VE.wait_ge(tC.h, tC.n)
                    vC.inc(VE.tensor_copy(cumT[:], p_sm[:, 0:128]))
                    VE.wait_ge(vC.h, vC.n)
                    TE.wait_ge(vC.h, vC.n)
                    for c in range(GRP):
                        tC.inc(TE.matmul(p_sm[0:1, c * 32:(c + 1) * 32],
                                         cx.P_onescol[:], dt4[:, c, :],
                                         start=(c == 0), stop=(c == GRP - 1)))
                    VE.wait_ge(tC.h, tC.n)
                    vC.inc(VE.tensor_copy(csrow[:], p_sm[0:1, 0:128]))
                    VE.wait_ge(vC.h, vC.n)
                    vC.inc(VE.tensor_copy(offr[0:1, 0:32], offr[0:1, 128:160]))
                    for c in range(1, GRP):
                        VE.wait_ge(vC.h, vC.n)
                        vC.inc(VE.tensor_tensor(offr[0:1, c * 32:(c + 1) * 32],
                                                offr[0:1, (c - 1) * 32:c * 32],
                                                csrow[0:1, (c - 1) * 32:c * 32], ALU.add))
                    VE.wait_ge(vC.h, vC.n)
                    vC.inc(VE.tensor_tensor(offr[0:1, 128:160], offr[0:1, 96:128],
                                            csrow[0:1, 96:128], ALU.add))
                    vC.inc(VE.tensor_tensor(acdr[:].rearrange("o (c h) -> o c h", h=32),
                                            csrow[:].rearrange("o (c h) -> o c h", h=32),
                                            cx.P_rowA[0:1, 0:32].unsqueeze(1)
                                            .broadcast_to([1, GRP, 32]), ALU.mult))
                    VE.wait_ge(vC.h, vC.n)
                    wOFF = vC.n
                    SC.wait_ge(vC.h, wOFF)
                    sC.inc(SC.activation(decr[:], acdr[:], AF.Exp, scale=-1.0))
                    TE.wait_ge(vC.h, wOFF)
                    tC.inc(TE.matmul(p_sm[:, 0:128], cx.P_ones_f[:], offr[0:1, 0:128],
                                     start=True, stop=True))
                    VE.wait_ge(tC.h, tC.n)
                    vC.inc(VE.tensor_tensor(gcum[:], cumT[:], p_sm[:, 0:128], ALU.add))
                    VE.wait_ge(vC.h, vC.n)
                    TE.wait_ge(vC.h, vC.n)
                    tC.inc(TE.matmul(p_sm[:, 0:128], cx.P_ones_f[:], acdr[:],
                                     start=True, stop=True))
                    VE.wait_ge(tC.h, tC.n)
                    vC.inc(VE.tensor_copy(acd_bc[:], p_sm[:, 0:128]))
                    VE.wait_ge(vC.h, vC.n)
                    TE.wait_ge(vC.h, vC.n); TE.wait_ge(sC.h, sC.n)
                    tC.inc(TE.matmul(p_sm[0:64, 0:128], cx.P_ones_f[0:1, 0:64], decr[:],
                                     start=True, stop=True))
                    VE.wait_ge(tC.h, tC.n)
                    vC.inc(VE.tensor_copy(dec_bc[:], p_sm[0:64, 0:128]))
                    VE.wait_ge(vC.h, vC.n)
                    vC.inc(VE.tensor_tensor(biasT[:].rearrange("p (c h) -> p c h", h=32),
                                            cumT[:].rearrange("p (c h) -> p c h", h=32),
                                            cx.P_A[:, 0:32].unsqueeze(1)
                                            .broadcast_to([128, GRP, 32]), ALU.mult))
                    VE.wait_ge(vC.h, vC.n); SC.wait_ge(vC.h, vC.n)
                    sC.inc(SC.activation(lamT[:], biasT[:], AF.Exp, scale=-1.0))
                    VE.wait_ge(sC.h, sC.n)
                    vC.inc(VE.tensor_tensor(biasT[:], biasT[:], lndt[:], ALU.add))
                    VE.wait_ge(vC.h, vC.n)
                    vC.inc(VE.tensor_tensor(gdtT[:], biasT[:], acd_bc[:], ALU.subtract))
                    VE.wait_ge(vC.h, vC.n); SC.wait_ge(vC.h, vC.n)
                    sC.inc(SC.activation(gdtT[:], gdtT[:], AF.Exp))
                    for half in range(2):
                        cc = (2 * half, 2 * half + 1)
                        SC_mark = sC.n
                        VE.wait_ge(sC.h, SC_mark)
                        for ci, c in enumerate(cc):
                            vC.inc(VE.tensor_tensor(
                                qa[:, ci * 512:(ci + 1) * 512].rearrange("p (h j) -> p h j", j=16),
                                gcum[:, c * 32:(c + 1) * 32].unsqueeze(2).broadcast_to([128, 32, 16]),
                                cx.P_if2pi[:].rearrange("p (h j) -> p h j", j=16), ALU.mult))
                        VE.wait_ge(vC.h, vC.n)
                        vC.inc(VE.tensor_copy(qi[:], qa[:]))
                        VE.wait_ge(vC.h, vC.n)
                        vC.inc(VE.tensor_copy(qf[:], qi[:]))
                        VE.wait_ge(vC.h, vC.n)
                        vC.inc(VE.tensor_tensor(fr[:], qa[:], qf[:], ALU.subtract))
                        VE.wait_ge(vC.h, vC.n)
                        SC.wait_ge(vC.h, vC.n)
                        for ci, c in enumerate(cc):
                            for dup in range(2):
                                sC.inc(SC.activation(
                                    sdup[:, c * 1024:(c + 1) * 1024]
                                        .rearrange("p (h d j) -> p h d j", d=2, j=16)[:, :, dup, :],
                                    fr[:, ci * 512:(ci + 1) * 512].rearrange("p (h j) -> p h j", j=16),
                                    AF.Sin, scale=TWO_PI))
                        vC.inc(VE.tensor_scalar(qa[:], qa[:], 0.25, None, ALU.add))
                        VE.wait_ge(vC.h, vC.n)
                        vC.inc(VE.tensor_copy(qi[:], qa[:]))
                        VE.wait_ge(vC.h, vC.n)
                        vC.inc(VE.tensor_copy(qf[:], qi[:]))
                        VE.wait_ge(vC.h, vC.n)
                        VE.wait_ge(sC.h, sC.n)
                        vC.inc(VE.tensor_tensor(fr[:], qa[:], qf[:], ALU.subtract))
                        VE.wait_ge(vC.h, vC.n)
                        SC.wait_ge(vC.h, vC.n)
                        for ci, c in enumerate(cc):
                            for dup in range(2):
                                sC.inc(SC.activation(
                                    cdup[:, c * 1024:(c + 1) * 1024]
                                        .rearrange("p (h d j) -> p h d j", d=2, j=16)[:, :, dup, :],
                                    fr[:, ci * 512:(ci + 1) * 512].rearrange("p (h j) -> p h j", j=16),
                                    AF.Sin, scale=TWO_PI))
                    TE.wait_ge(vC.h, max(vC.n, mk_W[0], mk_W[1]))
                    TE.wait_ge(sC.h, max(mk_E[0], mk_E[1]))
                    for c in range(GRP):
                        tC.inc(TE.matmul(p_R[0][0:32, c * 128:(c + 1) * 128],
                                         cumT[:, c * 32:(c + 1) * 32], cx.P_idt32[:],
                                         is_transpose=True, start=(c == 0), stop=(c == GRP - 1)))
                    VE.wait_ge(tC.h, tC.n)
                    vC.inc(VE.tensor_copy(cnm[:], p_R[0][0:32, 0:512]))
                    mk_R[0] = 10 ** 9
                    wA_v, wA_s = vC.n, sC.n
                    mk_R[0] = wA_v
                    if S2DBG and b == 0 and g == 0:
                        SY.wait_ge(vC.h, vC.n); SY.wait_ge(sC.h, sC.n)
                        for nm, tt in [("dbg_dt4", dt4), ("dbg_cumT", cumT), ("dbg_gcum", gcum),
                                       ("dbg_biasT", biasT), ("dbg_lamT", lamT),
                                       ("dbg_gdtT", gdtT), ("dbg_sdup", sdup),
                                       ("dbg_cdup", cdup), ("dbg_cnm", cnm), ("dbg_dec", dec_bc)]:
                            dS.inc(SY.dma_start(out=dbgt[nm][:], in_=tt[:].rearrange("p a b -> p (a b)") if tt[:].ndim == 3 else tt[:]), 16)
                    for h in range(NH):
                        hb = h % 2
                        VE.wait_ge(tC.h, mk_rot[hb])
                        VE.wait_ge(sC.h, wA_s)
                        co2 = cdup[:].rearrange("p (c h2 x) -> p c h2 x", h2=32, x=32)[:, :, h, :] \
                            .rearrange("p c (g j) -> p c g j", g=2)
                        si2 = sdup[:].rearrange("p (c h2 x) -> p c h2 x", h2=32, x=32)[:, :, h, :] \
                            .rearrange("p c (g j) -> p c g j", g=2)
                        bview = bc4[:].rearrange("p c (g n) -> p c g n", g=2)
                        rview = rot[hb][:].rearrange("p c (g n) -> p c g n", g=2)
                        bev, bod = bview[:, :, :, 0:32:2], bview[:, :, :, 1:32:2]
                        rev, rod = rview[:, :, :, 0:32:2], rview[:, :, :, 1:32:2]
                        rt = rtmp[:].rearrange("p c (g j) -> p c g j", g=2)
                        vC.inc(VE.tensor_copy(rview[:, :, :, 32:64], bview[:, :, :, 32:64]))
                        vC.inc(VE.tensor_tensor(rev, bev, co2, ALU.mult))
                        vC.inc(VE.tensor_tensor(rt, bod, si2, ALU.mult))
                        VE.wait_ge(vC.h, vC.n)
                        vC.inc(VE.tensor_tensor(rev, rev, rt, ALU.add))
                        vC.inc(VE.tensor_tensor(rod, bod, co2, ALU.mult))
                        vC.inc(VE.tensor_tensor(rt, bev, si2, ALU.mult))
                        VE.wait_ge(vC.h, vC.n)
                        vC.inc(VE.tensor_tensor(rod, rod, rt, ALU.subtract))
                        VE.wait_ge(vC.h, vC.n)
                        vC.inc(VE.tensor_tensor(
                            bgdt[hb][:], rot[hb][:, :, 0:64],
                            gdtT[:].rearrange("p (c h2) -> p c h2", h2=32)[:, :, h:h + 1]
                                .broadcast_to([128, GRP, 64]), ALU.mult))
                        vC.inc(VE.tensor_tensor(
                            cltm[hb][:], rot[hb][:, :, 64:128],
                            lamT[:].rearrange("p (c h2) -> p c h2", h2=32)[:, :, h:h + 1]
                                .broadcast_to([128, GRP, 64]), ALU.mult))
                        VE.wait_ge(vC.h, vC.n)
                        wROT = vC.n
                        for c in range(GRP):
                            cs = c % 2
                            TE.wait_ge(vC.h, max(wROT, mk_nm[0], mk_nm[1]))
                            tC.inc(TE.matmul(p_tpA[:, 0:128],
                                             rot[hb][:, c, :], cx.P_idt16[:],
                                             is_transpose=True, start=True, stop=True))
                            tC.inc(TE.matmul(p_tpB[0:64, 0:128],
                                             cltm[hb][:, c, :], cx.P_idt16[:],
                                             is_transpose=True, start=True, stop=True))
                            wTP = tC.n
                            VE.wait_ge(tC.h, wTP)
                            vC.inc(VE.tensor_copy(bnm[cs][:], p_tpA[0:64, 0:128]))
                            vC.inc(VE.tensor_copy(cnm2[cs][:], p_tpA[64:128, 0:128]))
                            vC.inc(VE.tensor_copy(clnm[cs][:], p_tpB[0:64, 0:128]))
                            wNM = vC.n
                            mk_nm[cs] = wNM
                            TE.wait_ge(sC.h, mk_E[cs])
                            TE.wait_ge(vC.h, mk_W[cs])
                            tC.inc(TE.matmul(p_R[cs][:, 0:128],
                                             cx.P_onehot[:, h * 128:(h + 1) * 128],
                                             cnm[:, c * 128:(c + 1) * 128],
                                             start=True, stop=True))
                            wR = tC.n
                            SC.wait_ge(tC.h, wR)
                            sC.inc(SC.activation(Etile[cs][:], p_R[cs][:, 0:128], AF.Exp,
                                                 scale=cx.P_negA[:, h:h + 1],
                                                 bias=biasT[:, c * 32 + h:c * 32 + h + 1]))
                            mk_E[cs] = sC.n
                            TE.wait_ge(vC.h, wNM)
                            TE.wait_ge(vC.h, mk_G[cs])
                            tC.inc(TE.matmul(p_G[cs][:, 0:128], bnm[cs][:],
                                             cnm2[cs][:], start=True, stop=True))
                            wG = tC.n
                            VE.wait_ge(tC.h, wG)
                            vC.inc(VE.scalar_tensor_tensor(Gm[cs][:], p_G[cs][:, 0:128], 1.0,
                                                           cx.P_wmask[:], ALU.mult, ALU.mult))
                            mk_G[cs] = vC.n
                            VE.wait_ge(vC.h, vC.n); VE.wait_ge(sC.h, mk_E[cs])
                            vC.inc(VE.scalar_tensor_tensor(Wt[cs][:], Etile[cs][:], BIG,
                                                           Gm[cs][:], ALU.min, ALU.mult))
                            wW = vC.n
                            TE.wait_ge(vC.h, max(wW, mk_Y, mk_Sc))
                            tC.inc(TE.matmul(p_Y[:, 0:64], Wt[cs][:],
                                             xs4[:, c, h * 64:(h + 1) * 64],
                                             start=True, stop=False))
                            tC.inc(TE.matmul(p_Y[:, 0:64], clnm[cs][:],
                                             H[hcur][:, h * 64:(h + 1) * 64],
                                             start=False, stop=True))
                            tC.inc(TE.matmul(p_Sc[:, 0:64], bgdt[hb][:, c, :],
                                             xs4[:, c, h * 64:(h + 1) * 64],
                                             start=True, stop=True))
                            wYS = tC.n
                            VE.wait_ge(tC.h, wYS)
                            vC.inc(VE.scalar_tensor_tensor(
                                H[1 - hcur][:, h * 64:(h + 1) * 64],
                                H[hcur][:, h * 64:(h + 1) * 64],
                                dec_bc[:, c * 32 + h:c * 32 + h + 1],
                                p_Sc[:, 0:64], ALU.mult, ALU.add))
                            mk_Sc = vC.n
                            vC.inc(VE.scalar_tensor_tensor(
                                ybuf[:, c, h * 64:(h + 1) * 64],
                                xs4[:, c, h * 64:(h + 1) * 64],
                                cx.P_dsk[:, h:h + 1],
                                p_Y[:, 0:64], ALU.mult, ALU.add))
                            mk_Y = vC.n
                            if S2DBG and b == 0 and g == 0 and h == 0 and c == 0:
                                SY.wait_ge(vC.h, vC.n); SY.wait_ge(sC.h, sC.n)
                                for nm, tt in [("dbg_rot", rot[hb]), ("dbg_bgdt", bgdt[hb]),
                                               ("dbg_cltm", cltm[hb]), ("dbg_bnm", bnm[cs]),
                                               ("dbg_cnm2", cnm2[cs]), ("dbg_E", Etile[cs]),
                                               ("dbg_Gm", Gm[cs]), ("dbg_Wt", Wt[cs]),
                                               ("dbg_clnm", clnm[cs]),
                                               ("dbg_H", H[1 - hcur])]:
                                    dS.inc(SY.dma_start(out=dbgt[nm][:], in_=tt[:].rearrange("p a b -> p (a b)") if tt[:].ndim == 3 else tt[:]), 16)
                            hcur = 1 - hcur
                        mk_rot[hb] = tC.n
                    VE.wait_ge(vC.h, vC.n)
                    vC.inc(VE.tensor_tensor(ybuf[:], ybuf[:], sz4[:], ALU.mult))
                    VE.wait_ge(vC.h, vC.n)
                    SC.wait_ge(vC.h, vC.n)
                    for c in range(GRP):
                        jdst = sdup if c < 2 else cdup
                        sC.inc(SC.activation(jdst[:, (c % 2) * 2048:(c % 2) * 2048 + 2048],
                                             ybuf[:, c, :], AF.Square,
                                             accum_out=yst[:, c, 0:1]))
                    SC.wait_ge(sC.h, sC.n)
                    for c in range(GRP):
                        sC.inc(SC.activation(yst[:, c, 1:2], yst[:, c, 0:1], AF.Ln,
                                             scale=1.0 / DI, bias=EPS_LN))
                    SC.wait_ge(sC.h, sC.n)
                    for c in range(GRP):
                        sC.inc(SC.activation(yst[:, c, 2:3], yst[:, c, 1:2], AF.Exp, scale=-0.5))
                    VE.wait_ge(sC.h, sC.n)
                    vC.inc(VE.tensor_tensor(ybuf[:], ybuf[:],
                                            yst[:, :, 2:3].broadcast_to([128, GRP, DI]),
                                            ALU.mult))
                    VE.wait_ge(vC.h, vC.n)
                    wYGN = vC.n
                    if S2DBG and b == 0 and g == 0:
                        SY.wait_ge(vC.h, vC.n)
                        dS.inc(SY.dma_start(out=dbgt["dbg_y"][:],
                                            in_=ybuf[:].rearrange("p a b -> p (a b)")), 16)
                    for c in range(GRP):
                        TE.wait_ge(vC.h, wYGN)
                        if b == 0 and g == 0 and c == 0:
                            TE.wait_ge(vC.h, wout_ready)
                        for k in range(16):
                            ks = k % 2
                            TE.wait_ge(vC.h, mk_tpA if ks == 0 else mk_tpB)
                            dstp = p_tpA if ks == 0 else p_tpB
                            tC.inc(TE.matmul(dstp[:, 0:128], ybuf[:, c, k * 128:(k + 1) * 128],
                                             cx.P_idt16[:], is_transpose=True,
                                             start=True, stop=True))
                            VE.wait_ge(tC.h, tC.n)
                            vC.inc(VE.tensor_copy(ydmb[:, k, :], dstp[:, 0:128]))
                            if ks == 0:
                                mk_tpA = vC.n
                            else:
                                mk_tpB = vC.n
                        wYD = vC.n
                        TE.wait_ge(vC.h, wYD)
                        TE.wait_ge(vC.h, max(mk_W[0], mk_W[1]))
                        TE.wait_ge(sC.h, max(mk_E[0], mk_E[1]))
                        for k in range(16):
                            for n in range(2):
                                tC.inc(TE.matmul(p_W[n][:, 0:512], ydmb[:, k, :],
                                                 wout_sb[:, k, n * 512:(n + 1) * 512],
                                                 start=(k == 0), stop=(k == 15)))
                        VE.wait_ge(tC.h, tC.n)
                        VE.wait_ge(dS.h, mk_mo[c % 2])
                        vC.inc(VE.tensor_copy(mo[c % 2][:, 0:512], p_W[0][:, 0:512]))
                        vC.inc(VE.tensor_copy(mo[c % 2][:, 512:1024], p_W[1][:, 0:512]))
                        mk_W[0] = vC.n; mk_W[1] = vC.n
                        r0 = g0 + c * 128
                        SY.wait_ge(vC.h, vC.n)
                        dS.inc(SY.dma_start(out=cx.mamba_tm[r0:r0 + 128, :], in_=mo[c % 2][:]), 16)
                        mk_mo[c % 2] = dS.n
            SY.wait_ge(dS.h, dS.n)
            ies.close()
        blk.sync(emit)


S3ABL = ("h", "tp", "v", "qk")

def _s3(cx):
    nc = cx.nc
    SY, TE, VE, SC, GP = cx.SY, cx.TE, cx.VE, cx.SC, cx.GP
    with nc.Block("s3") as blk:
        def emit(_):
            ies = ExitStack()
            dW = CT(nc, ies, "s3_dW"); dL = CT(nc, ies, "s3_dL"); dS = CT(nc, ies, "s3_dS")
            tC = CT(nc, ies, "s3_tC"); vC = CT(nc, ies, "s3_vC"); sC = CT(nc, ies, "s3_sC")
            wq_sb = ies.enter_context(nc.sbuf_tensor("s3_wq", [128, 8, D], BF16))
            wk_sb = ies.enter_context(nc.sbuf_tensor("s3_wk", [128, 8, D], BF16))
            wv_sb = ies.enter_context(nc.sbuf_tensor("s3_wv", [128, 8, D], BF16))
            xb = [ies.enter_context(nc.sbuf_tensor(f"s3_x{i}", [128, D], BF16)) for i in range(2)]
            st = ies.enter_context(nc.sbuf_tensor("s3_st", [128, 40], F32))
            h16 = [ies.enter_context(nc.sbuf_tensor(f"s3_h{i}", [128, D], BF16)) for i in range(2)]
            hdm = [ies.enter_context(nc.sbuf_tensor(f"s3_hd{i}", [128, 8, 128], BF16)) for i in range(2)]
            sq = ies.enter_context(nc.sbuf_tensor("s3_sq", [128, D], F32))
            junk = ies.enter_context(nc.sbuf_tensor("s3_j", [128, D], F32))
            qtm = ies.enter_context(nc.sbuf_tensor("s3_qtm", [128, D], BF16))
            qdmb = ies.enter_context(nc.sbuf_tensor("s3_qd", [128, 8, 128], BF16))
            vtm = [ies.enter_context(nc.sbuf_tensor(f"s3_v{i}", [128, D], BF16)) for i in range(2)]
            ptp = [ies.enter_context(nc.psum_tensor(f"s3_p{i}", [128, 1024], BF16)) for i in range(2)]
            pmm = [ies.enter_context(nc.psum_tensor(f"s3_m{i}", [128, 512], F32)) for i in range(2)]
            dW.inc(GP.dma_start(out=wq_sb[:], in_=cx.wq_t.rearrange("(k p) n -> p k n", p=128)), 16)
            dW.inc(GP.dma_start(out=wk_sb[:], in_=cx.wk_t.rearrange("(k p) n -> p k n", p=128)), 16)
            dW.inc(GP.dma_start(out=wv_sb[:], in_=cx.wv_t.rearrange("(k p) n -> p k n", p=128)), 16)
            iter_v, iter_s = [], []
            mk_tp = [0, 0]
            for t in range(NT):
                bi = t % 2
                if t >= 2:
                    SY.wait_ge(vC.h, iter_v[t - 2])
                    SY.wait_ge(sC.h, iter_s[t - 2])
                dL.inc(SY.dma_start(out=xb[bi][:], in_=cx.xn_tm[t * 128:(t + 1) * 128, :]), 16)
                wL = dL.n
                if "h" not in S3ABL:
                    iter_v.append(vC.n); iter_s.append(sC.n)
                    continue
                SC.wait_ge(dL.h, wL)
                sC.inc(SC.activation(junk[:], xb[bi][:], AF.Square, accum_out=st[:, 0:1]))
                SC.wait_ge(sC.h, sC.n)
                sC.inc(SC.activation(st[:, 1:2], st[:, 0:1], AF.Ln, scale=1.0 / D, bias=EPS_RMS))
                SC.wait_ge(sC.h, sC.n)
                sC.inc(SC.activation(st[:, 2:3], st[:, 1:2], AF.Exp, scale=-0.5))
                VE.wait_ge(sC.h, sC.n); VE.wait_ge(dL.h, wL)
                vC.inc(VE.tensor_scalar(h16[bi][:], xb[bi][:], st[:, 2:3], None, ALU.mult))
                VE.wait_ge(vC.h, vC.n)
                vC.inc(VE.tensor_tensor(h16[bi][:], h16[bi][:], cx.P_n1t[:], ALU.mult))
                VE.wait_ge(vC.h, vC.n)
                wH = vC.n
                if "tp" not in S3ABL:
                    iter_v.append(vC.n); iter_s.append(sC.n)
                    continue
                TE.wait_ge(vC.h, wH)
                if t == 0:
                    TE.wait_ge(dW.h, dW.n)
                for k in range(8):
                    ks = k % 2
                    TE.wait_ge(vC.h, mk_tp[ks])
                    tC.inc(TE.matmul(ptp[ks][:, 0:128], h16[bi][:, k * 128:(k + 1) * 128],
                                     cx.P_idt16[:], is_transpose=True, start=True, stop=True))
                    VE.wait_ge(tC.h, tC.n)
                    vC.inc(VE.tensor_copy(hdm[bi][:, k, :], ptp[ks][:, 0:128]))
                    mk_tp[ks] = vC.n
                wHD = vC.n
                if "v" not in S3ABL:
                    iter_v.append(vC.n); iter_s.append(sC.n)
                    continue
                TE.wait_ge(vC.h, wHD)
                if t >= 2:
                    TE.wait_ge(vC.h, iter_v[t - 2])
                for n in range(2):
                    for k in range(8):
                        tC.inc(TE.matmul(pmm[n][:], hdm[bi][:, k, :],
                                         wv_sb[:, k, n * 512:(n + 1) * 512],
                                         start=(k == 0), stop=(k == 7)))
                VE.wait_ge(tC.h, tC.n)
                vC.inc(VE.tensor_copy(vtm[bi][:, 0:512], pmm[0][:]))
                vC.inc(VE.tensor_copy(vtm[bi][:, 512:1024], pmm[1][:]))
                SY.wait_ge(vC.h, vC.n)
                dS.inc(SY.dma_start(out=cx.v_tm[t * 128:(t + 1) * 128, :], in_=vtm[bi][:]), 16)
                if "qk" not in S3ABL:
                    iter_v.append(vC.n); iter_s.append(sC.n)
                    continue
                for wsb, ntile, dst in ((wq_sb, cx.P_qnt, cx.q_dm), (wk_sb, cx.P_knt, cx.k_dm)):
                    TE.wait_ge(vC.h, vC.n)
                    for n in range(2):
                        for k in range(8):
                            tC.inc(TE.matmul(pmm[n][:], hdm[bi][:, k, :],
                                             wsb[:, k, n * 512:(n + 1) * 512],
                                             start=(k == 0), stop=(k == 7)))
                    wQ = tC.n
                    SC.wait_ge(tC.h, wQ)
                    sC.inc(SC.activation(sq[:, 0:512], pmm[0][:], AF.Square))
                    sC.inc(SC.activation(sq[:, 512:1024], pmm[1][:], AF.Square))
                    VE.wait_ge(sC.h, sC.n)
                    for hh in range(16):
                        vC.inc(VE.reduce_sum(st[:, 8 + hh:9 + hh],
                                             sq[:, hh * 64:(hh + 1) * 64], axis=AX.X))
                    VE.wait_ge(vC.h, vC.n)
                    SC.wait_ge(vC.h, vC.n)
                    sC.inc(SC.activation(st[:, 24:40], st[:, 8:24], AF.Ln,
                                         scale=1.0 / HD, bias=EPS_RMS))
                    SC.wait_ge(sC.h, sC.n)
                    sC.inc(SC.activation(st[:, 24:40], st[:, 24:40], AF.Exp, scale=-0.5))
                    VE.wait_ge(sC.h, sC.n)
                    vC.inc(VE.tensor_tensor(
                        qtm[:, 0:512].rearrange("p (h x) -> p h x", x=64),
                        pmm[0][:].rearrange("p (h x) -> p h x", x=64),
                        st[:, 24:32].unsqueeze(2).broadcast_to([128, 8, 64]), ALU.mult))
                    vC.inc(VE.tensor_tensor(
                        qtm[:, 512:1024].rearrange("p (h x) -> p h x", x=64),
                        pmm[1][:].rearrange("p (h x) -> p h x", x=64),
                        st[:, 32:40].unsqueeze(2).broadcast_to([128, 8, 64]), ALU.mult))
                    VE.wait_ge(vC.h, vC.n)
                    vC.inc(VE.tensor_tensor(qtm[:], qtm[:], ntile[:], ALU.mult))
                    VE.wait_ge(vC.h, vC.n)
                    wQT = vC.n
                    TE.wait_ge(vC.h, wQT)
                    for k in range(8):
                        ks = k % 2
                        TE.wait_ge(vC.h, mk_tp[ks])
                        tC.inc(TE.matmul(ptp[ks][:, 0:128], qtm[:, k * 128:(k + 1) * 128],
                                         cx.P_idt16[:], is_transpose=True, start=True, stop=True))
                        VE.wait_ge(tC.h, tC.n)
                        vC.inc(VE.tensor_copy(qdmb[:, k, :], ptp[ks][:, 0:128]))
                        mk_tp[ks] = vC.n
                    SY.wait_ge(vC.h, vC.n)
                    dS.inc(SY.dma_start(
                        out=dst.rearrange("(k p) T2 -> p k T2", p=128)[:, :, t * 128:(t + 1) * 128],
                        in_=qdmb[:]), 16)
                    SY.wait_ge(dS.h, dS.n)
                iter_v.append(vC.n); iter_s.append(sC.n)
            SY.wait_ge(dS.h, dS.n)
            ies.close()
        blk.sync(emit)


def _s4(cx):
    nc = cx.nc
    SY, TE, VE, SC = cx.SY, cx.TE, cx.VE, cx.SC
    with nc.Block("s4") as blk:
        def emit(_):
            ies = ExitStack()
            dL = CT(nc, ies, "s4_dL"); dS = CT(nc, ies, "s4_dS")
            tC = CT(nc, ies, "s4_tC"); vC = CT(nc, ies, "s4_vC"); sC = CT(nc, ies, "s4_sC")
            qb_ = [ies.enter_context(nc.sbuf_tensor(f"s4_q{i}", [64, L], BF16)) for i in range(2)]
            kb_ = [ies.enter_context(nc.sbuf_tensor(f"s4_k{i}", [64, L], BF16)) for i in range(2)]
            vb_ = [ies.enter_context(nc.sbuf_tensor(f"s4_v{i}", [128, 16, 65], BF16)) for i in range(2)]
            pb_ = [ies.enter_context(nc.sbuf_tensor(f"s4_pb{i}", [128, 512], BF16)) for i in range(2)]
            rcp = ies.enter_context(nc.sbuf_tensor("s4_rcp", [1, 512], F32))
            rbc = ies.enter_context(nc.sbuf_tensor("s4_rbc", [64, 512], F32))
            ob = [ies.enter_context(nc.sbuf_tensor(f"s4_ob{i}", [64, 512], BF16)) for i in range(2)]
            pS = [ies.enter_context(nc.psum_tensor(f"s4_s{i}", [128, 512], F32)) for i in range(2)]
            pO = ies.enter_context(nc.psum_tensor("s4_o", [65, 512], F32))
            pB = ies.enter_context(nc.psum_tensor("s4_b", [64, 512], F32))
            vC.inc(VE.memset(vb_[0][:, :, 64:65], 1.0))
            vC.inc(VE.memset(vb_[1][:, :, 64:65], 1.0))
            iter_t = []
            mk_S = [0, 0]
            mk_O = 0
            mk_B = 0
            mk_ob = [0, 0]
            for it in range(BPC * HEADS):
                b, h = divmod(it, HEADS)
                bi = it % 2
                if it >= 2:
                    SY.wait_ge(tC.h, iter_t[it - 2])
                dL.inc(SY.dma_start(out=qb_[bi][:],
                                    in_=cx.q_dm[h * 64:(h + 1) * 64, b * L:(b + 1) * L]), 16)
                dL.inc(SY.dma_start(out=kb_[bi][:],
                                    in_=cx.k_dm[h * 64:(h + 1) * 64, b * L:(b + 1) * L]), 16)
                dL.inc(SY.dma_start(
                    out=vb_[bi][:, :, 0:64],
                    in_=cx.v_tm[b * L:(b + 1) * L, :]
                        .rearrange("(kt p) (h2 j) -> p kt h2 j", p=128, j=64)[:, :, h, :]), 16)
                wL = dL.n
                TE.wait_ge(dL.h, wL)
                for qblk in range(4):
                    nkt = 4 * qblk + 4
                    for kt in range(nkt):
                        ks = kt % 2
                        TE.wait_ge(sC.h, mk_S[ks])
                        tC.inc(TE.matmul(pS[ks][:], kb_[bi][:, kt * 128:(kt + 1) * 128],
                                         qb_[bi][:, qblk * 512:(qblk + 1) * 512],
                                         start=True, stop=True))
                        wS = tC.n
                        dd = kt - 4 * qblk
                        if dd >= 0:
                            VE.wait_ge(tC.h, wS)
                            vC.inc(VE.tensor_tensor(pS[ks][:], pS[ks][:],
                                                    cx.P_amask[:, dd * 512:(dd + 1) * 512],
                                                    ALU.add))
                            SC.wait_ge(vC.h, vC.n)
                        SC.wait_ge(tC.h, wS)
                        sC.inc(SC.activation(pb_[ks][:], pS[ks][:], AF.Exp,
                                             scale=0.125, bias=cx.P_negc[:, 0:1]))
                        mk_S[ks] = sC.n
                        TE.wait_ge(sC.h, sC.n)
                        if kt == 0:
                            TE.wait_ge(vC.h, mk_O)
                        tC.inc(TE.matmul(pO[:], vb_[bi][:, kt, :], pb_[ks][:],
                                         start=(kt == 0), stop=(kt == nkt - 1)))
                    wPV = tC.n
                    VE.wait_ge(tC.h, wPV)
                    vC.inc(VE.reciprocal(rcp[0:1, :], pO[64:65, :]))
                    VE.wait_ge(vC.h, vC.n)
                    TE.wait_ge(vC.h, max(vC.n, mk_B))
                    tC.inc(TE.matmul(pB[:], cx.P_ones_f[0:1, 0:64], rcp[0:1, :],
                                     start=True, stop=True))
                    VE.wait_ge(tC.h, tC.n)
                    vC.inc(VE.tensor_copy(rbc[:], pB[:]))
                    mk_B = vC.n
                    VE.wait_ge(vC.h, vC.n)
                    VE.wait_ge(dS.h, mk_ob[qblk % 2])
                    vC.inc(VE.scalar_tensor_tensor(ob[qblk % 2][:], pO[0:64, :], 1.0, rbc[:],
                                                   ALU.mult, ALU.mult))
                    mk_O = vC.n
                    SY.wait_ge(vC.h, vC.n)
                    dS.inc(SY.dma_start(
                        out=cx.ao_dm[h * 64:(h + 1) * 64,
                                     b * L + qblk * 512:b * L + (qblk + 1) * 512],
                        in_=ob[qblk % 2][:]), 16)
                    mk_ob[qblk % 2] = dS.n
                iter_t.append(tC.n)
            SY.wait_ge(dS.h, dS.n)
            ies.close()
        blk.sync(emit)


def _s4b(cx):
    nc = cx.nc
    SY, TE, VE, GP = cx.SY, cx.TE, cx.VE, cx.GP
    with nc.Block("s4b") as blk:
        def emit(_):
            ies = ExitStack()
            dW = CT(nc, ies, "s4b_dW"); dL = CT(nc, ies, "s4b_dL"); dS = CT(nc, ies, "s4b_dS")
            tC = CT(nc, ies, "s4b_tC"); vC = CT(nc, ies, "s4b_vC")
            wo_sb = ies.enter_context(nc.sbuf_tensor("s4b_wo", [128, 8, D - MZ], BF16))
            ab = [ies.enter_context(nc.sbuf_tensor(f"s4b_a{i}", [128, 8, 128], BF16)) for i in range(2)]
            ob = [ies.enter_context(nc.sbuf_tensor(f"s4b_o{i}", [128, D - MZ], F32)) for i in range(2)]
            pm = [ies.enter_context(nc.psum_tensor(f"s4b_p{i}", [128, 512], F32)) for i in range(2)]
            dW.inc(GP.dma_start(out=wo_sb[:],
                                in_=cx.wo_t[:, MZ:].rearrange("(k p) n -> p k n", p=128)), 16)
            iter_v = []
            for t in range(NT):
                bi = t % 2
                if t >= 2:
                    SY.wait_ge(vC.h, iter_v[t - 2])
                dL.inc(SY.dma_start(
                    out=ab[bi][:],
                    in_=cx.ao_dm.rearrange("(k p) T2 -> p k T2", p=128)[:, :, t * 128:(t + 1) * 128]), 16)
                wL = dL.n
                TE.wait_ge(dL.h, wL)
                if t == 0:
                    TE.wait_ge(dW.h, dW.n)
                if t >= 2:
                    TE.wait_ge(vC.h, iter_v[t - 2])
                for n in range(2):
                    w0, w1 = n * 512, min((n + 1) * 512, D - MZ)
                    for k in range(8):
                        tC.inc(TE.matmul(pm[n][:, 0:w1 - w0], ab[bi][:, k, :],
                                         wo_sb[:, k, w0:w1], start=(k == 0), stop=(k == 7)))
                VE.wait_ge(tC.h, tC.n)
                vC.inc(VE.tensor_copy(ob[bi][:, 0:512], pm[0][:]))
                vC.inc(VE.tensor_copy(ob[bi][:, 512:768], pm[1][:, 0:256]))
                SY.wait_ge(vC.h, vC.n)
                dS.inc(SY.dma_start(out=cx.acc_tm[t * 128:(t + 1) * 128, :], in_=ob[bi][:]), 16)
                iter_v.append(vC.n)
            SY.wait_ge(dS.h, dS.n)
            ies.close()
        blk.sync(emit)


def _s5(cx):
    nc = cx.nc
    SY, TE, VE, SC, GP = cx.SY, cx.TE, cx.VE, cx.SC, cx.GP
    with nc.Block("s5") as blk:
        def emit(_):
            ies = ExitStack()
            dW = CT(nc, ies, "s5_dW"); dL = CT(nc, ies, "s5_dL"); dS = CT(nc, ies, "s5_dS")
            tC = CT(nc, ies, "s5_tC"); vC = CT(nc, ies, "s5_vC"); sC = CT(nc, ies, "s5_sC")
            gw_sb = ies.enter_context(nc.sbuf_tensor("s5_gw", [128, 8, 4], BF16))
            xnb = [ies.enter_context(nc.sbuf_tensor(f"s5_x{i}", [128, D], F32)) for i in range(2)]
            acb = [ies.enter_context(nc.sbuf_tensor(f"s5_a{i}", [128, D - MZ], F32)) for i in range(2)]
            x2 = ies.enter_context(nc.sbuf_tensor("s5_x2", [128, D], F32))
            st = ies.enter_context(nc.sbuf_tensor("s5_st", [128, 4], F32))
            junk = ies.enter_context(nc.sbuf_tensor("s5_j", [128, D], F32))
            xf16 = ies.enter_context(nc.sbuf_tensor("s5_xf", [128, D], BF16))
            xfd = [ies.enter_context(nc.sbuf_tensor(f"s5_d{i}", [128, 8, 128], BF16)) for i in range(2)]
            gate = ies.enter_context(nc.sbuf_tensor("s5_g", [128, 24], F32))
            gm = ies.enter_context(nc.sbuf_tensor("s5_gm", [128, 32], F32))
            wt4 = [ies.enter_context(nc.sbuf_tensor(f"s5_w{i}", [128, 4], F32)) for i in range(2)]
            ptp = [ies.enter_context(nc.psum_tensor(f"s5_p{i}", [128, 1024], BF16)) for i in range(2)]
            pg = ies.enter_context(nc.psum_tensor("s5_pg", [128, 512], F32))
            dW.inc(GP.dma_start(out=gw_sb[:], in_=cx.gw_t.rearrange("(k p) n -> p k n", p=128)), 16)
            iter_v, iter_s = [], []
            mk_tp = [0, 0]
            mk_pg = 0
            for t in range(NT):
                bi = t % 2
                if t >= 2:
                    SY.wait_ge(vC.h, iter_v[t - 2])
                    SY.wait_ge(sC.h, iter_s[t - 2])
                dL.inc(SY.dma_start(out=xnb[bi][:], in_=cx.xn_tm32[t * 128:(t + 1) * 128, :]), 16)
                dL.inc(SY.dma_start(out=acb[bi][:], in_=cx.acc_tm[t * 128:(t + 1) * 128, :]), 16)
                wL = dL.n
                VE.wait_ge(dL.h, wL)
                vC.inc(VE.tensor_copy(x2[:, 0:MZ], xnb[bi][:, 0:MZ]))
                vC.inc(VE.tensor_tensor(x2[:, MZ:], xnb[bi][:, MZ:], acb[bi][:], ALU.add))
                VE.wait_ge(vC.h, vC.n)
                SC.wait_ge(vC.h, vC.n)
                sC.inc(SC.activation(junk[:], x2[:], AF.Square, accum_out=st[:, 0:1]))
                SC.wait_ge(sC.h, sC.n)
                sC.inc(SC.activation(st[:, 1:2], st[:, 0:1], AF.Ln, scale=1.0 / D, bias=EPS_RMS))
                SC.wait_ge(sC.h, sC.n)
                sC.inc(SC.activation(st[:, 2:3], st[:, 1:2], AF.Exp, scale=-0.5))
                VE.wait_ge(sC.h, sC.n)
                vC.inc(VE.tensor_scalar(xf16[:], x2[:], st[:, 2:3], None, ALU.mult))
                VE.wait_ge(vC.h, vC.n)
                vC.inc(VE.tensor_tensor(xf16[:], xf16[:], cx.P_n2t[:], ALU.mult))
                VE.wait_ge(vC.h, vC.n)
                wXF = vC.n
                TE.wait_ge(vC.h, wXF)
                if t == 0:
                    TE.wait_ge(dW.h, dW.n)
                for k in range(8):
                    ks = k % 2
                    TE.wait_ge(vC.h, mk_tp[ks])
                    tC.inc(TE.matmul(ptp[ks][:, 0:128], xf16[:, k * 128:(k + 1) * 128],
                                     cx.P_idt16[:], is_transpose=True, start=True, stop=True))
                    VE.wait_ge(tC.h, tC.n)
                    vC.inc(VE.tensor_copy(xfd[bi][:, k, :], ptp[ks][:, 0:128]))
                    mk_tp[ks] = vC.n
                wXD = vC.n
                SY.wait_ge(vC.h, wXD)
                dS.inc(SY.dma_start(
                    out=cx.xf_dm.rearrange("(k p) T2 -> p k T2", p=128)[:, :, t * 128:(t + 1) * 128],
                    in_=xfd[bi][:]), 16)
                TE.wait_ge(vC.h, wXD)
                TE.wait_ge(vC.h, mk_pg)
                for k in range(8):
                    tC.inc(TE.matmul(pg[:, 0:4], xfd[bi][:, k, :], gw_sb[:, k, :],
                                     start=(k == 0), stop=(k == 7)))
                wG = tC.n
                VE.wait_ge(tC.h, wG)
                vC.inc(VE.reduce_max(st[:, 3:4], pg[:, 0:4], axis=AX.X, negate=True))
                VE.wait_ge(vC.h, vC.n)
                SC.wait_ge(vC.h, vC.n)
                sC.inc(SC.activation(gate[:, 0:4], pg[:, 0:4], AF.Exp,
                                     bias=st[:, 3:4], accum_out=gate[:, 4:5]))
                mk_pg = sC.n
                VE.wait_ge(sC.h, sC.n)
                vC.inc(VE.reciprocal(gate[:, 5:6], gate[:, 4:5]))
                VE.wait_ge(vC.h, vC.n)
                vC.inc(VE.tensor_scalar(gate[:, 0:4], gate[:, 0:4], gate[:, 5:6], None, ALU.mult))
                VE.wait_ge(vC.h, vC.n)
                vC.inc(VE.tensor_tensor(gm[:, 0:16].rearrange("p (e j) -> p e j", j=4),
                                        gate[:, 0:4].unsqueeze(1).broadcast_to([128, 4, 4]),
                                        gate[:, 0:4].unsqueeze(2).broadcast_to([128, 4, 4]),
                                        ALU.is_gt))
                vC.inc(VE.tensor_tensor(gm[:, 16:32].rearrange("p (e j) -> p e j", j=4),
                                        gate[:, 0:4].unsqueeze(1).broadcast_to([128, 4, 4]),
                                        gate[:, 0:4].unsqueeze(2).broadcast_to([128, 4, 4]),
                                        ALU.is_equal))
                VE.wait_ge(vC.h, vC.n)
                vC.inc(VE.tensor_tensor(gm[:, 16:32], gm[:, 16:32], cx.P_ltm[:], ALU.mult))
                VE.wait_ge(vC.h, vC.n)
                vC.inc(VE.tensor_tensor(gm[:, 0:16], gm[:, 0:16], gm[:, 16:32], ALU.add))
                VE.wait_ge(vC.h, vC.n)
                vC.inc(VE.tensor_reduce(gate[:, 8:12].unsqueeze(2),
                                        gm[:, 0:16].rearrange("p (e j) -> p e j", j=4),
                                        AX.X, ALU.add))
                VE.wait_ge(vC.h, vC.n)
                vC.inc(VE.tensor_scalar(gate[:, 12:16], gate[:, 8:12], 1.5, None, ALU.is_le))
                VE.wait_ge(vC.h, vC.n)
                vC.inc(VE.tensor_tensor(gate[:, 16:20], gate[:, 0:4], gate[:, 12:16], ALU.mult))
                VE.wait_ge(vC.h, vC.n)
                vC.inc(VE.reduce_sum(gate[:, 20:21], gate[:, 16:20], axis=AX.X))
                VE.wait_ge(vC.h, vC.n)
                vC.inc(VE.tensor_scalar(gate[:, 21:22], gate[:, 20:21], 1e-8, None, ALU.add))
                VE.wait_ge(vC.h, vC.n)
                vC.inc(VE.reciprocal(gate[:, 22:23], gate[:, 21:22]))
                VE.wait_ge(vC.h, vC.n)
                vC.inc(VE.tensor_scalar(wt4[bi][:], gate[:, 16:20], gate[:, 22:23], None, ALU.mult))
                SY.wait_ge(vC.h, vC.n)
                dS.inc(SY.dma_start(out=cx.wts_tm[t * 128:(t + 1) * 128, :], in_=wt4[bi][:]), 16)
                iter_v.append(vC.n); iter_s.append(sC.n)
            SY.wait_ge(dS.h, dS.n)
            ies.close()
        blk.sync(emit)


def _s6(cx, name, kind):
    nc = cx.nc
    SY, TE, VE, SC, GP = cx.SY, cx.TE, cx.VE, cx.SC, cx.GP
    with nc.Block(name) as blk:
        def emit(_):
            ies = ExitStack()
            dW = CT(nc, ies, f"{name}_dW"); dL = CT(nc, ies, f"{name}_dL")
            dS = CT(nc, ies, f"{name}_dS")
            tC = CT(nc, ies, f"{name}_tC"); vC = CT(nc, ies, f"{name}_vC")
            sC = CT(nc, ies, f"{name}_sC")
            sbt = lambda n_, s, d: ies.enter_context(nc.sbuf_tensor(name + n_, list(s), d))
            pst = lambda n_, s: ies.enter_context(nc.psum_tensor(name + n_, list(s), F32))
            eidx = {"e0": 0, "e1": 1, "e2b": 2, "e3": 3}.get(kind)
            rr = lambda ap_: ap_.rearrange("(k p) n -> p k n", p=128)
            if kind == "e0":
                wup = sbt("w_up", [128, 8, HID], BF16)
                wdn = sbt("w_dn", [128, 16, D - MZ], BF16)
                dW.inc(GP.dma_start(out=wup[:], in_=rr(cx.e0_up_t)), 16)
                dW.inc(GP.dma_start(out=wdn[:], in_=rr(cx.e0_dn_t[:, MZ:])), 16)
            elif kind == "e1":
                wup = sbt("w_u1", [128, 8, HID], BF16)
                wu2 = sbt("w_u2", [128, 8, HID], BF16)
                wdn = sbt("w_dn", [128, 16, D - MZ], BF16)
                dW.inc(GP.dma_start(out=wup[:], in_=rr(cx.e1_w1_t)), 16)
                dW.inc(GP.dma_start(out=wu2[:], in_=rr(cx.e1_w2_t)), 16)
                dW.inc(GP.dma_start(out=wdn[:], in_=rr(cx.e1_dn_t[:, MZ:])), 16)
            elif kind == "e3":
                wd0 = sbt("w_d0", [128, 8, D], BF16)
                wup = sbt("w_up", [128, 8, HID], BF16)
                wdn = sbt("w_dn", [128, 16, D - MZ], BF16)
                dW.inc(GP.dma_start(out=wd0[:], in_=rr(cx.e3_dn_t)), 16)
                dW.inc(GP.dma_start(out=wup[:], in_=rr(cx.e3_up_t)), 16)
                dW.inc(GP.dma_start(out=wdn[:], in_=rr(cx.e3_out_t[:, MZ:])), 16)
            elif kind == "e2a":
                wl1 = sbt("w_l1", [128, 8, HID], BF16)
                wl2 = sbt("w_l2", [128, 16, HID], BF16)
                dW.inc(GP.dma_start(out=wl1[:], in_=rr(cx.e2_l1_t)), 16)
                dW.inc(GP.dma_start(out=wl2[:], in_=rr(cx.e2_l2_t)), 16)
            else:
                wl3 = sbt("w_l3", [128, 16, HID], BF16)
                wdn = sbt("w_dn", [128, 16, D - MZ], BF16)
                dW.inc(GP.dma_start(out=wl3[:], in_=rr(cx.e2_l3_t)), 16)
                dW.inc(GP.dma_start(out=wdn[:], in_=rr(cx.e2_l4_t[:, MZ:])), 16)
            xfb = [sbt(f"xf{i}", [128, 8, 512], BF16) for i in range(2)]
            h2in = [sbt(f"h2i{i}", [128, 16, 512], BF16) for i in range(2)] if kind == "e2b" else None
            h1 = sbt("h1", [128, 16, 512], BF16)
            h1b = sbt("h1b", [128, 16, 512], BF16) if kind in ("e1", "e2a") else None
            h0 = sbt("h0", [128, 8, 512], BF16) if kind == "e3" else None
            wtb = [sbt(f"wt{i}", [128, 4, 4], F32) for i in range(2)]
            acb = sbt("ac", [128, D - MZ], F32)
            pu = [pst(f"pu{i}", [128, 512]) for i in range(2)]
            pd = [pst(f"pd{i}", [128, 512]) for i in range(2)]
            NGR = T // 512
            iter_end = []
            for gidx in range(NGR):
                bi = gidx % 2
                r0 = gidx * 512
                if gidx >= 2:
                    SY.wait_ge(tC.h, iter_end[gidx - 2][0])
                    SY.wait_ge(vC.h, iter_end[gidx - 2][1])
                if kind == "e2b":
                    dL.inc(SY.dma_start(
                        out=h2in[bi][:],
                        in_=cx.h2_dm.rearrange("(k p) T2 -> p k T2", p=128)[:, :, r0:r0 + 512]), 16)
                else:
                    dL.inc(SY.dma_start(
                        out=xfb[bi][:],
                        in_=cx.xf_dm.rearrange("(k p) T2 -> p k T2", p=128)[:, :, r0:r0 + 512]), 16)
                if kind != "e2a":
                    dL.inc(SY.dma_start(
                        out=wtb[bi][:],
                        in_=cx.wts_tm[r0:r0 + 512, :].rearrange("(s p) e -> p s e", p=128)), 16)
                wL = dL.n
                TE.wait_ge(dL.h, wL)
                if gidx == 0:
                    TE.wait_ge(dW.h, dW.n)
                def up_chain(wsb, nkb, src_idx, dst_sb, act):
                    for m in range(16):
                        ms = m % 2
                        TE.wait_ge(sC.h, sC.n); TE.wait_ge(vC.h, vC.n)
                        for k in range(nkb):
                            tC.inc(TE.matmul(pu[ms][:], wsb[:, k, m * 128:(m + 1) * 128],
                                             src_idx(k), start=(k == 0), stop=(k == nkb - 1)))
                        if act is not None:
                            SC.wait_ge(tC.h, tC.n)
                            sC.inc(SC.activation(dst_sb[:, m, :], pu[ms][:], act))
                        else:
                            VE.wait_ge(tC.h, tC.n)
                            vC.inc(VE.tensor_copy(dst_sb[:, m, :], pu[ms][:]))
                if kind == "e0":
                    up_chain(wup, 8, lambda k: xfb[bi][:, k, :], h1, AF.Gelu)
                elif kind == "e1":
                    up_chain(wup, 8, lambda k: xfb[bi][:, k, :], h1, AF.Silu)
                    up_chain(wu2, 8, lambda k: xfb[bi][:, k, :], h1b, None)
                    VE.wait_ge(sC.h, sC.n); VE.wait_ge(vC.h, vC.n)
                    vC.inc(VE.tensor_tensor(h1[:], h1[:], h1b[:], ALU.mult))
                    VE.wait_ge(vC.h, vC.n)
                elif kind == "e3":
                    for m in range(8):
                        ms = m % 2
                        TE.wait_ge(sC.h, sC.n)
                        for k in range(8):
                            tC.inc(TE.matmul(pu[ms][:], wd0[:, k, m * 128:(m + 1) * 128],
                                             xfb[bi][:, k, :], start=(k == 0), stop=(k == 7)))
                        SC.wait_ge(tC.h, tC.n)
                        sC.inc(SC.activation(h0[:, m, :], pu[ms][:], AF.Gelu))
                    TE.wait_ge(sC.h, sC.n)
                    up_chain(wup, 8, lambda k: h0[:, k, :], h1, AF.Gelu)
                elif kind == "e2a":
                    up_chain(wl1, 8, lambda k: xfb[bi][:, k, :], h1b, AF.Gelu)
                    TE.wait_ge(sC.h, sC.n)
                    up_chain(wl2, 16, lambda k: h1b[:, k, :], h1, AF.Gelu)
                    SY.wait_ge(sC.h, sC.n)
                    dS.inc(SY.dma_start(
                        out=cx.h2_dm.rearrange("(k p) T2 -> p k T2", p=128)[:, :, r0:r0 + 512],
                        in_=h1[:]), 16)
                    iter_end.append((tC.n, vC.n))
                    continue
                else:
                    up_chain(wl3, 16, lambda k: h2in[bi][:, k, :], h1, AF.Gelu)
                TE.wait_ge(sC.h, sC.n); TE.wait_ge(vC.h, vC.n)
                for msub in range(4):
                    SY.wait_ge(dS.h, dS.n)
                    dL.inc(SY.dma_start(out=acb[:],
                                        in_=cx.acc_tm[r0 + msub * 128:r0 + (msub + 1) * 128, :]), 16)
                    wA = dL.n
                    if msub >= 1:
                        TE.wait_ge(vC.h, vC.n)
                    for n in range(2):
                        w0, w1 = n * 512, min((n + 1) * 512, D - MZ)
                        for k in range(16):
                            tC.inc(TE.matmul(pd[n][:, 0:w1 - w0],
                                             h1[:, k, msub * 128:(msub + 1) * 128],
                                             wdn[:, k, w0:w1], start=(k == 0), stop=(k == 15)))
                    wD = tC.n
                    VE.wait_ge(tC.h, wD); VE.wait_ge(dL.h, wA)
                    vC.inc(VE.scalar_tensor_tensor(acb[:, 0:512], pd[0][:],
                                                   wtb[bi][:, msub, eidx:eidx + 1],
                                                   acb[:, 0:512], ALU.mult, ALU.add))
                    vC.inc(VE.scalar_tensor_tensor(acb[:, 512:768], pd[1][:, 0:256],
                                                   wtb[bi][:, msub, eidx:eidx + 1],
                                                   acb[:, 512:768], ALU.mult, ALU.add))
                    SY.wait_ge(vC.h, vC.n)
                    dS.inc(SY.dma_start(out=cx.acc_tm[r0 + msub * 128:r0 + (msub + 1) * 128, :],
                                        in_=acb[:]), 16)
                iter_end.append((tC.n, vC.n))
            SY.wait_ge(dS.h, dS.n)
            ies.close()
        blk.sync(emit)


def _s7(cx):
    nc = cx.nc
    SY, VE = cx.SY, cx.VE
    with nc.Block("s7") as blk:
        def emit(_):
            ies = ExitStack()
            dL = CT(nc, ies, "s7_dL"); dS = CT(nc, ies, "s7_dS"); vC = CT(nc, ies, "s7_vC")
            xb = [ies.enter_context(nc.sbuf_tensor(f"s7_x{i}", [128, D], F32)) for i in range(2)]
            mb = [ies.enter_context(nc.sbuf_tensor(f"s7_m{i}", [128, D], F32)) for i in range(2)]
            nb = [ies.enter_context(nc.sbuf_tensor(f"s7_n{i}", [128, D], F32)) for i in range(2)]
            ab = [ies.enter_context(nc.sbuf_tensor(f"s7_a{i}", [128, D - MZ], F32)) for i in range(2)]
            ob = [ies.enter_context(nc.sbuf_tensor(f"s7_o{i}", [128, D], F32)) for i in range(2)]
            iter_v = []
            for t in range(NT):
                bi = t % 2
                if t >= 2:
                    SY.wait_ge(vC.h, iter_v[t - 2])
                dL.inc(SY.dma_start(out=xb[bi][:], in_=cx.x[t * 128:(t + 1) * 128, :]), 16)
                dL.inc(SY.dma_start(out=mb[bi][:], in_=cx.mamba_tm[t * 128:(t + 1) * 128, :]), 16)
                dL.inc(SY.dma_start(out=nb[bi][:], in_=cx.xn_tm32[t * 128:(t + 1) * 128, :]), 16)
                dL.inc(SY.dma_start(out=ab[bi][:], in_=cx.acc_tm[t * 128:(t + 1) * 128, :]), 16)
                wL = dL.n
                VE.wait_ge(dL.h, wL)
                vC.inc(VE.tensor_tensor(ob[bi][:], xb[bi][:], mb[bi][:], ALU.add))
                VE.wait_ge(vC.h, vC.n)
                vC.inc(VE.tensor_tensor(ob[bi][:], ob[bi][:], nb[bi][:], ALU.add))
                VE.wait_ge(vC.h, vC.n)
                vC.inc(VE.tensor_tensor(ob[bi][:, MZ:], ob[bi][:, MZ:], ab[bi][:], ALU.add))
                SY.wait_ge(vC.h, vC.n)
                dS.inc(SY.dma_start(out=cx.out[t * 128:(t + 1) * 128, :], in_=ob[bi][:]), 16)
                iter_v.append(vC.n)
            SY.wait_ge(dS.h, dS.n)
            ies.close()
        blk.sync(emit)


def build(dbg=False, stages=("s0", "s1", "s2", "s3", "s4", "s5", "s6", "s7")):
    nc = bass.Bass("TRN2", target_bir_lowering=False)
    cx = _declare(nc, dbg)
    _init_stage(cx)
    if "s0" in stages: _s0(cx)
    if "s1" in stages: _s1(cx)
    if "s2" in stages: _s2(cx)
    if "s3" in stages: _s3(cx)
    if "s4" in stages: _s4(cx)
    if "s4" in stages: _s4b(cx)
    if "s5" in stages: _s5(cx)
    if "s6" in stages:
        _s6(cx, "s6e1", "e1")
        _s6(cx, "s6e0", "e0")
        _s6(cx, "s6e3", "e3")
        _s6(cx, "s6e2a", "e2a")
        _s6(cx, "s6e2b", "e2b")
    if "s7" in stages: _s7(cx)
    cx.es.close()
    return nc


_CACHE = {}

def kernel(**inputs):
    consts = _host_consts()
    if "nc" not in _CACHE:
        _CACHE["nc"] = build(dbg=False)
    nc = _CACHE["nc"]
    in_maps = []
    for core in range(NCORES):
        m = dict(_host_layout(inputs, core))
        m.update(consts)
        in_maps.append(m)
    res = run_bass_kernel_spmd(nc, in_maps, core_ids=list(range(NCORES)))
    outs = [np.asarray(r["out"], np.float32).reshape(BPC, L, D) for r in res.results]
    return np.concatenate(outs, axis=0)


# version 02
# speedup vs baseline: 1.0229x; 1.0229x over previous
import numpy as np
import ml_dtypes
import concourse.bass as bass
import concourse.mybir as mybir
from concourse.bass_utils import run_bass_kernel_spmd
from contextlib import ExitStack

AF = mybir.ActivationFunctionType
ALU = mybir.AluOpType
AX = mybir.AxisListType
F32, BF16, I32 = mybir.dt.float32, mybir.dt.bfloat16, mybir.dt.int32

NCORES = 8
BPC = 4
L = 2048
T = BPC * L
D = 1024
DI = 2048
NH = 32
Q = 128
NCH = L // Q
GRP = 4
NG = NCH // GRP
HEADS = 16
HD = 64
MZ = 256
HID = 2048
NT = T // 128
EPS_RMS = 1.1920929e-07
EPS_LN = 1e-5
TWO_PI = float(2.0 * np.pi)
BIG = 1e30


class CT:
    def __init__(self, nc, es, name):
        self.h = es.enter_context(nc.semaphore(name))
        self.n = 0
    def inc(self, inst, k=1):
        inst.then_inc(self.h, k)
        self.n += k
        return self.n


def _host_consts():
    c = {}
    le = np.where(np.arange(128)[:, None] <= np.arange(128)[None, :], 1.0, 0.0)
    c["c_cum"] = le.astype(np.float32)
    c["c_wmask"] = le.astype(ml_dtypes.bfloat16)
    c["c_idt16"] = np.eye(128, dtype=np.float32).astype(ml_dtypes.bfloat16)
    c["c_idt32"] = np.eye(128, dtype=np.float32)
    oh = np.zeros((32, 32 * 128), np.float32)
    for h in range(32):
        oh[h, h * 128:(h + 1) * 128] = 1.0
    c["c_onehot"] = oh
    inv2pi = (1.0 / 10000.0 ** (np.arange(16, dtype=np.float64) / 16.0)) / (2 * np.pi)
    c["c_if2pi"] = np.broadcast_to(inv2pi.astype(np.float32), (128, 32, 16)).reshape(128, 512).copy()
    am = np.zeros((4, 128, 512), np.float32)
    for i, dd in enumerate((0, 128, 256, 384)):
        am[i] = np.where(np.arange(128)[:, None] + dd <= np.arange(512)[None, :], 0.0, -30000.0)
    c["c_amask"] = np.ascontiguousarray(am.reshape(4 * 128, 512))
    ltm = np.zeros((4, 4), np.float32)
    for e in range(4):
        ltm[e, :e] = 1.0
    c["c_ltm"] = np.broadcast_to(ltm.reshape(1, 16), (128, 16)).copy()
    c["c_ones_f"] = np.ones((1, 128), np.float32)
    c["c_onescol"] = np.ones((128, 1), np.float32)
    return c


def _host_layout(inputs, core):
    g = lambda k: np.asarray(inputs[k])
    tr = lambda a: np.ascontiguousarray(np.asarray(a).T)
    d = {}
    d["x"] = np.ascontiguousarray(g("L0_attn_x")[core * BPC:(core + 1) * BPC].reshape(T, D))
    W_in = g("L0_attn_W_in")
    d["w_z"] = np.ascontiguousarray(W_in[:, :DI])
    d["w_x"] = np.ascontiguousarray(W_in[:, DI:2 * DI])
    d["w_bcd"] = np.ascontiguousarray(W_in[:, 2 * DI:])
    d["w_out"] = np.ascontiguousarray(g("L0_attn_W_out"))
    d["wq_t"] = tr(g("L0_ffn_wq")); d["wk_t"] = tr(g("L0_ffn_wk"))
    d["wv_t"] = tr(g("L0_ffn_wv")); d["wo_t"] = tr(g("L0_ffn_wo"))
    d["gw_t"] = tr(g("L0_ffn_gate_w"))
    d["e0_up_t"] = tr(g("L0_ffn_e0_up")); d["e0_dn_t"] = tr(g("L0_ffn_e0_down"))
    d["e1_w1_t"] = tr(g("L0_ffn_e1_w1")); d["e1_w2_t"] = tr(g("L0_ffn_e1_w2"))
    d["e1_dn_t"] = tr(g("L0_ffn_e1_down"))
    d["e2_l1_t"] = tr(g("L0_ffn_e2_l1")); d["e2_l2_t"] = tr(g("L0_ffn_e2_l2"))
    d["e2_l3_t"] = tr(g("L0_ffn_e2_l3")); d["e2_l4_t"] = tr(g("L0_ffn_e2_l4"))
    d["e3_dn_t"] = tr(g("L0_ffn_e3_down")); d["e3_up_t"] = tr(g("L0_ffn_e3_up"))
    d["e3_out_t"] = tr(g("L0_ffn_e3_out"))
    d["a_log_r"] = np.ascontiguousarray(g("L0_attn_A_log").reshape(1, NH))
    d["qn_r"] = np.ascontiguousarray(g("L0_ffn_qn_w").reshape(1, HD))
    d["kn_r"] = np.ascontiguousarray(g("L0_ffn_kn_w").reshape(1, HD))
    d["dtb_bc"] = np.ascontiguousarray(np.broadcast_to(g("L0_attn_dt_bias"), (128, NH)))
    d["dsk_bc"] = np.ascontiguousarray(np.broadcast_to(g("L0_attn_D_skip"), (128, NH)))
    d["n1t"] = np.ascontiguousarray(np.broadcast_to(g("L0_ffn_n1_w"), (128, D)))
    d["n2t"] = np.ascontiguousarray(np.broadcast_to(g("L0_ffn_n2_w"), (128, D)))
    d["qnt"] = np.ascontiguousarray(np.broadcast_to(np.tile(g("L0_ffn_qn_w"), HEADS), (128, D)))
    d["knt"] = np.ascontiguousarray(np.broadcast_to(np.tile(g("L0_ffn_kn_w"), HEADS), (128, D)))
    d["nw_c"] = np.ascontiguousarray(g("L0_attn_norm_weight").reshape(16, 128).T)
    return d


class Ctx:
    pass


def _declare(nc, dbg):
    cx = Ctx()
    cx.nc = nc
    cx.es = ExitStack()
    cx.SY, cx.TE, cx.VE, cx.SC, cx.GP = nc.sync, nc.tensor, nc.vector, nc.scalar, nc.gpsimd
    def din(name, shape, dt=F32):
        ap = nc.dram_tensor(name, list(shape), dt, kind="ExternalInput").ap()
        setattr(cx, name, ap)
    din("x", [T, D])
    din("w_z", [D, DI]); din("w_x", [D, DI]); din("w_bcd", [D, 160])
    din("w_out", [DI, D])
    din("wq_t", [D, D]); din("wk_t", [D, D]); din("wv_t", [D, D])
    din("wo_t", [D, D]); din("gw_t", [D, 4])
    din("e0_up_t", [D, HID]); din("e0_dn_t", [HID, D])
    din("e1_w1_t", [D, HID]); din("e1_w2_t", [D, HID]); din("e1_dn_t", [HID, D])
    din("e2_l1_t", [D, HID]); din("e2_l2_t", [HID, HID])
    din("e2_l3_t", [HID, HID]); din("e2_l4_t", [HID, D])
    din("e3_dn_t", [D, D]); din("e3_up_t", [D, HID]); din("e3_out_t", [HID, D])
    din("a_log_r", [1, NH]); din("qn_r", [1, HD]); din("kn_r", [1, HD])
    din("dtb_bc", [128, NH]); din("dsk_bc", [128, NH])
    din("n1t", [128, D]); din("n2t", [128, D]); din("qnt", [128, D]); din("knt", [128, D])
    din("nw_c", [128, 16])
    din("c_cum", [128, 128]); din("c_wmask", [128, 128], BF16)
    din("c_idt16", [128, 128], BF16); din("c_idt32", [128, 128])
    din("c_onehot", [32, 32 * 128]); din("c_if2pi", [128, 512])
    din("c_amask", [4 * 128, 512]); din("c_ltm", [128, 16])
    din("c_ones_f", [1, 128]); din("c_onescol", [128, 1])
    skind = "ExternalOutput" if dbg else "Internal"
    def dscr(name, shape, dt):
        ap = nc.dram_tensor(name, list(shape), dt, kind=skind).ap()
        setattr(cx, name, ap)
    dscr("xn_tm", [T, D], BF16)
    dscr("xn_tm32", [T, D], F32)
    dscr("xn_dm", [D, T], BF16)
    dscr("sz_tm", [T, DI], BF16)
    dscr("xs_tm", [T, DI], BF16)
    dscr("bc_tm", [T, 128], BF16)
    dscr("dtr_tm", [T, NH], F32)
    dscr("mamba_tm", [T, D], F32)
    dscr("q_dm", [D, T], BF16)
    dscr("k_dm", [D, T], BF16)
    dscr("v_tm", [T, D], BF16)
    dscr("ao_dm", [D, T], BF16)
    dscr("acc_tm", [T, D - MZ], F32)
    dscr("xf_dm", [D, T], BF16)
    dscr("wts_tm", [T, 4], F32)
    dscr("h2_dm", [HID, T], BF16)
    cx.out = nc.dram_tensor("out", [T, D], F32, kind="ExternalOutput").ap()
    def psb(name, shape, dt):
        setattr(cx, name, cx.es.enter_context(nc.sbuf_tensor(name, list(shape), dt)))
    psb("P_idt16", [128, 128], BF16)
    psb("P_idt32", [128, 128], F32)
    psb("P_cum", [128, 128], F32)
    psb("P_wmask", [128, 128], BF16)
    psb("P_onehot", [32, 32 * 128], F32)
    psb("P_if2pi", [128, 512], F32)
    psb("P_amask", [128, 4 * 512], F32)
    psb("P_ltm", [128, 16], F32)
    psb("P_ones_f", [1, 128], F32)
    psb("P_onescol", [128, 1], F32)
    psb("P_dtb", [128, NH], F32)
    psb("P_dsk", [128, NH], F32)
    psb("P_n1t", [128, D], BF16)
    psb("P_n2t", [128, D], BF16)
    psb("P_qnt", [128, D], BF16)
    psb("P_knt", [128, D], BF16)
    psb("P_nwc", [128, 16], F32)
    psb("P_A", [128, NH], F32)
    psb("P_negA", [128, NH], F32)
    psb("P_negc", [128, 1], F32)
    psb("P_tiny", [1, 512], F32)
    psb("P_rowA", [1, 128], F32)
    psb("P_rowNegA", [1, 128], F32)
    psb("P_rowNegc", [1, 128], F32)
    psb("P_const", [128, 2], F32)
    nc.const_aps.register(F32, EPS_LN, cx.P_const[:, 0:1])
    nc.const_aps.register(F32, EPS_RMS, cx.P_const[:, 1:2])
    return cx


def _init_stage(cx):
    nc = cx.nc
    SY, TE, VE, SC, GP = cx.SY, cx.TE, cx.VE, cx.SC, cx.GP
    with nc.Block("init") as blk:
        def emit(_):
            ies = ExitStack()
            dL = CT(nc, ies, "i_dL"); vC = CT(nc, ies, "i_vC")
            sC = CT(nc, ies, "i_sC"); tC = CT(nc, ies, "i_tC")
            pb = ies.enter_context(nc.psum_tensor("i_pb", [128, 512], F32))
            for dst, src in [(cx.P_cum, cx.c_cum), (cx.P_wmask, cx.c_wmask),
                             (cx.P_idt16, cx.c_idt16), (cx.P_idt32, cx.c_idt32),
                             (cx.P_onehot, cx.c_onehot), (cx.P_if2pi, cx.c_if2pi),
                             (cx.P_ltm, cx.c_ltm), (cx.P_ones_f, cx.c_ones_f),
                             (cx.P_onescol, cx.c_onescol), (cx.P_dtb, cx.dtb_bc),
                             (cx.P_dsk, cx.dsk_bc), (cx.P_nwc, cx.nw_c)]:
                dL.inc(SY.dma_start(out=dst[:], in_=src[:]), 16)
            dL.inc(SY.dma_start(out=cx.P_amask[:].rearrange("p (d n) -> p d n", n=512),
                                in_=cx.c_amask.rearrange("(d p) n -> p d n", p=128)), 16)
            for dst, src in [(cx.P_n1t, cx.n1t), (cx.P_n2t, cx.n2t),
                             (cx.P_qnt, cx.qnt), (cx.P_knt, cx.knt)]:
                dL.inc(GP.dma_start(out=dst[:], in_=src[:]), 16)
            vC.inc(VE.memset(cx.P_const[:, 0:1], EPS_LN))
            vC.inc(VE.memset(cx.P_const[:, 1:2], EPS_RMS))
            dL.inc(SY.dma_start(out=cx.P_tiny[0:1, 0:NH], in_=cx.a_log_r[:]), 16)
            dL.inc(SY.dma_start(out=cx.P_tiny[0:1, 32:96], in_=cx.qn_r[:]), 16)
            dL.inc(SY.dma_start(out=cx.P_tiny[0:1, 96:160], in_=cx.kn_r[:]), 16)
            VE.wait_ge(dL.h, dL.n); SC.wait_ge(dL.h, dL.n)
            sC.inc(SC.activation(cx.P_rowA[0:1, 0:NH], cx.P_tiny[0:1, 0:NH], AF.Exp))
            vC.inc(VE.reduce_max(cx.P_tiny[0:1, 160:161], cx.P_tiny[0:1, 32:96], axis=AX.X,
                                 apply_absolute_value=True))
            vC.inc(VE.reduce_max(cx.P_tiny[0:1, 161:162], cx.P_tiny[0:1, 96:160], axis=AX.X,
                                 apply_absolute_value=True))
            VE.wait_ge(vC.h, vC.n)
            vC.inc(VE.tensor_tensor(cx.P_tiny[0:1, 162:163], cx.P_tiny[0:1, 160:161],
                                    cx.P_tiny[0:1, 161:162], ALU.mult))
            VE.wait_ge(vC.h, vC.n)
            vC.inc(VE.tensor_scalar(cx.P_rowNegc[0:1, 0:1], cx.P_tiny[0:1, 162:163],
                                    -8.0, None, ALU.mult))
            SC.wait_ge(sC.h, sC.n)
            sC.inc(SC.mul(cx.P_rowNegA[0:1, 0:NH], cx.P_rowA[0:1, 0:NH], -1.0))
            TE.wait_ge(sC.h, sC.n); TE.wait_ge(vC.h, vC.n)
            tC.inc(TE.matmul(pb[:, 0:NH], cx.P_ones_f[:], cx.P_rowA[0:1, 0:NH],
                             start=True, stop=True))
            VE.wait_ge(tC.h, tC.n)
            vC.inc(VE.tensor_copy(cx.P_A[:], pb[:, 0:NH]))
            VE.wait_ge(vC.h, vC.n)
            TE.wait_ge(vC.h, vC.n)
            tC.inc(TE.matmul(pb[:, 0:NH], cx.P_ones_f[:], cx.P_rowNegA[0:1, 0:NH],
                             start=True, stop=True))
            VE.wait_ge(tC.h, tC.n)
            vC.inc(VE.tensor_copy(cx.P_negA[:], pb[:, 0:NH]))
            VE.wait_ge(vC.h, vC.n)
            TE.wait_ge(vC.h, vC.n)
            tC.inc(TE.matmul(pb[:, 0:1], cx.P_ones_f[:], cx.P_rowNegc[0:1, 0:1],
                             start=True, stop=True))
            VE.wait_ge(tC.h, tC.n)
            vC.inc(VE.tensor_copy(cx.P_negc[:], pb[:, 0:1]))
            SY.wait_ge(vC.h, vC.n)
            ies.close()
        blk.sync(emit)


def _s0(cx):
    nc = cx.nc
    SY, TE, VE, SC = cx.SY, cx.TE, cx.VE, cx.SC
    with nc.Block("s0") as blk:
        def emit(_):
            ies = ExitStack()
            dL = CT(nc, ies, "s0_dL"); dS = CT(nc, ies, "s0_dS")
            tC = CT(nc, ies, "s0_tC"); vC = CT(nc, ies, "s0_vC"); sC = CT(nc, ies, "s0_sC")
            xbuf = [ies.enter_context(nc.sbuf_tensor(f"s0_x{i}", [128, D], F32)) for i in range(2)]
            junk = ies.enter_context(nc.sbuf_tensor("s0_junk", [128, D], F32))
            st = ies.enter_context(nc.sbuf_tensor("s0_st", [128, 8], F32))
            xn16 = [ies.enter_context(nc.sbuf_tensor(f"s0_a{i}", [128, D], BF16)) for i in range(2)]
            xn32 = [ies.enter_context(nc.sbuf_tensor(f"s0_b{i}", [128, D], F32)) for i in range(2)]
            dmb = [ies.enter_context(nc.sbuf_tensor(f"s0_d{i}", [128, 8, 128], BF16)) for i in range(2)]
            ptp = [ies.enter_context(nc.psum_tensor(f"s0_p{i}", [128, 1024], BF16)) for i in range(2)]
            iter_v, iter_s = [], []
            tp_evac = [0, 0]
            for t in range(NT):
                bi = t % 2
                if t >= 2:
                    SY.wait_ge(vC.h, iter_v[t - 2])
                    SY.wait_ge(sC.h, iter_s[t - 2])
                dL.inc(SY.dma_start(out=xbuf[bi][:], in_=cx.x[t * 128:(t + 1) * 128, :]), 16)
                wL = dL.n
                VE.wait_ge(dL.h, wL); SC.wait_ge(dL.h, wL)
                vC.inc(VE.reduce_sum(st[:, 0:1], xbuf[bi][:], axis=AX.X))
                sC.inc(SC.activation(junk[:], xbuf[bi][:], AF.Square, accum_out=st[:, 1:2]))
                VE.wait_ge(sC.h, sC.n); VE.wait_ge(vC.h, vC.n)
                vC.inc(VE.tensor_scalar(st[:, 2:3], st[:, 0:1], 1.0 / D, None, ALU.mult))
                VE.wait_ge(vC.h, vC.n)
                vC.inc(VE.tensor_tensor(st[:, 3:4], st[:, 2:3], st[:, 2:3], ALU.mult))
                VE.wait_ge(vC.h, vC.n)
                vC.inc(VE.tensor_scalar(st[:, 4:5], st[:, 1:2], 1.0 / D, st[:, 3:4],
                                        ALU.mult, ALU.subtract))
                SC.wait_ge(vC.h, vC.n)
                sC.inc(SC.activation(st[:, 5:6], st[:, 4:5], AF.Ln, bias=EPS_LN))
                SC.wait_ge(sC.h, sC.n)
                sC.inc(SC.activation(st[:, 6:7], st[:, 5:6], AF.Exp, scale=-0.5))
                VE.wait_ge(sC.h, sC.n)
                vC.inc(VE.scalar_tensor_tensor(st[:, 7:8], st[:, 2:3], -1.0,
                                               st[:, 6:7], ALU.mult, ALU.mult))
                VE.wait_ge(vC.h, vC.n); SC.wait_ge(vC.h, vC.n)
                sC.inc(SC.activation(xn16[bi][:], xbuf[bi][:], AF.Identity,
                                     bias=st[:, 7:8], scale=st[:, 6:7]))
                vC.inc(VE.scalar_tensor_tensor(xn32[bi][:], xbuf[bi][:], st[:, 6:7],
                                               st[:, 7:8].broadcast_to([128, D]),
                                               ALU.mult, ALU.add))
                wS, wV = sC.n, vC.n
                TE.wait_ge(sC.h, wS)
                for k in range(8):
                    ks = k % 2
                    TE.wait_ge(vC.h, tp_evac[ks])
                    tC.inc(TE.matmul(ptp[ks][:, 0:128], xn16[bi][:, k * 128:(k + 1) * 128],
                                     cx.P_idt16[:], is_transpose=True, start=True, stop=True))
                    VE.wait_ge(tC.h, tC.n)
                    vC.inc(VE.tensor_copy(dmb[bi][:, k, :], ptp[ks][:, 0:128]))
                    tp_evac[ks] = vC.n
                SY.wait_ge(sC.h, wS)
                dS.inc(SY.dma_start(out=cx.xn_tm[t * 128:(t + 1) * 128, :], in_=xn16[bi][:]), 16)
                SY.wait_ge(vC.h, wV)
                dS.inc(SY.dma_start(out=cx.xn_tm32[t * 128:(t + 1) * 128, :], in_=xn32[bi][:]), 16)
                SY.wait_ge(vC.h, vC.n)
                dS.inc(SY.dma_start(
                    out=cx.xn_dm.rearrange("(k p) T2 -> p k T2", p=128)[:, :, t * 128:(t + 1) * 128],
                    in_=dmb[bi][:]), 16)
                iter_v.append(vC.n); iter_s.append(sC.n)
            SY.wait_ge(dS.h, dS.n)
            ies.close()
        blk.sync(emit)


def _s1(cx):
    nc = cx.nc
    SY, TE, VE, SC, GP = cx.SY, cx.TE, cx.VE, cx.SC, cx.GP
    with nc.Block("s1") as blk:
        def emit(_):
            ies = ExitStack()
            dW = CT(nc, ies, "s1_dW"); dL = CT(nc, ies, "s1_dL"); dS = CT(nc, ies, "s1_dS")
            tC = CT(nc, ies, "s1_tC"); vC = CT(nc, ies, "s1_vC"); sC = CT(nc, ies, "s1_sC")
            wz_sb = ies.enter_context(nc.sbuf_tensor("s1_wz", [128, 8, DI], BF16))
            wx_sb = ies.enter_context(nc.sbuf_tensor("s1_wx", [128, 8, DI], BF16))
            wb_sb = ies.enter_context(nc.sbuf_tensor("s1_wb", [128, 8, 160], BF16))
            xb = [ies.enter_context(nc.sbuf_tensor(f"s1_x{i}", [128, 8, 128], BF16)) for i in range(2)]
            zb = [ies.enter_context(nc.sbuf_tensor(f"s1_z{i}", [128, DI], BF16)) for i in range(2)]
            sxb = [ies.enter_context(nc.sbuf_tensor(f"s1_s{i}", [128, DI], BF16)) for i in range(2)]
            bcb = [ies.enter_context(nc.sbuf_tensor(f"s1_b{i}", [128, 128], BF16)) for i in range(2)]
            dtb = [ies.enter_context(nc.sbuf_tensor(f"s1_t{i}", [128, 32], F32)) for i in range(2)]
            pz = [ies.enter_context(nc.psum_tensor(f"s1_p{i}", [128, 512], F32)) for i in range(8)]
            dW.inc(GP.dma_start(out=wz_sb[:], in_=cx.w_z.rearrange("(k p) n -> p k n", p=128)), 16)
            dW.inc(GP.dma_start(out=wx_sb[:], in_=cx.w_x.rearrange("(k p) n -> p k n", p=128)), 16)
            dW.inc(GP.dma_start(out=wb_sb[:], in_=cx.w_bcd.rearrange("(k p) n -> p k n", p=128)), 16)
            iter_v, iter_s = [], []
            for t in range(NT):
                bi = t % 2
                if t >= 2:
                    SY.wait_ge(vC.h, iter_v[t - 2])
                    SY.wait_ge(sC.h, iter_s[t - 2])
                dL.inc(SY.dma_start(
                    out=xb[bi][:],
                    in_=cx.xn_dm.rearrange("(k p) T2 -> p k T2", p=128)[:, :, t * 128:(t + 1) * 128]), 16)
                wL = dL.n
                TE.wait_ge(dL.h, wL)
                if t == 0:
                    TE.wait_ge(dW.h, dW.n)
                if t >= 2:
                    TE.wait_ge(sC.h, iter_s[t - 2])
                    TE.wait_ge(vC.h, iter_v[t - 2])
                for k in range(8):
                    for n in range(8):
                        src = wz_sb[:, k, (n % 4) * 512:(n % 4 + 1) * 512] if n < 4 else \
                              wx_sb[:, k, (n - 4) * 512:(n - 3) * 512]
                        tC.inc(TE.matmul(pz[n][:], xb[bi][:, k, :], src,
                                         start=(k == 0), stop=(k == 7)))
                wT = tC.n
                SC.wait_ge(tC.h, wT); VE.wait_ge(tC.h, wT)
                for n in range(4):
                    sC.inc(SC.activation(zb[bi][:, n * 512:(n + 1) * 512], pz[n][:], AF.Silu))
                for n in range(4):
                    vC.inc(VE.tensor_copy(sxb[bi][:, n * 512:(n + 1) * 512], pz[4 + n][:]))
                wE_s, wE_v = sC.n, vC.n
                TE.wait_ge(sC.h, wE_s)
                for k in range(8):
                    tC.inc(TE.matmul(pz[0][:, 0:160], xb[bi][:, k, :], wb_sb[:, k, :],
                                     start=(k == 0), stop=(k == 7)))
                VE.wait_ge(tC.h, tC.n)
                vC.inc(VE.tensor_copy(bcb[bi][:], pz[0][:, 0:128]))
                vC.inc(VE.tensor_copy(dtb[bi][:], pz[0][:, 128:160]))
                wB = vC.n
                SY.wait_ge(sC.h, wE_s)
                dS.inc(SY.dma_start(out=cx.sz_tm[t * 128:(t + 1) * 128, :], in_=zb[bi][:]), 16)
                SY.wait_ge(vC.h, wE_v)
                dS.inc(SY.dma_start(out=cx.xs_tm[t * 128:(t + 1) * 128, :], in_=sxb[bi][:]), 16)
                SY.wait_ge(vC.h, wB)
                dS.inc(SY.dma_start(out=cx.bc_tm[t * 128:(t + 1) * 128, :], in_=bcb[bi][:]), 16)
                dS.inc(SY.dma_start(out=cx.dtr_tm[t * 128:(t + 1) * 128, :], in_=dtb[bi][:]), 16)
                iter_v.append(vC.n); iter_s.append(sC.n)
            SY.wait_ge(dS.h, dS.n)
            ies.close()
        blk.sync(emit)


S2DBG = False

def _s2(cx):
    nc = cx.nc
    SY, TE, VE, SC, GP = cx.SY, cx.TE, cx.VE, cx.SC, cx.GP
    dbgt = {}
    if S2DBG:
        for nm, sh, dt in [("dbg_dt4", [128, GRP * 32], F32), ("dbg_cumT", [128, GRP * 32], F32),
                           ("dbg_gcum", [128, GRP * 32], F32), ("dbg_biasT", [128, GRP * 32], F32),
                           ("dbg_lamT", [128, GRP * 32], F32), ("dbg_gdtT", [128, GRP * 32], F32),
                           ("dbg_sdup", [128, GRP * 1024], BF16), ("dbg_cdup", [128, GRP * 1024], BF16),
                           ("dbg_cnm", [32, GRP * 128], F32), ("dbg_dec", [64, 128], F32),
                           ("dbg_rot", [128, GRP * 128], BF16), ("dbg_bgdt", [128, GRP * 64], BF16),
                           ("dbg_cltm", [128, GRP * 64], BF16),
                           ("dbg_bnm", [64, 128], BF16), ("dbg_cnm2", [64, 128], BF16),
                           ("dbg_E", [128, 128], F32), ("dbg_Gm", [128, 128], BF16),
                           ("dbg_Wt", [128, 128], BF16), ("dbg_clnm", [64, 128], BF16),
                           ("dbg_H", [64, DI], BF16), ("dbg_y", [128, GRP * DI], BF16)]:
            dbgt[nm] = nc.dram_tensor(nm, sh, dt, kind="ExternalOutput").ap()
    with nc.Block("s2") as blk:
        def emit(_):
            ies = ExitStack()
            dW = CT(nc, ies, "s2_dW"); dL = CT(nc, ies, "s2_dL"); dS = CT(nc, ies, "s2_dS")
            tC = CT(nc, ies, "s2_tC"); vC = CT(nc, ies, "s2_vC"); sC = CT(nc, ies, "s2_sC")
            sbt = lambda n_, s, d: ies.enter_context(nc.sbuf_tensor(n_, list(s), d))
            wout_sb = sbt("s2_wo", [128, 16, D], BF16)
            dtr4 = sbt("s2_dtr", [128, GRP, 32], F32)
            bc4 = sbt("s2_bc", [128, GRP, 128], BF16)
            xs4 = sbt("s2_xs", [128, GRP, DI], BF16)
            sz4 = sbt("s2_sz", [128, GRP, DI], BF16)
            dt4 = sbt("s2_dt", [128, GRP, 32], F32)
            tmpa = sbt("s2_tmpa", [128, GRP, 32], F32)
            cumT = sbt("s2_cumT", [128, GRP * 32], F32)
            gcum = sbt("s2_gcum", [128, GRP * 32], F32)
            biasT = sbt("s2_biasT", [128, GRP * 32], F32)
            lamT = sbt("s2_lamT", [128, GRP * 32], F32)
            gdtT = sbt("s2_gdtT", [128, GRP * 32], F32)
            lndt = sbt("s2_lndt", [128, GRP * 32], F32)
            qa = sbt("s2_qa", [128, 2 * 512], F32)
            qi = sbt("s2_qi", [128, 2 * 512], I32)
            qf = sbt("s2_qf", [128, 2 * 512], F32)
            fr = sbt("s2_fr", [128, 2 * 512], F32)
            sdup = sbt("s2_sd", [128, GRP * 1024], BF16)
            cdup = sbt("s2_cd", [128, GRP * 1024], BF16)
            cnm = sbt("s2_cnm", [32, GRP * 128], F32)
            csrow = sbt("s2_cs", [1, 128], F32)
            offr = sbt("s2_off", [1, 160], F32)
            acdr = sbt("s2_acd", [1, 128], F32)
            decr = sbt("s2_dec", [1, 128], F32)
            dec_bc = sbt("s2_decb", [64, 128], F32)
            acd_bc = sbt("s2_acdb", [128, 128], F32)
            H = [sbt(f"s2_H{i}", [64, DI], BF16) for i in range(2)]
            rot = [sbt(f"s2_rot{i}", [128, GRP, 128], BF16) for i in range(2)]
            rtmp = sbt("s2_rtmp", [128, GRP, 32], F32)
            bgdt = [sbt(f"s2_bg{i}", [128, GRP, 64], BF16) for i in range(2)]
            cltm = [sbt(f"s2_cl{i}", [128, GRP, 64], BF16) for i in range(2)]
            bnm = [sbt(f"s2_bn{i}", [64, 128], BF16) for i in range(2)]
            cnm2 = [sbt(f"s2_cm{i}", [64, 128], BF16) for i in range(2)]
            clnm = [sbt(f"s2_cn{i}", [64, 128], BF16) for i in range(2)]
            Etile = [sbt(f"s2_E{i}", [128, 128], F32) for i in range(2)]
            Gm = [sbt(f"s2_Gm{i}", [128, 128], BF16) for i in range(2)]
            Wt = [sbt(f"s2_W{i}", [128, 128], BF16) for i in range(2)]
            ybuf = sbt("s2_y", [128, GRP, DI], BF16)
            yst = sbt("s2_yst", [128, GRP, 4], F32)
            ydmb = sbt("s2_ydm", [128, 16, 128], BF16)
            mo = [sbt(f"s2_mo{i}", [128, D], F32) for i in range(2)]
            pst = lambda n_, s, d=F32: ies.enter_context(nc.psum_tensor(n_, list(s), d))
            p_tpA = pst("s2_pta", [128, 1024], BF16)
            p_tpB = pst("s2_ptb", [128, 1024], BF16)
            p_R = [pst(f"s2_pr{i}", [128, 512]) for i in range(2)]
            p_G = [pst(f"s2_pg{i}", [128, 512]) for i in range(2)]
            p_Y = pst("s2_py", [128, 512])
            p_Sc = pst("s2_psc", [64, 512])
            p_sm = p_G[0]
            p_W = p_R
            dW.inc(GP.dma_start(out=wout_sb[:], in_=cx.w_out.rearrange("(k p) n -> p k n", p=128)), 16)
            VE.wait_ge(dW.h, dW.n)
            for k in range(16):
                vC.inc(VE.tensor_scalar(wout_sb[:, k, :], wout_sb[:, k, :],
                                        cx.P_nwc[:, k:k + 1], None, ALU.mult))
            wout_ready = vC.n
            mk_tpA = 0; mk_tpB = 0
            mk_R = [0, 0]; mk_G = [0, 0]; mk_Y = 0; mk_Sc = 0
            mk_rot = [0, 0]
            mk_nm = [0, 0]
            mk_W = [0, 0]
            mk_E = [0, 0]
            mk_mo = [0, 0]
            hcur = 1
            for b in range(BPC):
                VE.wait_ge(tC.h, tC.n)
                vC.inc(VE.memset(H[0][:], 0.0))
                vC.inc(VE.memset(H[1][:], 0.0))
                vC.inc(VE.memset(offr[:], 0.0))
                hinit = vC.n
                for g in range(NG):
                    g0 = b * L + g * GRP * Q
                    SY.wait_ge(vC.h, vC.n)
                    SY.wait_ge(sC.h, sC.n)
                    SY.wait_ge(tC.h, tC.n)
                    dL.inc(SY.dma_start(out=dtr4[:], in_=cx.dtr_tm[g0:g0 + 512, :]
                                        .rearrange("(c p) h -> p c h", p=128)), 16)
                    dL.inc(SY.dma_start(out=bc4[:], in_=cx.bc_tm[g0:g0 + 512, :]
                                        .rearrange("(c p) n -> p c n", p=128)), 16)
                    dL.inc(SY.dma_start(out=xs4[:], in_=cx.xs_tm[g0:g0 + 512, :]
                                        .rearrange("(c p) n -> p c n", p=128)), 16)
                    dL.inc(SY.dma_start(out=sz4[:], in_=cx.sz_tm[g0:g0 + 512, :]
                                        .rearrange("(c p) n -> p c n", p=128)), 16)
                    wL = dL.n
                    VE.wait_ge(dL.h, wL); SC.wait_ge(dL.h, wL)
                    VE.wait_ge(vC.h, hinit)
                    vC.inc(VE.tensor_tensor(dt4[:], dtr4[:],
                                            cx.P_dtb[:].unsqueeze(1).broadcast_to([128, GRP, 32]),
                                            ALU.add))
                    VE.wait_ge(vC.h, vC.n); SC.wait_ge(vC.h, vC.n)
                    sC.inc(SC.activation(tmpa[:], dt4[:], AF.Abs))
                    SC.wait_ge(sC.h, sC.n)
                    sC.inc(SC.activation(tmpa[:], tmpa[:], AF.Exp, scale=-1.0))
                    SC.wait_ge(sC.h, sC.n)
                    sC.inc(SC.activation(tmpa[:], tmpa[:], AF.Ln, bias=1.0))
                    VE.wait_ge(sC.h, sC.n)
                    vC.inc(VE.tensor_scalar(dt4[:], dt4[:], 0.0, None, ALU.max))
                    VE.wait_ge(vC.h, vC.n)
                    vC.inc(VE.tensor_tensor(dt4[:], dt4[:], tmpa[:], ALU.add))
                    VE.wait_ge(vC.h, vC.n)
                    wDT = vC.n
                    SC.wait_ge(vC.h, wDT)
                    sC.inc(SC.activation(lndt[:].rearrange("p (c h) -> p c h", h=32),
                                         dt4[:], AF.Ln))
                    TE.wait_ge(vC.h, max(wDT, mk_G[0], mk_W[0], mk_W[1]))
                    for c in range(GRP):
                        tC.inc(TE.matmul(p_sm[:, c * 32:(c + 1) * 32], cx.P_cum[:],
                                         dt4[:, c, :], start=(c == 0), stop=(c == GRP - 1)))
                    VE.wait_ge(tC.h, tC.n)
                    vC.inc(VE.tensor_copy(cumT[:], p_sm[:, 0:128]))
                    VE.wait_ge(vC.h, vC.n)
                    TE.wait_ge(vC.h, vC.n)
                    for c in range(GRP):
                        tC.inc(TE.matmul(p_sm[0:1, c * 32:(c + 1) * 32],
                                         cx.P_onescol[:], dt4[:, c, :],
                                         start=(c == 0), stop=(c == GRP - 1)))
                    VE.wait_ge(tC.h, tC.n)
                    vC.inc(VE.tensor_copy(csrow[:], p_sm[0:1, 0:128]))
                    VE.wait_ge(vC.h, vC.n)
                    vC.inc(VE.tensor_copy(offr[0:1, 0:32], offr[0:1, 128:160]))
                    for c in range(1, GRP):
                        VE.wait_ge(vC.h, vC.n)
                        vC.inc(VE.tensor_tensor(offr[0:1, c * 32:(c + 1) * 32],
                                                offr[0:1, (c - 1) * 32:c * 32],
                                                csrow[0:1, (c - 1) * 32:c * 32], ALU.add))
                    VE.wait_ge(vC.h, vC.n)
                    vC.inc(VE.tensor_tensor(offr[0:1, 128:160], offr[0:1, 96:128],
                                            csrow[0:1, 96:128], ALU.add))
                    vC.inc(VE.tensor_tensor(acdr[:].rearrange("o (c h) -> o c h", h=32),
                                            csrow[:].rearrange("o (c h) -> o c h", h=32),
                                            cx.P_rowA[0:1, 0:32].unsqueeze(1)
                                            .broadcast_to([1, GRP, 32]), ALU.mult))
                    VE.wait_ge(vC.h, vC.n)
                    wOFF = vC.n
                    SC.wait_ge(vC.h, wOFF)
                    sC.inc(SC.activation(decr[:], acdr[:], AF.Exp, scale=-1.0))
                    TE.wait_ge(vC.h, wOFF)
                    tC.inc(TE.matmul(p_sm[:, 0:128], cx.P_ones_f[:], offr[0:1, 0:128],
                                     start=True, stop=True))
                    VE.wait_ge(tC.h, tC.n)
                    vC.inc(VE.tensor_tensor(gcum[:], cumT[:], p_sm[:, 0:128], ALU.add))
                    VE.wait_ge(vC.h, vC.n)
                    TE.wait_ge(vC.h, vC.n)
                    tC.inc(TE.matmul(p_sm[:, 0:128], cx.P_ones_f[:], acdr[:],
                                     start=True, stop=True))
                    VE.wait_ge(tC.h, tC.n)
                    vC.inc(VE.tensor_copy(acd_bc[:], p_sm[:, 0:128]))
                    VE.wait_ge(vC.h, vC.n)
                    TE.wait_ge(vC.h, vC.n); TE.wait_ge(sC.h, sC.n)
                    tC.inc(TE.matmul(p_sm[0:64, 0:128], cx.P_ones_f[0:1, 0:64], decr[:],
                                     start=True, stop=True))
                    VE.wait_ge(tC.h, tC.n)
                    vC.inc(VE.tensor_copy(dec_bc[:], p_sm[0:64, 0:128]))
                    VE.wait_ge(vC.h, vC.n)
                    vC.inc(VE.tensor_tensor(biasT[:].rearrange("p (c h) -> p c h", h=32),
                                            cumT[:].rearrange("p (c h) -> p c h", h=32),
                                            cx.P_A[:, 0:32].unsqueeze(1)
                                            .broadcast_to([128, GRP, 32]), ALU.mult))
                    VE.wait_ge(vC.h, vC.n); SC.wait_ge(vC.h, vC.n)
                    sC.inc(SC.activation(lamT[:], biasT[:], AF.Exp, scale=-1.0))
                    VE.wait_ge(sC.h, sC.n)
                    vC.inc(VE.tensor_tensor(biasT[:], biasT[:], lndt[:], ALU.add))
                    VE.wait_ge(vC.h, vC.n)
                    vC.inc(VE.tensor_tensor(gdtT[:], biasT[:], acd_bc[:], ALU.subtract))
                    VE.wait_ge(vC.h, vC.n); SC.wait_ge(vC.h, vC.n)
                    sC.inc(SC.activation(gdtT[:], gdtT[:], AF.Exp))
                    for half in range(2):
                        cc = (2 * half, 2 * half + 1)
                        SC_mark = sC.n
                        VE.wait_ge(sC.h, SC_mark)
                        for ci, c in enumerate(cc):
                            vC.inc(VE.tensor_tensor(
                                qa[:, ci * 512:(ci + 1) * 512].rearrange("p (h j) -> p h j", j=16),
                                gcum[:, c * 32:(c + 1) * 32].unsqueeze(2).broadcast_to([128, 32, 16]),
                                cx.P_if2pi[:].rearrange("p (h j) -> p h j", j=16), ALU.mult))
                        VE.wait_ge(vC.h, vC.n)
                        vC.inc(VE.tensor_copy(qi[:], qa[:]))
                        VE.wait_ge(vC.h, vC.n)
                        vC.inc(VE.tensor_copy(qf[:], qi[:]))
                        VE.wait_ge(vC.h, vC.n)
                        vC.inc(VE.tensor_tensor(fr[:], qa[:], qf[:], ALU.subtract))
                        VE.wait_ge(vC.h, vC.n)
                        SC.wait_ge(vC.h, vC.n)
                        for ci, c in enumerate(cc):
                            for dup in range(2):
                                sC.inc(SC.activation(
                                    sdup[:, c * 1024:(c + 1) * 1024]
                                        .rearrange("p (h d j) -> p h d j", d=2, j=16)[:, :, dup, :],
                                    fr[:, ci * 512:(ci + 1) * 512].rearrange("p (h j) -> p h j", j=16),
                                    AF.Sin, scale=TWO_PI))
                        vC.inc(VE.tensor_scalar(qa[:], qa[:], 0.25, None, ALU.add))
                        VE.wait_ge(vC.h, vC.n)
                        vC.inc(VE.tensor_copy(qi[:], qa[:]))
                        VE.wait_ge(vC.h, vC.n)
                        vC.inc(VE.tensor_copy(qf[:], qi[:]))
                        VE.wait_ge(vC.h, vC.n)
                        VE.wait_ge(sC.h, sC.n)
                        vC.inc(VE.tensor_tensor(fr[:], qa[:], qf[:], ALU.subtract))
                        VE.wait_ge(vC.h, vC.n)
                        SC.wait_ge(vC.h, vC.n)
                        for ci, c in enumerate(cc):
                            for dup in range(2):
                                sC.inc(SC.activation(
                                    cdup[:, c * 1024:(c + 1) * 1024]
                                        .rearrange("p (h d j) -> p h d j", d=2, j=16)[:, :, dup, :],
                                    fr[:, ci * 512:(ci + 1) * 512].rearrange("p (h j) -> p h j", j=16),
                                    AF.Sin, scale=TWO_PI))
                    TE.wait_ge(vC.h, max(vC.n, mk_W[0], mk_W[1]))
                    TE.wait_ge(sC.h, max(mk_E[0], mk_E[1]))
                    for c in range(GRP):
                        tC.inc(TE.matmul(p_R[0][0:32, c * 128:(c + 1) * 128],
                                         cumT[:, c * 32:(c + 1) * 32], cx.P_idt32[:],
                                         is_transpose=True, start=(c == 0), stop=(c == GRP - 1)))
                    VE.wait_ge(tC.h, tC.n)
                    vC.inc(VE.tensor_copy(cnm[:], p_R[0][0:32, 0:512]))
                    mk_R[0] = 10 ** 9
                    wA_v, wA_s = vC.n, sC.n
                    mk_R[0] = wA_v
                    if S2DBG and b == 0 and g == 0:
                        SY.wait_ge(vC.h, vC.n); SY.wait_ge(sC.h, sC.n)
                        for nm, tt in [("dbg_dt4", dt4), ("dbg_cumT", cumT), ("dbg_gcum", gcum),
                                       ("dbg_biasT", biasT), ("dbg_lamT", lamT),
                                       ("dbg_gdtT", gdtT), ("dbg_sdup", sdup),
                                       ("dbg_cdup", cdup), ("dbg_cnm", cnm), ("dbg_dec", dec_bc)]:
                            dS.inc(SY.dma_start(out=dbgt[nm][:], in_=tt[:].rearrange("p a b -> p (a b)") if tt[:].ndim == 3 else tt[:]), 16)
                    for h in range(NH):
                        hb = h % 2
                        VE.wait_ge(tC.h, mk_rot[hb])
                        VE.wait_ge(sC.h, wA_s)
                        co2 = cdup[:].rearrange("p (c h2 x) -> p c h2 x", h2=32, x=32)[:, :, h, :] \
                            .rearrange("p c (g j) -> p c g j", g=2)
                        si2 = sdup[:].rearrange("p (c h2 x) -> p c h2 x", h2=32, x=32)[:, :, h, :] \
                            .rearrange("p c (g j) -> p c g j", g=2)
                        bview = bc4[:].rearrange("p c (g n) -> p c g n", g=2)
                        rview = rot[hb][:].rearrange("p c (g n) -> p c g n", g=2)
                        bev, bod = bview[:, :, :, 0:32:2], bview[:, :, :, 1:32:2]
                        rev, rod = rview[:, :, :, 0:32:2], rview[:, :, :, 1:32:2]
                        rt = rtmp[:].rearrange("p c (g j) -> p c g j", g=2)
                        vC.inc(VE.tensor_copy(rview[:, :, :, 32:64], bview[:, :, :, 32:64]))
                        vC.inc(VE.tensor_tensor(rev, bev, co2, ALU.mult))
                        vC.inc(VE.tensor_tensor(rt, bod, si2, ALU.mult))
                        VE.wait_ge(vC.h, vC.n)
                        vC.inc(VE.tensor_tensor(rev, rev, rt, ALU.add))
                        vC.inc(VE.tensor_tensor(rod, bod, co2, ALU.mult))
                        vC.inc(VE.tensor_tensor(rt, bev, si2, ALU.mult))
                        VE.wait_ge(vC.h, vC.n)
                        vC.inc(VE.tensor_tensor(rod, rod, rt, ALU.subtract))
                        VE.wait_ge(vC.h, vC.n)
                        vC.inc(VE.tensor_tensor(
                            bgdt[hb][:], rot[hb][:, :, 0:64],
                            gdtT[:].rearrange("p (c h2) -> p c h2", h2=32)[:, :, h:h + 1]
                                .broadcast_to([128, GRP, 64]), ALU.mult))
                        vC.inc(VE.tensor_tensor(
                            cltm[hb][:], rot[hb][:, :, 64:128],
                            lamT[:].rearrange("p (c h2) -> p c h2", h2=32)[:, :, h:h + 1]
                                .broadcast_to([128, GRP, 64]), ALU.mult))
                        VE.wait_ge(vC.h, vC.n)
                        wROT = vC.n
                        for c in range(GRP):
                            cs = c % 2
                            TE.wait_ge(vC.h, max(wROT, mk_nm[0], mk_nm[1]))
                            tC.inc(TE.matmul(p_tpA[:, 0:128],
                                             rot[hb][:, c, :], cx.P_idt16[:],
                                             is_transpose=True, start=True, stop=True))
                            tC.inc(TE.matmul(p_tpB[0:64, 0:128],
                                             cltm[hb][:, c, :], cx.P_idt16[:],
                                             is_transpose=True, start=True, stop=True))
                            wTP = tC.n
                            VE.wait_ge(tC.h, wTP)
                            vC.inc(VE.tensor_copy(bnm[cs][:], p_tpA[0:64, 0:128]))
                            vC.inc(VE.tensor_copy(cnm2[cs][:], p_tpA[64:128, 0:128]))
                            vC.inc(VE.tensor_copy(clnm[cs][:], p_tpB[0:64, 0:128]))
                            wNM = vC.n
                            mk_nm[cs] = wNM
                            TE.wait_ge(sC.h, mk_E[cs])
                            TE.wait_ge(vC.h, mk_W[cs])
                            tC.inc(TE.matmul(p_R[cs][:, 0:128],
                                             cx.P_onehot[:, h * 128:(h + 1) * 128],
                                             cnm[:, c * 128:(c + 1) * 128],
                                             start=True, stop=True))
                            wR = tC.n
                            SC.wait_ge(tC.h, wR)
                            sC.inc(SC.activation(Etile[cs][:], p_R[cs][:, 0:128], AF.Exp,
                                                 scale=cx.P_negA[:, h:h + 1],
                                                 bias=biasT[:, c * 32 + h:c * 32 + h + 1]))
                            mk_E[cs] = sC.n
                            TE.wait_ge(vC.h, wNM)
                            TE.wait_ge(vC.h, mk_G[cs])
                            tC.inc(TE.matmul(p_G[cs][:, 0:128], bnm[cs][:],
                                             cnm2[cs][:], start=True, stop=True))
                            wG = tC.n
                            VE.wait_ge(tC.h, wG)
                            vC.inc(VE.scalar_tensor_tensor(Gm[cs][:], p_G[cs][:, 0:128], 1.0,
                                                           cx.P_wmask[:], ALU.mult, ALU.mult))
                            mk_G[cs] = vC.n
                            VE.wait_ge(vC.h, vC.n); VE.wait_ge(sC.h, mk_E[cs])
                            vC.inc(VE.scalar_tensor_tensor(Wt[cs][:], Etile[cs][:], BIG,
                                                           Gm[cs][:], ALU.min, ALU.mult))
                            wW = vC.n
                            TE.wait_ge(vC.h, max(wW, mk_Y, mk_Sc))
                            tC.inc(TE.matmul(p_Y[:, 0:64], Wt[cs][:],
                                             xs4[:, c, h * 64:(h + 1) * 64],
                                             start=True, stop=False))
                            tC.inc(TE.matmul(p_Y[:, 0:64], clnm[cs][:],
                                             H[hcur][:, h * 64:(h + 1) * 64],
                                             start=False, stop=True))
                            tC.inc(TE.matmul(p_Sc[:, 0:64], bgdt[hb][:, c, :],
                                             xs4[:, c, h * 64:(h + 1) * 64],
                                             start=True, stop=True))
                            wYS = tC.n
                            VE.wait_ge(tC.h, wYS)
                            vC.inc(VE.scalar_tensor_tensor(
                                H[1 - hcur][:, h * 64:(h + 1) * 64],
                                H[hcur][:, h * 64:(h + 1) * 64],
                                dec_bc[:, c * 32 + h:c * 32 + h + 1],
                                p_Sc[:, 0:64], ALU.mult, ALU.add))
                            mk_Sc = vC.n
                            vC.inc(VE.scalar_tensor_tensor(
                                ybuf[:, c, h * 64:(h + 1) * 64],
                                xs4[:, c, h * 64:(h + 1) * 64],
                                cx.P_dsk[:, h:h + 1],
                                p_Y[:, 0:64], ALU.mult, ALU.add))
                            mk_Y = vC.n
                            if S2DBG and b == 0 and g == 0 and h == 0 and c == 0:
                                SY.wait_ge(vC.h, vC.n); SY.wait_ge(sC.h, sC.n)
                                for nm, tt in [("dbg_rot", rot[hb]), ("dbg_bgdt", bgdt[hb]),
                                               ("dbg_cltm", cltm[hb]), ("dbg_bnm", bnm[cs]),
                                               ("dbg_cnm2", cnm2[cs]), ("dbg_E", Etile[cs]),
                                               ("dbg_Gm", Gm[cs]), ("dbg_Wt", Wt[cs]),
                                               ("dbg_clnm", clnm[cs]),
                                               ("dbg_H", H[1 - hcur])]:
                                    dS.inc(SY.dma_start(out=dbgt[nm][:], in_=tt[:].rearrange("p a b -> p (a b)") if tt[:].ndim == 3 else tt[:]), 16)
                            hcur = 1 - hcur
                        mk_rot[hb] = tC.n
                    VE.wait_ge(vC.h, vC.n)
                    vC.inc(VE.tensor_tensor(ybuf[:], ybuf[:], sz4[:], ALU.mult))
                    VE.wait_ge(vC.h, vC.n)
                    SC.wait_ge(vC.h, vC.n)
                    for c in range(GRP):
                        jdst = sdup if c < 2 else cdup
                        sC.inc(SC.activation(jdst[:, (c % 2) * 2048:(c % 2) * 2048 + 2048],
                                             ybuf[:, c, :], AF.Square,
                                             accum_out=yst[:, c, 0:1]))
                    SC.wait_ge(sC.h, sC.n)
                    for c in range(GRP):
                        sC.inc(SC.activation(yst[:, c, 1:2], yst[:, c, 0:1], AF.Ln,
                                             scale=1.0 / DI, bias=EPS_LN))
                    SC.wait_ge(sC.h, sC.n)
                    for c in range(GRP):
                        sC.inc(SC.activation(yst[:, c, 2:3], yst[:, c, 1:2], AF.Exp, scale=-0.5))
                    VE.wait_ge(sC.h, sC.n)
                    vC.inc(VE.tensor_tensor(ybuf[:], ybuf[:],
                                            yst[:, :, 2:3].broadcast_to([128, GRP, DI]),
                                            ALU.mult))
                    VE.wait_ge(vC.h, vC.n)
                    wYGN = vC.n
                    if S2DBG and b == 0 and g == 0:
                        SY.wait_ge(vC.h, vC.n)
                        dS.inc(SY.dma_start(out=dbgt["dbg_y"][:],
                                            in_=ybuf[:].rearrange("p a b -> p (a b)")), 16)
                    for c in range(GRP):
                        TE.wait_ge(vC.h, wYGN)
                        if b == 0 and g == 0 and c == 0:
                            TE.wait_ge(vC.h, wout_ready)
                        for k in range(16):
                            ks = k % 2
                            TE.wait_ge(vC.h, mk_tpA if ks == 0 else mk_tpB)
                            dstp = p_tpA if ks == 0 else p_tpB
                            tC.inc(TE.matmul(dstp[:, 0:128], ybuf[:, c, k * 128:(k + 1) * 128],
                                             cx.P_idt16[:], is_transpose=True,
                                             start=True, stop=True))
                            VE.wait_ge(tC.h, tC.n)
                            vC.inc(VE.tensor_copy(ydmb[:, k, :], dstp[:, 0:128]))
                            if ks == 0:
                                mk_tpA = vC.n
                            else:
                                mk_tpB = vC.n
                        wYD = vC.n
                        TE.wait_ge(vC.h, wYD)
                        TE.wait_ge(vC.h, max(mk_W[0], mk_W[1]))
                        TE.wait_ge(sC.h, max(mk_E[0], mk_E[1]))
                        for k in range(16):
                            for n in range(2):
                                tC.inc(TE.matmul(p_W[n][:, 0:512], ydmb[:, k, :],
                                                 wout_sb[:, k, n * 512:(n + 1) * 512],
                                                 start=(k == 0), stop=(k == 15)))
                        VE.wait_ge(tC.h, tC.n)
                        VE.wait_ge(dS.h, mk_mo[c % 2])
                        vC.inc(VE.tensor_copy(mo[c % 2][:, 0:512], p_W[0][:, 0:512]))
                        vC.inc(VE.tensor_copy(mo[c % 2][:, 512:1024], p_W[1][:, 0:512]))
                        mk_W[0] = vC.n; mk_W[1] = vC.n
                        r0 = g0 + c * 128
                        SY.wait_ge(vC.h, vC.n)
                        dS.inc(SY.dma_start(out=cx.mamba_tm[r0:r0 + 128, :], in_=mo[c % 2][:]), 16)
                        mk_mo[c % 2] = dS.n
            SY.wait_ge(dS.h, dS.n)
            ies.close()
        blk.sync(emit)


S3ABL = ("h", "tp", "v", "qk")

def _s3(cx):
    nc = cx.nc
    SY, TE, VE, SC, GP = cx.SY, cx.TE, cx.VE, cx.SC, cx.GP
    with nc.Block("s3") as blk:
        def emit(_):
            ies = ExitStack()
            dW = CT(nc, ies, "s3_dW"); dL = CT(nc, ies, "s3_dL"); dS = CT(nc, ies, "s3_dS")
            tC = CT(nc, ies, "s3_tC"); vC = CT(nc, ies, "s3_vC"); sC = CT(nc, ies, "s3_sC")
            wq_sb = ies.enter_context(nc.sbuf_tensor("s3_wq", [128, 8, D], BF16))
            wk_sb = ies.enter_context(nc.sbuf_tensor("s3_wk", [128, 8, D], BF16))
            wv_sb = ies.enter_context(nc.sbuf_tensor("s3_wv", [128, 8, D], BF16))
            xb = [ies.enter_context(nc.sbuf_tensor(f"s3_x{i}", [128, D], BF16)) for i in range(2)]
            st = ies.enter_context(nc.sbuf_tensor("s3_st", [128, 40], F32))
            h16 = [ies.enter_context(nc.sbuf_tensor(f"s3_h{i}", [128, D], BF16)) for i in range(2)]
            hdm = [ies.enter_context(nc.sbuf_tensor(f"s3_hd{i}", [128, 8, 128], BF16)) for i in range(2)]
            sq = ies.enter_context(nc.sbuf_tensor("s3_sq", [128, D], F32))
            junk = ies.enter_context(nc.sbuf_tensor("s3_j", [128, D], F32))
            qtm = ies.enter_context(nc.sbuf_tensor("s3_qtm", [128, D], BF16))
            qdmb = ies.enter_context(nc.sbuf_tensor("s3_qd", [128, 8, 128], BF16))
            vtm = [ies.enter_context(nc.sbuf_tensor(f"s3_v{i}", [128, D], BF16)) for i in range(2)]
            ptp = [ies.enter_context(nc.psum_tensor(f"s3_p{i}", [128, 1024], BF16)) for i in range(2)]
            pmm = [ies.enter_context(nc.psum_tensor(f"s3_m{i}", [128, 512], F32)) for i in range(2)]
            dW.inc(GP.dma_start(out=wq_sb[:], in_=cx.wq_t.rearrange("(k p) n -> p k n", p=128)), 16)
            dW.inc(GP.dma_start(out=wk_sb[:], in_=cx.wk_t.rearrange("(k p) n -> p k n", p=128)), 16)
            dW.inc(GP.dma_start(out=wv_sb[:], in_=cx.wv_t.rearrange("(k p) n -> p k n", p=128)), 16)
            iter_v, iter_s = [], []
            mk_tp = [0, 0]
            for t in range(NT):
                bi = t % 2
                if t >= 2:
                    SY.wait_ge(vC.h, iter_v[t - 2])
                    SY.wait_ge(sC.h, iter_s[t - 2])
                dL.inc(SY.dma_start(out=xb[bi][:], in_=cx.xn_tm[t * 128:(t + 1) * 128, :]), 16)
                wL = dL.n
                if "h" not in S3ABL:
                    iter_v.append(vC.n); iter_s.append(sC.n)
                    continue
                SC.wait_ge(dL.h, wL)
                sC.inc(SC.activation(junk[:], xb[bi][:], AF.Square, accum_out=st[:, 0:1]))
                SC.wait_ge(sC.h, sC.n)
                sC.inc(SC.activation(st[:, 1:2], st[:, 0:1], AF.Ln, scale=1.0 / D, bias=EPS_RMS))
                SC.wait_ge(sC.h, sC.n)
                sC.inc(SC.activation(st[:, 2:3], st[:, 1:2], AF.Exp, scale=-0.5))
                VE.wait_ge(sC.h, sC.n); VE.wait_ge(dL.h, wL)
                vC.inc(VE.tensor_scalar(h16[bi][:], xb[bi][:], st[:, 2:3], None, ALU.mult))
                VE.wait_ge(vC.h, vC.n)
                vC.inc(VE.tensor_tensor(h16[bi][:], h16[bi][:], cx.P_n1t[:], ALU.mult))
                VE.wait_ge(vC.h, vC.n)
                wH = vC.n
                if "tp" not in S3ABL:
                    iter_v.append(vC.n); iter_s.append(sC.n)
                    continue
                TE.wait_ge(vC.h, wH)
                if t == 0:
                    TE.wait_ge(dW.h, dW.n)
                for k in range(8):
                    ks = k % 2
                    TE.wait_ge(vC.h, mk_tp[ks])
                    tC.inc(TE.matmul(ptp[ks][:, 0:128], h16[bi][:, k * 128:(k + 1) * 128],
                                     cx.P_idt16[:], is_transpose=True, start=True, stop=True))
                    VE.wait_ge(tC.h, tC.n)
                    vC.inc(VE.tensor_copy(hdm[bi][:, k, :], ptp[ks][:, 0:128]))
                    mk_tp[ks] = vC.n
                wHD = vC.n
                if "v" not in S3ABL:
                    iter_v.append(vC.n); iter_s.append(sC.n)
                    continue
                TE.wait_ge(vC.h, wHD)
                if t >= 2:
                    TE.wait_ge(vC.h, iter_v[t - 2])
                for n in range(2):
                    for k in range(8):
                        tC.inc(TE.matmul(pmm[n][:], hdm[bi][:, k, :],
                                         wv_sb[:, k, n * 512:(n + 1) * 512],
                                         start=(k == 0), stop=(k == 7)))
                VE.wait_ge(tC.h, tC.n)
                vC.inc(VE.tensor_copy(vtm[bi][:, 0:512], pmm[0][:]))
                vC.inc(VE.tensor_copy(vtm[bi][:, 512:1024], pmm[1][:]))
                SY.wait_ge(vC.h, vC.n)
                dS.inc(SY.dma_start(out=cx.v_tm[t * 128:(t + 1) * 128, :], in_=vtm[bi][:]), 16)
                if "qk" not in S3ABL:
                    iter_v.append(vC.n); iter_s.append(sC.n)
                    continue
                for wsb, ntile, dst in ((wq_sb, cx.P_qnt, cx.q_dm), (wk_sb, cx.P_knt, cx.k_dm)):
                    TE.wait_ge(vC.h, vC.n)
                    for n in range(2):
                        for k in range(8):
                            tC.inc(TE.matmul(pmm[n][:], hdm[bi][:, k, :],
                                             wsb[:, k, n * 512:(n + 1) * 512],
                                             start=(k == 0), stop=(k == 7)))
                    wQ = tC.n
                    SC.wait_ge(tC.h, wQ)
                    sC.inc(SC.activation(sq[:, 0:512], pmm[0][:], AF.Square))
                    sC.inc(SC.activation(sq[:, 512:1024], pmm[1][:], AF.Square))
                    VE.wait_ge(sC.h, sC.n)
                    for hh in range(16):
                        vC.inc(VE.reduce_sum(st[:, 8 + hh:9 + hh],
                                             sq[:, hh * 64:(hh + 1) * 64], axis=AX.X))
                    VE.wait_ge(vC.h, vC.n)
                    SC.wait_ge(vC.h, vC.n)
                    sC.inc(SC.activation(st[:, 24:40], st[:, 8:24], AF.Ln,
                                         scale=1.0 / HD, bias=EPS_RMS))
                    SC.wait_ge(sC.h, sC.n)
                    sC.inc(SC.activation(st[:, 24:40], st[:, 24:40], AF.Exp, scale=-0.5))
                    VE.wait_ge(sC.h, sC.n)
                    vC.inc(VE.tensor_tensor(
                        qtm[:, 0:512].rearrange("p (h x) -> p h x", x=64),
                        pmm[0][:].rearrange("p (h x) -> p h x", x=64),
                        st[:, 24:32].unsqueeze(2).broadcast_to([128, 8, 64]), ALU.mult))
                    vC.inc(VE.tensor_tensor(
                        qtm[:, 512:1024].rearrange("p (h x) -> p h x", x=64),
                        pmm[1][:].rearrange("p (h x) -> p h x", x=64),
                        st[:, 32:40].unsqueeze(2).broadcast_to([128, 8, 64]), ALU.mult))
                    VE.wait_ge(vC.h, vC.n)
                    vC.inc(VE.tensor_tensor(qtm[:], qtm[:], ntile[:], ALU.mult))
                    VE.wait_ge(vC.h, vC.n)
                    wQT = vC.n
                    TE.wait_ge(vC.h, wQT)
                    for k in range(8):
                        ks = k % 2
                        TE.wait_ge(vC.h, mk_tp[ks])
                        tC.inc(TE.matmul(ptp[ks][:, 0:128], qtm[:, k * 128:(k + 1) * 128],
                                         cx.P_idt16[:], is_transpose=True, start=True, stop=True))
                        VE.wait_ge(tC.h, tC.n)
                        vC.inc(VE.tensor_copy(qdmb[:, k, :], ptp[ks][:, 0:128]))
                        mk_tp[ks] = vC.n
                    SY.wait_ge(vC.h, vC.n)
                    dS.inc(SY.dma_start(
                        out=dst.rearrange("(k p) T2 -> p k T2", p=128)[:, :, t * 128:(t + 1) * 128],
                        in_=qdmb[:]), 16)
                    SY.wait_ge(dS.h, dS.n)
                iter_v.append(vC.n); iter_s.append(sC.n)
            SY.wait_ge(dS.h, dS.n)
            ies.close()
        blk.sync(emit)


def _s4(cx):
    nc = cx.nc
    SY, TE, VE, SC = cx.SY, cx.TE, cx.VE, cx.SC
    with nc.Block("s4") as blk:
        def emit(_):
            ies = ExitStack()
            dL = CT(nc, ies, "s4_dL"); dS = CT(nc, ies, "s4_dS")
            tC = CT(nc, ies, "s4_tC"); vC = CT(nc, ies, "s4_vC"); sC = CT(nc, ies, "s4_sC")
            NS = 4
            qb_ = [ies.enter_context(nc.sbuf_tensor(f"s4_q{i}", [64, L], BF16)) for i in range(2)]
            kb_ = [ies.enter_context(nc.sbuf_tensor(f"s4_k{i}", [64, L], BF16)) for i in range(2)]
            vb_ = [ies.enter_context(nc.sbuf_tensor(f"s4_v{i}", [128, 16, 65], BF16)) for i in range(2)]
            pb_ = [ies.enter_context(nc.sbuf_tensor(f"s4_pb{i}", [128, 512], BF16)) for i in range(NS)]
            o_un = [ies.enter_context(nc.sbuf_tensor(f"s4_ou{i}", [65, 4 * 512], F32)) for i in range(2)]
            rcp = ies.enter_context(nc.sbuf_tensor("s4_rcp", [1, 4 * 512], F32))
            rbc = ies.enter_context(nc.sbuf_tensor("s4_rbc", [64, 512], F32))
            ob = [ies.enter_context(nc.sbuf_tensor(f"s4_ob{i}", [64, 512], BF16)) for i in range(2)]
            pS = [ies.enter_context(nc.psum_tensor(f"s4_s{i}", [128, 512], F32)) for i in range(NS)]
            pO = ies.enter_context(nc.psum_tensor("s4_o", [65, 512], F32))
            pB = ies.enter_context(nc.psum_tensor("s4_b", [64, 512], F32))
            vC.inc(VE.memset(vb_[0][:, :, 64:65], 1.0))
            vC.inc(VE.memset(vb_[1][:, :, 64:65], 1.0))
            iter_t = []
            mk_S = [0] * NS
            mk_O = 0
            mk_B = 0
            mk_ou = [0, 0]
            mk_ob = [0, 0]
            for it in range(BPC * HEADS):
                b, h = divmod(it, HEADS)
                bi = it % 2
                if it >= 2:
                    SY.wait_ge(tC.h, iter_t[it - 2])
                dL.inc(SY.dma_start(out=qb_[bi][:],
                                    in_=cx.q_dm[h * 64:(h + 1) * 64, b * L:(b + 1) * L]), 16)
                dL.inc(SY.dma_start(out=kb_[bi][:],
                                    in_=cx.k_dm[h * 64:(h + 1) * 64, b * L:(b + 1) * L]), 16)
                dL.inc(SY.dma_start(
                    out=vb_[bi][:, :, 0:64],
                    in_=cx.v_tm[b * L:(b + 1) * L, :]
                        .rearrange("(kt p) (h2 j) -> p kt h2 j", p=128, j=64)[:, :, h, :]), 16)
                wL = dL.n
                TE.wait_ge(dL.h, wL)
                ks_ctr = 0
                for qblk in range(4):
                    nkt = 4 * qblk + 4
                    for kt in range(nkt):
                        ks = ks_ctr % NS
                        ks_ctr += 1
                        TE.wait_ge(sC.h, mk_S[ks])
                        tC.inc(TE.matmul(pS[ks][:], kb_[bi][:, kt * 128:(kt + 1) * 128],
                                         qb_[bi][:, qblk * 512:(qblk + 1) * 512],
                                         start=True, stop=True))
                        wS = tC.n
                        dd = kt - 4 * qblk
                        if dd >= 0:
                            VE.wait_ge(tC.h, wS)
                            vC.inc(VE.tensor_tensor(pS[ks][:], pS[ks][:],
                                                    cx.P_amask[:, dd * 512:(dd + 1) * 512],
                                                    ALU.add))
                            SC.wait_ge(vC.h, vC.n)
                        SC.wait_ge(tC.h, wS)
                        sC.inc(SC.activation(pb_[ks][:], pS[ks][:], AF.Exp,
                                             scale=0.125, bias=cx.P_negc[:, 0:1]))
                        mk_S[ks] = sC.n
                        TE.wait_ge(sC.h, sC.n)
                        if kt == 0:
                            TE.wait_ge(vC.h, mk_O)
                        tC.inc(TE.matmul(pO[:], vb_[bi][:, kt, :], pb_[ks][:],
                                         start=(kt == 0), stop=(kt == nkt - 1)))
                    wPV = tC.n
                    VE.wait_ge(tC.h, wPV)
                    if qblk == 0:
                        VE.wait_ge(vC.h, mk_ou[bi])
                    vC.inc(VE.tensor_copy(o_un[bi][:, qblk * 512:(qblk + 1) * 512], pO[:]))
                    mk_O = vC.n
                SC.wait_ge(vC.h, mk_O)
                sC.inc(SC.activation(rcp[0:1, :], o_un[bi][64:65, :], AF.Ln))
                SC.wait_ge(sC.h, sC.n)
                sC.inc(SC.activation(rcp[0:1, :], rcp[0:1, :], AF.Exp, scale=-1.0))
                wRC = sC.n
                for qblk in range(4):
                    TE.wait_ge(sC.h, wRC)
                    TE.wait_ge(vC.h, mk_B)
                    tC.inc(TE.matmul(pB[:], cx.P_ones_f[0:1, 0:64],
                                     rcp[0:1, qblk * 512:(qblk + 1) * 512],
                                     start=True, stop=True))
                    VE.wait_ge(tC.h, tC.n)
                    vC.inc(VE.tensor_copy(rbc[:], pB[:]))
                    mk_B = vC.n
                    VE.wait_ge(vC.h, vC.n)
                    VE.wait_ge(dS.h, mk_ob[qblk % 2])
                    vC.inc(VE.scalar_tensor_tensor(ob[qblk % 2][:],
                                                   o_un[bi][0:64, qblk * 512:(qblk + 1) * 512],
                                                   1.0, rbc[:], ALU.mult, ALU.mult))
                    SY.wait_ge(vC.h, vC.n)
                    dS.inc(SY.dma_start(
                        out=cx.ao_dm[h * 64:(h + 1) * 64,
                                     b * L + qblk * 512:b * L + (qblk + 1) * 512],
                        in_=ob[qblk % 2][:]), 16)
                    mk_ob[qblk % 2] = dS.n
                mk_ou[bi] = vC.n
                iter_t.append(tC.n)
            SY.wait_ge(dS.h, dS.n)
            ies.close()
        blk.sync(emit)


def _s4b(cx):
    nc = cx.nc
    SY, TE, VE, GP = cx.SY, cx.TE, cx.VE, cx.GP
    with nc.Block("s4b") as blk:
        def emit(_):
            ies = ExitStack()
            dW = CT(nc, ies, "s4b_dW"); dL = CT(nc, ies, "s4b_dL"); dS = CT(nc, ies, "s4b_dS")
            tC = CT(nc, ies, "s4b_tC"); vC = CT(nc, ies, "s4b_vC")
            wo_sb = ies.enter_context(nc.sbuf_tensor("s4b_wo", [128, 8, D - MZ], BF16))
            ab = [ies.enter_context(nc.sbuf_tensor(f"s4b_a{i}", [128, 8, 128], BF16)) for i in range(2)]
            ob = [ies.enter_context(nc.sbuf_tensor(f"s4b_o{i}", [128, D - MZ], F32)) for i in range(2)]
            pm = [ies.enter_context(nc.psum_tensor(f"s4b_p{i}", [128, 512], F32)) for i in range(2)]
            dW.inc(GP.dma_start(out=wo_sb[:],
                                in_=cx.wo_t[:, MZ:].rearrange("(k p) n -> p k n", p=128)), 16)
            iter_v = []
            for t in range(NT):
                bi = t % 2
                if t >= 2:
                    SY.wait_ge(vC.h, iter_v[t - 2])
                dL.inc(SY.dma_start(
                    out=ab[bi][:],
                    in_=cx.ao_dm.rearrange("(k p) T2 -> p k T2", p=128)[:, :, t * 128:(t + 1) * 128]), 16)
                wL = dL.n
                TE.wait_ge(dL.h, wL)
                if t == 0:
                    TE.wait_ge(dW.h, dW.n)
                if t >= 2:
                    TE.wait_ge(vC.h, iter_v[t - 2])
                for n in range(2):
                    w0, w1 = n * 512, min((n + 1) * 512, D - MZ)
                    for k in range(8):
                        tC.inc(TE.matmul(pm[n][:, 0:w1 - w0], ab[bi][:, k, :],
                                         wo_sb[:, k, w0:w1], start=(k == 0), stop=(k == 7)))
                VE.wait_ge(tC.h, tC.n)
                vC.inc(VE.tensor_copy(ob[bi][:, 0:512], pm[0][:]))
                vC.inc(VE.tensor_copy(ob[bi][:, 512:768], pm[1][:, 0:256]))
                SY.wait_ge(vC.h, vC.n)
                dS.inc(SY.dma_start(out=cx.acc_tm[t * 128:(t + 1) * 128, :], in_=ob[bi][:]), 16)
                iter_v.append(vC.n)
            SY.wait_ge(dS.h, dS.n)
            ies.close()
        blk.sync(emit)


def _s5(cx):
    nc = cx.nc
    SY, TE, VE, SC, GP = cx.SY, cx.TE, cx.VE, cx.SC, cx.GP
    with nc.Block("s5") as blk:
        def emit(_):
            ies = ExitStack()
            dW = CT(nc, ies, "s5_dW"); dL = CT(nc, ies, "s5_dL"); dS = CT(nc, ies, "s5_dS")
            tC = CT(nc, ies, "s5_tC"); vC = CT(nc, ies, "s5_vC"); sC = CT(nc, ies, "s5_sC")
            gw_sb = ies.enter_context(nc.sbuf_tensor("s5_gw", [128, 8, 4], BF16))
            xnb = [ies.enter_context(nc.sbuf_tensor(f"s5_x{i}", [128, D], F32)) for i in range(2)]
            acb = [ies.enter_context(nc.sbuf_tensor(f"s5_a{i}", [128, D - MZ], F32)) for i in range(2)]
            x2 = ies.enter_context(nc.sbuf_tensor("s5_x2", [128, D], F32))
            st = ies.enter_context(nc.sbuf_tensor("s5_st", [128, 4], F32))
            junk = ies.enter_context(nc.sbuf_tensor("s5_j", [128, D], F32))
            xf16 = ies.enter_context(nc.sbuf_tensor("s5_xf", [128, D], BF16))
            xfd = [ies.enter_context(nc.sbuf_tensor(f"s5_d{i}", [128, 8, 128], BF16)) for i in range(2)]
            gate = ies.enter_context(nc.sbuf_tensor("s5_g", [128, 24], F32))
            gm = ies.enter_context(nc.sbuf_tensor("s5_gm", [128, 32], F32))
            wt4 = [ies.enter_context(nc.sbuf_tensor(f"s5_w{i}", [128, 4], F32)) for i in range(2)]
            ptp = [ies.enter_context(nc.psum_tensor(f"s5_p{i}", [128, 1024], BF16)) for i in range(2)]
            pg = ies.enter_context(nc.psum_tensor("s5_pg", [128, 512], F32))
            dW.inc(GP.dma_start(out=gw_sb[:], in_=cx.gw_t.rearrange("(k p) n -> p k n", p=128)), 16)
            iter_v, iter_s = [], []
            mk_tp = [0, 0]
            mk_pg = 0
            for t in range(NT):
                bi = t % 2
                if t >= 2:
                    SY.wait_ge(vC.h, iter_v[t - 2])
                    SY.wait_ge(sC.h, iter_s[t - 2])
                dL.inc(SY.dma_start(out=xnb[bi][:], in_=cx.xn_tm32[t * 128:(t + 1) * 128, :]), 16)
                dL.inc(SY.dma_start(out=acb[bi][:], in_=cx.acc_tm[t * 128:(t + 1) * 128, :]), 16)
                wL = dL.n
                VE.wait_ge(dL.h, wL)
                vC.inc(VE.tensor_copy(x2[:, 0:MZ], xnb[bi][:, 0:MZ]))
                vC.inc(VE.tensor_tensor(x2[:, MZ:], xnb[bi][:, MZ:], acb[bi][:], ALU.add))
                VE.wait_ge(vC.h, vC.n)
                SC.wait_ge(vC.h, vC.n)
                sC.inc(SC.activation(junk[:], x2[:], AF.Square, accum_out=st[:, 0:1]))
                SC.wait_ge(sC.h, sC.n)
                sC.inc(SC.activation(st[:, 1:2], st[:, 0:1], AF.Ln, scale=1.0 / D, bias=EPS_RMS))
                SC.wait_ge(sC.h, sC.n)
                sC.inc(SC.activation(st[:, 2:3], st[:, 1:2], AF.Exp, scale=-0.5))
                VE.wait_ge(sC.h, sC.n)
                vC.inc(VE.tensor_scalar(xf16[:], x2[:], st[:, 2:3], None, ALU.mult))
                VE.wait_ge(vC.h, vC.n)
                vC.inc(VE.tensor_tensor(xf16[:], xf16[:], cx.P_n2t[:], ALU.mult))
                VE.wait_ge(vC.h, vC.n)
                wXF = vC.n
                TE.wait_ge(vC.h, wXF)
                if t == 0:
                    TE.wait_ge(dW.h, dW.n)
                for k in range(8):
                    ks = k % 2
                    TE.wait_ge(vC.h, mk_tp[ks])
                    tC.inc(TE.matmul(ptp[ks][:, 0:128], xf16[:, k * 128:(k + 1) * 128],
                                     cx.P_idt16[:], is_transpose=True, start=True, stop=True))
                    VE.wait_ge(tC.h, tC.n)
                    vC.inc(VE.tensor_copy(xfd[bi][:, k, :], ptp[ks][:, 0:128]))
                    mk_tp[ks] = vC.n
                wXD = vC.n
                SY.wait_ge(vC.h, wXD)
                dS.inc(SY.dma_start(
                    out=cx.xf_dm.rearrange("(k p) T2 -> p k T2", p=128)[:, :, t * 128:(t + 1) * 128],
                    in_=xfd[bi][:]), 16)
                TE.wait_ge(vC.h, wXD)
                TE.wait_ge(vC.h, mk_pg)
                for k in range(8):
                    tC.inc(TE.matmul(pg[:, 0:4], xfd[bi][:, k, :], gw_sb[:, k, :],
                                     start=(k == 0), stop=(k == 7)))
                wG = tC.n
                VE.wait_ge(tC.h, wG)
                vC.inc(VE.reduce_max(st[:, 3:4], pg[:, 0:4], axis=AX.X, negate=True))
                VE.wait_ge(vC.h, vC.n)
                SC.wait_ge(vC.h, vC.n)
                sC.inc(SC.activation(gate[:, 0:4], pg[:, 0:4], AF.Exp,
                                     bias=st[:, 3:4], accum_out=gate[:, 4:5]))
                mk_pg = sC.n
                VE.wait_ge(sC.h, sC.n)
                vC.inc(VE.reciprocal(gate[:, 5:6], gate[:, 4:5]))
                VE.wait_ge(vC.h, vC.n)
                vC.inc(VE.tensor_scalar(gate[:, 0:4], gate[:, 0:4], gate[:, 5:6], None, ALU.mult))
                VE.wait_ge(vC.h, vC.n)
                vC.inc(VE.tensor_tensor(gm[:, 0:16].rearrange("p (e j) -> p e j", j=4),
                                        gate[:, 0:4].unsqueeze(1).broadcast_to([128, 4, 4]),
                                        gate[:, 0:4].unsqueeze(2).broadcast_to([128, 4, 4]),
                                        ALU.is_gt))
                vC.inc(VE.tensor_tensor(gm[:, 16:32].rearrange("p (e j) -> p e j", j=4),
                                        gate[:, 0:4].unsqueeze(1).broadcast_to([128, 4, 4]),
                                        gate[:, 0:4].unsqueeze(2).broadcast_to([128, 4, 4]),
                                        ALU.is_equal))
                VE.wait_ge(vC.h, vC.n)
                vC.inc(VE.tensor_tensor(gm[:, 16:32], gm[:, 16:32], cx.P_ltm[:], ALU.mult))
                VE.wait_ge(vC.h, vC.n)
                vC.inc(VE.tensor_tensor(gm[:, 0:16], gm[:, 0:16], gm[:, 16:32], ALU.add))
                VE.wait_ge(vC.h, vC.n)
                vC.inc(VE.tensor_reduce(gate[:, 8:12].unsqueeze(2),
                                        gm[:, 0:16].rearrange("p (e j) -> p e j", j=4),
                                        AX.X, ALU.add))
                VE.wait_ge(vC.h, vC.n)
                vC.inc(VE.tensor_scalar(gate[:, 12:16], gate[:, 8:12], 1.5, None, ALU.is_le))
                VE.wait_ge(vC.h, vC.n)
                vC.inc(VE.tensor_tensor(gate[:, 16:20], gate[:, 0:4], gate[:, 12:16], ALU.mult))
                VE.wait_ge(vC.h, vC.n)
                vC.inc(VE.reduce_sum(gate[:, 20:21], gate[:, 16:20], axis=AX.X))
                VE.wait_ge(vC.h, vC.n)
                vC.inc(VE.tensor_scalar(gate[:, 21:22], gate[:, 20:21], 1e-8, None, ALU.add))
                VE.wait_ge(vC.h, vC.n)
                vC.inc(VE.reciprocal(gate[:, 22:23], gate[:, 21:22]))
                VE.wait_ge(vC.h, vC.n)
                vC.inc(VE.tensor_scalar(wt4[bi][:], gate[:, 16:20], gate[:, 22:23], None, ALU.mult))
                SY.wait_ge(vC.h, vC.n)
                dS.inc(SY.dma_start(out=cx.wts_tm[t * 128:(t + 1) * 128, :], in_=wt4[bi][:]), 16)
                iter_v.append(vC.n); iter_s.append(sC.n)
            SY.wait_ge(dS.h, dS.n)
            ies.close()
        blk.sync(emit)


def _s6(cx, name, kind):
    nc = cx.nc
    SY, TE, VE, SC, GP = cx.SY, cx.TE, cx.VE, cx.SC, cx.GP
    with nc.Block(name) as blk:
        def emit(_):
            ies = ExitStack()
            dW = CT(nc, ies, f"{name}_dW"); dL = CT(nc, ies, f"{name}_dL")
            dS = CT(nc, ies, f"{name}_dS")
            tC = CT(nc, ies, f"{name}_tC"); vC = CT(nc, ies, f"{name}_vC")
            sC = CT(nc, ies, f"{name}_sC")
            sbt = lambda n_, s, d: ies.enter_context(nc.sbuf_tensor(name + n_, list(s), d))
            pst = lambda n_, s: ies.enter_context(nc.psum_tensor(name + n_, list(s), F32))
            eidx = {"e0": 0, "e1": 1, "e2b": 2, "e3": 3}.get(kind)
            rr = lambda ap_: ap_.rearrange("(k p) n -> p k n", p=128)
            if kind == "e0":
                wup = sbt("w_up", [128, 8, HID], BF16)
                wdn = sbt("w_dn", [128, 16, D - MZ], BF16)
                dW.inc(GP.dma_start(out=wup[:], in_=rr(cx.e0_up_t)), 16)
                dW.inc(GP.dma_start(out=wdn[:], in_=rr(cx.e0_dn_t[:, MZ:])), 16)
            elif kind == "e1":
                wup = sbt("w_u1", [128, 8, HID], BF16)
                wu2 = sbt("w_u2", [128, 8, HID], BF16)
                wdn = sbt("w_dn", [128, 16, D - MZ], BF16)
                dW.inc(GP.dma_start(out=wup[:], in_=rr(cx.e1_w1_t)), 16)
                dW.inc(GP.dma_start(out=wu2[:], in_=rr(cx.e1_w2_t)), 16)
                dW.inc(GP.dma_start(out=wdn[:], in_=rr(cx.e1_dn_t[:, MZ:])), 16)
            elif kind == "e3":
                wd0 = sbt("w_d0", [128, 8, D], BF16)
                wup = sbt("w_up", [128, 8, HID], BF16)
                wdn = sbt("w_dn", [128, 16, D - MZ], BF16)
                dW.inc(GP.dma_start(out=wd0[:], in_=rr(cx.e3_dn_t)), 16)
                dW.inc(GP.dma_start(out=wup[:], in_=rr(cx.e3_up_t)), 16)
                dW.inc(GP.dma_start(out=wdn[:], in_=rr(cx.e3_out_t[:, MZ:])), 16)
            elif kind == "e2a":
                wl1 = sbt("w_l1", [128, 8, HID], BF16)
                wl2 = sbt("w_l2", [128, 16, HID], BF16)
                dW.inc(GP.dma_start(out=wl1[:], in_=rr(cx.e2_l1_t)), 16)
                dW.inc(GP.dma_start(out=wl2[:], in_=rr(cx.e2_l2_t)), 16)
            else:
                wl3 = sbt("w_l3", [128, 16, HID], BF16)
                wdn = sbt("w_dn", [128, 16, D - MZ], BF16)
                dW.inc(GP.dma_start(out=wl3[:], in_=rr(cx.e2_l3_t)), 16)
                dW.inc(GP.dma_start(out=wdn[:], in_=rr(cx.e2_l4_t[:, MZ:])), 16)
            xfb = [sbt(f"xf{i}", [128, 8, 512], BF16) for i in range(2)]
            h2in = [sbt(f"h2i{i}", [128, 16, 512], BF16) for i in range(2)] if kind == "e2b" else None
            h1 = sbt("h1", [128, 16, 512], BF16)
            h1b = sbt("h1b", [128, 16, 512], BF16) if kind in ("e1", "e2a") else None
            h0 = sbt("h0", [128, 8, 512], BF16) if kind == "e3" else None
            wtb = [sbt(f"wt{i}", [128, 4, 4], F32) for i in range(2)]
            acb = sbt("ac", [128, D - MZ], F32)
            pu = [pst(f"pu{i}", [128, 512]) for i in range(2)]
            pd = [pst(f"pd{i}", [128, 512]) for i in range(2)]
            NGR = T // 512
            iter_end = []
            for gidx in range(NGR):
                bi = gidx % 2
                r0 = gidx * 512
                if gidx >= 2:
                    SY.wait_ge(tC.h, iter_end[gidx - 2][0])
                    SY.wait_ge(vC.h, iter_end[gidx - 2][1])
                if kind == "e2b":
                    dL.inc(SY.dma_start(
                        out=h2in[bi][:],
                        in_=cx.h2_dm.rearrange("(k p) T2 -> p k T2", p=128)[:, :, r0:r0 + 512]), 16)
                else:
                    dL.inc(SY.dma_start(
                        out=xfb[bi][:],
                        in_=cx.xf_dm.rearrange("(k p) T2 -> p k T2", p=128)[:, :, r0:r0 + 512]), 16)
                if kind != "e2a":
                    dL.inc(SY.dma_start(
                        out=wtb[bi][:],
                        in_=cx.wts_tm[r0:r0 + 512, :].rearrange("(s p) e -> p s e", p=128)), 16)
                wL = dL.n
                TE.wait_ge(dL.h, wL)
                if gidx == 0:
                    TE.wait_ge(dW.h, dW.n)
                def up_chain(wsb, nkb, src_idx, dst_sb, act):
                    for m in range(16):
                        ms = m % 2
                        TE.wait_ge(sC.h, sC.n); TE.wait_ge(vC.h, vC.n)
                        for k in range(nkb):
                            tC.inc(TE.matmul(pu[ms][:], wsb[:, k, m * 128:(m + 1) * 128],
                                             src_idx(k), start=(k == 0), stop=(k == nkb - 1)))
                        if act is not None:
                            SC.wait_ge(tC.h, tC.n)
                            sC.inc(SC.activation(dst_sb[:, m, :], pu[ms][:], act))
                        else:
                            VE.wait_ge(tC.h, tC.n)
                            vC.inc(VE.tensor_copy(dst_sb[:, m, :], pu[ms][:]))
                if kind == "e0":
                    up_chain(wup, 8, lambda k: xfb[bi][:, k, :], h1, AF.Gelu)
                elif kind == "e1":
                    up_chain(wup, 8, lambda k: xfb[bi][:, k, :], h1, AF.Silu)
                    up_chain(wu2, 8, lambda k: xfb[bi][:, k, :], h1b, None)
                    VE.wait_ge(sC.h, sC.n); VE.wait_ge(vC.h, vC.n)
                    vC.inc(VE.tensor_tensor(h1[:], h1[:], h1b[:], ALU.mult))
                    VE.wait_ge(vC.h, vC.n)
                elif kind == "e3":
                    for m in range(8):
                        ms = m % 2
                        TE.wait_ge(sC.h, sC.n)
                        for k in range(8):
                            tC.inc(TE.matmul(pu[ms][:], wd0[:, k, m * 128:(m + 1) * 128],
                                             xfb[bi][:, k, :], start=(k == 0), stop=(k == 7)))
                        SC.wait_ge(tC.h, tC.n)
                        sC.inc(SC.activation(h0[:, m, :], pu[ms][:], AF.Gelu))
                    TE.wait_ge(sC.h, sC.n)
                    up_chain(wup, 8, lambda k: h0[:, k, :], h1, AF.Gelu)
                elif kind == "e2a":
                    up_chain(wl1, 8, lambda k: xfb[bi][:, k, :], h1b, AF.Gelu)
                    TE.wait_ge(sC.h, sC.n)
                    up_chain(wl2, 16, lambda k: h1b[:, k, :], h1, AF.Gelu)
                    SY.wait_ge(sC.h, sC.n)
                    dS.inc(SY.dma_start(
                        out=cx.h2_dm.rearrange("(k p) T2 -> p k T2", p=128)[:, :, r0:r0 + 512],
                        in_=h1[:]), 16)
                    iter_end.append((tC.n, vC.n))
                    continue
                else:
                    up_chain(wl3, 16, lambda k: h2in[bi][:, k, :], h1, AF.Gelu)
                TE.wait_ge(sC.h, sC.n); TE.wait_ge(vC.h, vC.n)
                for msub in range(4):
                    SY.wait_ge(dS.h, dS.n)
                    dL.inc(SY.dma_start(out=acb[:],
                                        in_=cx.acc_tm[r0 + msub * 128:r0 + (msub + 1) * 128, :]), 16)
                    wA = dL.n
                    if msub >= 1:
                        TE.wait_ge(vC.h, vC.n)
                    for n in range(2):
                        w0, w1 = n * 512, min((n + 1) * 512, D - MZ)
                        for k in range(16):
                            tC.inc(TE.matmul(pd[n][:, 0:w1 - w0],
                                             h1[:, k, msub * 128:(msub + 1) * 128],
                                             wdn[:, k, w0:w1], start=(k == 0), stop=(k == 15)))
                    wD = tC.n
                    VE.wait_ge(tC.h, wD); VE.wait_ge(dL.h, wA)
                    vC.inc(VE.scalar_tensor_tensor(acb[:, 0:512], pd[0][:],
                                                   wtb[bi][:, msub, eidx:eidx + 1],
                                                   acb[:, 0:512], ALU.mult, ALU.add))
                    vC.inc(VE.scalar_tensor_tensor(acb[:, 512:768], pd[1][:, 0:256],
                                                   wtb[bi][:, msub, eidx:eidx + 1],
                                                   acb[:, 512:768], ALU.mult, ALU.add))
                    SY.wait_ge(vC.h, vC.n)
                    dS.inc(SY.dma_start(out=cx.acc_tm[r0 + msub * 128:r0 + (msub + 1) * 128, :],
                                        in_=acb[:]), 16)
                iter_end.append((tC.n, vC.n))
            SY.wait_ge(dS.h, dS.n)
            ies.close()
        blk.sync(emit)


def _s7(cx):
    nc = cx.nc
    SY, VE = cx.SY, cx.VE
    with nc.Block("s7") as blk:
        def emit(_):
            ies = ExitStack()
            dL = CT(nc, ies, "s7_dL"); dS = CT(nc, ies, "s7_dS"); vC = CT(nc, ies, "s7_vC")
            xb = [ies.enter_context(nc.sbuf_tensor(f"s7_x{i}", [128, D], F32)) for i in range(2)]
            mb = [ies.enter_context(nc.sbuf_tensor(f"s7_m{i}", [128, D], F32)) for i in range(2)]
            nb = [ies.enter_context(nc.sbuf_tensor(f"s7_n{i}", [128, D], F32)) for i in range(2)]
            ab = [ies.enter_context(nc.sbuf_tensor(f"s7_a{i}", [128, D - MZ], F32)) for i in range(2)]
            ob = [ies.enter_context(nc.sbuf_tensor(f"s7_o{i}", [128, D], F32)) for i in range(2)]
            iter_v = []
            for t in range(NT):
                bi = t % 2
                if t >= 2:
                    SY.wait_ge(vC.h, iter_v[t - 2])
                dL.inc(SY.dma_start(out=xb[bi][:], in_=cx.x[t * 128:(t + 1) * 128, :]), 16)
                dL.inc(SY.dma_start(out=mb[bi][:], in_=cx.mamba_tm[t * 128:(t + 1) * 128, :]), 16)
                dL.inc(SY.dma_start(out=nb[bi][:], in_=cx.xn_tm32[t * 128:(t + 1) * 128, :]), 16)
                dL.inc(SY.dma_start(out=ab[bi][:], in_=cx.acc_tm[t * 128:(t + 1) * 128, :]), 16)
                wL = dL.n
                VE.wait_ge(dL.h, wL)
                vC.inc(VE.tensor_tensor(ob[bi][:], xb[bi][:], mb[bi][:], ALU.add))
                VE.wait_ge(vC.h, vC.n)
                vC.inc(VE.tensor_tensor(ob[bi][:], ob[bi][:], nb[bi][:], ALU.add))
                VE.wait_ge(vC.h, vC.n)
                vC.inc(VE.tensor_tensor(ob[bi][:, MZ:], ob[bi][:, MZ:], ab[bi][:], ALU.add))
                SY.wait_ge(vC.h, vC.n)
                dS.inc(SY.dma_start(out=cx.out[t * 128:(t + 1) * 128, :], in_=ob[bi][:]), 16)
                iter_v.append(vC.n)
            SY.wait_ge(dS.h, dS.n)
            ies.close()
        blk.sync(emit)


def build(dbg=False, stages=("s0", "s1", "s2", "s3", "s4", "s5", "s6", "s7")):
    nc = bass.Bass("TRN2", target_bir_lowering=False)
    cx = _declare(nc, dbg)
    _init_stage(cx)
    if "s0" in stages: _s0(cx)
    if "s1" in stages: _s1(cx)
    if "s2" in stages: _s2(cx)
    if "s3" in stages: _s3(cx)
    if "s4" in stages: _s4(cx)
    if "s4" in stages: _s4b(cx)
    if "s5" in stages: _s5(cx)
    if "s6" in stages:
        _s6(cx, "s6e1", "e1")
        _s6(cx, "s6e0", "e0")
        _s6(cx, "s6e3", "e3")
        _s6(cx, "s6e2a", "e2a")
        _s6(cx, "s6e2b", "e2b")
    if "s7" in stages: _s7(cx)
    cx.es.close()
    return nc


_CACHE = {}

def kernel(**inputs):
    consts = _host_consts()
    if "nc" not in _CACHE:
        _CACHE["nc"] = build(dbg=False)
    nc = _CACHE["nc"]
    in_maps = []
    for core in range(NCORES):
        m = dict(_host_layout(inputs, core))
        m.update(consts)
        in_maps.append(m)
    res = run_bass_kernel_spmd(nc, in_maps, core_ids=list(range(NCORES)))
    outs = [np.asarray(r["out"], np.float32).reshape(BPC, L, D) for r in res.results]
    return np.concatenate(outs, axis=0)


# version 05
# speedup vs baseline: 1.1652x; 1.1391x over previous
import numpy as np
import ml_dtypes
import concourse.bass as bass
import concourse.mybir as mybir
from concourse.bass_utils import run_bass_kernel_spmd
from contextlib import ExitStack

AF = mybir.ActivationFunctionType
ALU = mybir.AluOpType
AX = mybir.AxisListType
F32, BF16, I32 = mybir.dt.float32, mybir.dt.bfloat16, mybir.dt.int32

NCORES = 8
BPC = 4
L = 2048
T = BPC * L
D = 1024
DI = 2048
NH = 32
Q = 128
NCH = L // Q
GRP = 4
NG = NCH // GRP
HEADS = 16
HD = 64
MZ = 256
HID = 2048
NT = T // 128
EPS_RMS = 1.1920929e-07
EPS_LN = 1e-5
TWO_PI = float(2.0 * np.pi)
BIG = 1e30


class CT:
    def __init__(self, nc, es, name):
        self.h = es.enter_context(nc.semaphore(name))
        self.n = 0
    def inc(self, inst, k=1):
        inst.then_inc(self.h, k)
        self.n += k
        return self.n


def _host_consts():
    c = {}
    le = np.where(np.arange(128)[:, None] <= np.arange(128)[None, :], 1.0, 0.0)
    c["c_cum"] = le.astype(np.float32)
    c["c_wmask"] = le.astype(ml_dtypes.bfloat16)
    c["c_idt16"] = np.eye(128, dtype=np.float32).astype(ml_dtypes.bfloat16)
    c["c_idt32"] = np.eye(128, dtype=np.float32)
    oh = np.zeros((32, 32 * 128), np.float32)
    for h in range(32):
        oh[h, h * 128:(h + 1) * 128] = 1.0
    c["c_onehot"] = oh
    inv2pi = (1.0 / 10000.0 ** (np.arange(16, dtype=np.float64) / 16.0)) / (2 * np.pi)
    c["c_if2pi"] = np.broadcast_to(inv2pi.astype(np.float32), (128, 32, 16)).reshape(128, 512).copy()
    am = np.zeros((4, 128, 512), np.float32)
    for i, dd in enumerate((0, 128, 256, 384)):
        am[i] = np.where(np.arange(128)[:, None] + dd <= np.arange(512)[None, :], 0.0, -30000.0)
    c["c_amask"] = np.ascontiguousarray(am.reshape(4 * 128, 512))
    ltm = np.zeros((4, 4), np.float32)
    for e in range(4):
        ltm[e, :e] = 1.0
    c["c_ltm"] = np.broadcast_to(ltm.reshape(1, 16), (128, 16)).copy()
    c["c_ones_f"] = np.ones((1, 128), np.float32)
    c["c_onescol"] = np.ones((128, 1), np.float32)
    return c


def _host_layout(inputs, core):
    g = lambda k: np.asarray(inputs[k])
    tr = lambda a: np.ascontiguousarray(np.asarray(a).T)
    d = {}
    d["x"] = np.ascontiguousarray(g("L0_attn_x")[core * BPC:(core + 1) * BPC].reshape(T, D))
    W_in = g("L0_attn_W_in")
    d["w_z"] = np.ascontiguousarray(W_in[:, :DI])
    d["w_x"] = np.ascontiguousarray(W_in[:, DI:2 * DI])
    d["w_bcd"] = np.ascontiguousarray(W_in[:, 2 * DI:])
    d["w_out"] = np.ascontiguousarray(g("L0_attn_W_out"))
    d["wq_t"] = tr(g("L0_ffn_wq")); d["wk_t"] = tr(g("L0_ffn_wk"))
    d["wv_t"] = tr(g("L0_ffn_wv")); d["wo_t"] = tr(g("L0_ffn_wo"))
    d["gw_t"] = tr(g("L0_ffn_gate_w"))
    d["e0_up_t"] = tr(g("L0_ffn_e0_up")); d["e0_dn_t"] = tr(g("L0_ffn_e0_down"))
    d["e1_w1_t"] = tr(g("L0_ffn_e1_w1")); d["e1_w2_t"] = tr(g("L0_ffn_e1_w2"))
    d["e1_dn_t"] = tr(g("L0_ffn_e1_down"))
    d["e2_l1_t"] = tr(g("L0_ffn_e2_l1")); d["e2_l2_t"] = tr(g("L0_ffn_e2_l2"))
    d["e2_l3_t"] = tr(g("L0_ffn_e2_l3")); d["e2_l4_t"] = tr(g("L0_ffn_e2_l4"))
    d["e3_dn_t"] = tr(g("L0_ffn_e3_down")); d["e3_up_t"] = tr(g("L0_ffn_e3_up"))
    d["e3_out_t"] = tr(g("L0_ffn_e3_out"))
    d["a_log_r"] = np.ascontiguousarray(g("L0_attn_A_log").reshape(1, NH))
    d["qn_r"] = np.ascontiguousarray(g("L0_ffn_qn_w").reshape(1, HD))
    d["kn_r"] = np.ascontiguousarray(g("L0_ffn_kn_w").reshape(1, HD))
    d["dtb_bc"] = np.ascontiguousarray(np.broadcast_to(g("L0_attn_dt_bias"), (128, NH)))
    d["dsk_bc"] = np.ascontiguousarray(np.broadcast_to(g("L0_attn_D_skip"), (128, NH)))
    d["n1t"] = np.ascontiguousarray(np.broadcast_to(g("L0_ffn_n1_w"), (128, D)))
    d["n2t"] = np.ascontiguousarray(np.broadcast_to(g("L0_ffn_n2_w"), (128, D)))
    d["qnt"] = np.ascontiguousarray(np.broadcast_to(np.tile(g("L0_ffn_qn_w"), HEADS), (128, D)))
    d["knt"] = np.ascontiguousarray(np.broadcast_to(np.tile(g("L0_ffn_kn_w"), HEADS), (128, D)))
    d["nw_c"] = np.ascontiguousarray(g("L0_attn_norm_weight").reshape(16, 128).T)
    return d


class Ctx:
    pass


def _declare(nc, dbg):
    cx = Ctx()
    cx.nc = nc
    cx.es = ExitStack()
    cx.SY, cx.TE, cx.VE, cx.SC, cx.GP = nc.sync, nc.tensor, nc.vector, nc.scalar, nc.gpsimd
    def din(name, shape, dt=F32):
        ap = nc.dram_tensor(name, list(shape), dt, kind="ExternalInput").ap()
        setattr(cx, name, ap)
    din("x", [T, D])
    din("w_z", [D, DI]); din("w_x", [D, DI]); din("w_bcd", [D, 160])
    din("w_out", [DI, D])
    din("wq_t", [D, D]); din("wk_t", [D, D]); din("wv_t", [D, D])
    din("wo_t", [D, D]); din("gw_t", [D, 4])
    din("e0_up_t", [D, HID]); din("e0_dn_t", [HID, D])
    din("e1_w1_t", [D, HID]); din("e1_w2_t", [D, HID]); din("e1_dn_t", [HID, D])
    din("e2_l1_t", [D, HID]); din("e2_l2_t", [HID, HID])
    din("e2_l3_t", [HID, HID]); din("e2_l4_t", [HID, D])
    din("e3_dn_t", [D, D]); din("e3_up_t", [D, HID]); din("e3_out_t", [HID, D])
    din("a_log_r", [1, NH]); din("qn_r", [1, HD]); din("kn_r", [1, HD])
    din("dtb_bc", [128, NH]); din("dsk_bc", [128, NH])
    din("n1t", [128, D]); din("n2t", [128, D]); din("qnt", [128, D]); din("knt", [128, D])
    din("nw_c", [128, 16])
    din("c_cum", [128, 128]); din("c_wmask", [128, 128], BF16)
    din("c_idt16", [128, 128], BF16); din("c_idt32", [128, 128])
    din("c_onehot", [32, 32 * 128]); din("c_if2pi", [128, 512])
    din("c_amask", [4 * 128, 512]); din("c_ltm", [128, 16])
    din("c_ones_f", [1, 128]); din("c_onescol", [128, 1])
    skind = "ExternalOutput" if dbg else "Internal"
    def dscr(name, shape, dt):
        ap = nc.dram_tensor(name, list(shape), dt, kind=skind).ap()
        setattr(cx, name, ap)
    dscr("xn_tm", [T, D], BF16)
    dscr("xn_tm32", [T, D], F32)
    dscr("xn_dm", [D, T], BF16)
    dscr("sz_tm", [T, DI], BF16)
    dscr("xs_tm", [T, DI], BF16)
    dscr("bc_tm", [T, 128], BF16)
    dscr("dtr_tm", [T, NH], F32)
    dscr("mamba_tm", [T, D], F32)
    dscr("q_dm", [D, T], BF16)
    dscr("k_dm", [D, T], BF16)
    dscr("v_tm", [T, D], BF16)
    dscr("ao_dm", [D, T], BF16)
    dscr("acc_tm", [T, D - MZ], F32)
    dscr("xf_dm", [D, T], BF16)
    dscr("wts_tm", [T, 4], F32)
    dscr("h2_dm", [HID, T], BF16)
    cx.out = nc.dram_tensor("out", [T, D], F32, kind="ExternalOutput").ap()
    def psb(name, shape, dt):
        setattr(cx, name, cx.es.enter_context(nc.sbuf_tensor(name, list(shape), dt)))
    psb("P_idt16", [128, 128], BF16)
    psb("P_idt32", [128, 128], F32)
    psb("P_cum", [128, 128], F32)
    psb("P_wmask", [128, 128], BF16)
    psb("P_onehot", [32, 32 * 128], F32)
    psb("P_if2pi", [128, 512], F32)
    psb("P_amask", [128, 4 * 512], F32)
    psb("P_ltm", [128, 16], F32)
    psb("P_ones_f", [1, 128], F32)
    psb("P_onescol", [128, 1], F32)
    psb("P_dtb", [128, NH], F32)
    psb("P_dsk", [128, NH], F32)
    psb("P_n1t", [128, D], BF16)
    psb("P_n2t", [128, D], BF16)
    psb("P_qnt", [128, D], BF16)
    psb("P_knt", [128, D], BF16)
    psb("P_nwc", [128, 16], F32)
    psb("P_A", [128, NH], F32)
    psb("P_negA", [128, NH], F32)
    psb("P_negc", [128, 1], F32)
    psb("P_tiny", [1, 512], F32)
    psb("P_rowA", [1, 128], F32)
    psb("P_rowNegA", [1, 128], F32)
    psb("P_rowNegc", [1, 128], F32)
    psb("P_const", [128, 2], F32)
    nc.const_aps.register(F32, EPS_LN, cx.P_const[:, 0:1])
    nc.const_aps.register(F32, EPS_RMS, cx.P_const[:, 1:2])
    return cx


def _init_stage(cx):
    nc = cx.nc
    SY, TE, VE, SC, GP = cx.SY, cx.TE, cx.VE, cx.SC, cx.GP
    with nc.Block("init") as blk:
        def emit(_):
            ies = ExitStack()
            dL = CT(nc, ies, "i_dL"); vC = CT(nc, ies, "i_vC")
            sC = CT(nc, ies, "i_sC"); tC = CT(nc, ies, "i_tC")
            pb = ies.enter_context(nc.psum_tensor("i_pb", [128, 512], F32))
            for dst, src in [(cx.P_cum, cx.c_cum), (cx.P_wmask, cx.c_wmask),
                             (cx.P_idt16, cx.c_idt16), (cx.P_idt32, cx.c_idt32),
                             (cx.P_onehot, cx.c_onehot), (cx.P_if2pi, cx.c_if2pi),
                             (cx.P_ltm, cx.c_ltm), (cx.P_ones_f, cx.c_ones_f),
                             (cx.P_onescol, cx.c_onescol), (cx.P_dtb, cx.dtb_bc),
                             (cx.P_dsk, cx.dsk_bc), (cx.P_nwc, cx.nw_c)]:
                dL.inc(SY.dma_start(out=dst[:], in_=src[:]), 16)
            dL.inc(SY.dma_start(out=cx.P_amask[:].rearrange("p (d n) -> p d n", n=512),
                                in_=cx.c_amask.rearrange("(d p) n -> p d n", p=128)), 16)
            for dst, src in [(cx.P_n1t, cx.n1t), (cx.P_n2t, cx.n2t),
                             (cx.P_qnt, cx.qnt), (cx.P_knt, cx.knt)]:
                dL.inc(GP.dma_start(out=dst[:], in_=src[:]), 16)
            vC.inc(VE.memset(cx.P_const[:, 0:1], EPS_LN))
            vC.inc(VE.memset(cx.P_const[:, 1:2], EPS_RMS))
            dL.inc(SY.dma_start(out=cx.P_tiny[0:1, 0:NH], in_=cx.a_log_r[:]), 16)
            dL.inc(SY.dma_start(out=cx.P_tiny[0:1, 32:96], in_=cx.qn_r[:]), 16)
            dL.inc(SY.dma_start(out=cx.P_tiny[0:1, 96:160], in_=cx.kn_r[:]), 16)
            VE.wait_ge(dL.h, dL.n); SC.wait_ge(dL.h, dL.n)
            sC.inc(SC.activation(cx.P_rowA[0:1, 0:NH], cx.P_tiny[0:1, 0:NH], AF.Exp))
            vC.inc(VE.reduce_max(cx.P_tiny[0:1, 160:161], cx.P_tiny[0:1, 32:96], axis=AX.X,
                                 apply_absolute_value=True))
            vC.inc(VE.reduce_max(cx.P_tiny[0:1, 161:162], cx.P_tiny[0:1, 96:160], axis=AX.X,
                                 apply_absolute_value=True))
            VE.wait_ge(vC.h, vC.n)
            vC.inc(VE.tensor_tensor(cx.P_tiny[0:1, 162:163], cx.P_tiny[0:1, 160:161],
                                    cx.P_tiny[0:1, 161:162], ALU.mult))
            VE.wait_ge(vC.h, vC.n)
            vC.inc(VE.tensor_scalar(cx.P_rowNegc[0:1, 0:1], cx.P_tiny[0:1, 162:163],
                                    -8.0, None, ALU.mult))
            SC.wait_ge(sC.h, sC.n)
            sC.inc(SC.mul(cx.P_rowNegA[0:1, 0:NH], cx.P_rowA[0:1, 0:NH], -1.0))
            TE.wait_ge(sC.h, sC.n); TE.wait_ge(vC.h, vC.n)
            tC.inc(TE.matmul(pb[:, 0:NH], cx.P_ones_f[:], cx.P_rowA[0:1, 0:NH],
                             start=True, stop=True))
            VE.wait_ge(tC.h, tC.n)
            vC.inc(VE.tensor_copy(cx.P_A[:], pb[:, 0:NH]))
            VE.wait_ge(vC.h, vC.n)
            TE.wait_ge(vC.h, vC.n)
            tC.inc(TE.matmul(pb[:, 0:NH], cx.P_ones_f[:], cx.P_rowNegA[0:1, 0:NH],
                             start=True, stop=True))
            VE.wait_ge(tC.h, tC.n)
            vC.inc(VE.tensor_copy(cx.P_negA[:], pb[:, 0:NH]))
            VE.wait_ge(vC.h, vC.n)
            TE.wait_ge(vC.h, vC.n)
            tC.inc(TE.matmul(pb[:, 0:1], cx.P_ones_f[:], cx.P_rowNegc[0:1, 0:1],
                             start=True, stop=True))
            VE.wait_ge(tC.h, tC.n)
            vC.inc(VE.tensor_copy(cx.P_negc[:], pb[:, 0:1]))
            SY.wait_ge(vC.h, vC.n)
            ies.close()
        blk.sync(emit)


def _s0(cx):
    nc = cx.nc
    SY, TE, VE, SC = cx.SY, cx.TE, cx.VE, cx.SC
    with nc.Block("s0") as blk:
        def emit(_):
            ies = ExitStack()
            dL = CT(nc, ies, "s0_dL"); dS = CT(nc, ies, "s0_dS")
            tC = CT(nc, ies, "s0_tC"); vC = CT(nc, ies, "s0_vC"); sC = CT(nc, ies, "s0_sC")
            xbuf = [ies.enter_context(nc.sbuf_tensor(f"s0_x{i}", [128, D], F32)) for i in range(2)]
            junk = ies.enter_context(nc.sbuf_tensor("s0_junk", [128, D], F32))
            st = ies.enter_context(nc.sbuf_tensor("s0_st", [128, 8], F32))
            xn16 = [ies.enter_context(nc.sbuf_tensor(f"s0_a{i}", [128, D], BF16)) for i in range(2)]
            xn32 = [ies.enter_context(nc.sbuf_tensor(f"s0_b{i}", [128, D], F32)) for i in range(2)]
            dmb = [ies.enter_context(nc.sbuf_tensor(f"s0_d{i}", [128, 8, 128], BF16)) for i in range(2)]
            ptp = [ies.enter_context(nc.psum_tensor(f"s0_p{i}", [128, 1024], BF16)) for i in range(2)]
            iter_v, iter_s = [], []
            tp_evac = [0, 0]
            for t in range(NT):
                bi = t % 2
                if t >= 2:
                    SY.wait_ge(vC.h, iter_v[t - 2])
                    SY.wait_ge(sC.h, iter_s[t - 2])
                dL.inc(SY.dma_start(out=xbuf[bi][:], in_=cx.x[t * 128:(t + 1) * 128, :]), 16)
                wL = dL.n
                VE.wait_ge(dL.h, wL); SC.wait_ge(dL.h, wL)
                vC.inc(VE.reduce_sum(st[:, 0:1], xbuf[bi][:], axis=AX.X))
                sC.inc(SC.activation(junk[:], xbuf[bi][:], AF.Square, accum_out=st[:, 1:2]))
                VE.wait_ge(sC.h, sC.n); VE.wait_ge(vC.h, vC.n)
                vC.inc(VE.tensor_scalar(st[:, 2:3], st[:, 0:1], 1.0 / D, None, ALU.mult))
                VE.wait_ge(vC.h, vC.n)
                vC.inc(VE.tensor_tensor(st[:, 3:4], st[:, 2:3], st[:, 2:3], ALU.mult))
                VE.wait_ge(vC.h, vC.n)
                vC.inc(VE.tensor_scalar(st[:, 4:5], st[:, 1:2], 1.0 / D, st[:, 3:4],
                                        ALU.mult, ALU.subtract))
                SC.wait_ge(vC.h, vC.n)
                sC.inc(SC.activation(st[:, 5:6], st[:, 4:5], AF.Ln, bias=EPS_LN))
                SC.wait_ge(sC.h, sC.n)
                sC.inc(SC.activation(st[:, 6:7], st[:, 5:6], AF.Exp, scale=-0.5))
                VE.wait_ge(sC.h, sC.n)
                vC.inc(VE.scalar_tensor_tensor(st[:, 7:8], st[:, 2:3], -1.0,
                                               st[:, 6:7], ALU.mult, ALU.mult))
                VE.wait_ge(vC.h, vC.n); SC.wait_ge(vC.h, vC.n)
                sC.inc(SC.activation(xn16[bi][:], xbuf[bi][:], AF.Identity,
                                     bias=st[:, 7:8], scale=st[:, 6:7]))
                vC.inc(VE.scalar_tensor_tensor(xn32[bi][:], xbuf[bi][:], st[:, 6:7],
                                               st[:, 7:8].broadcast_to([128, D]),
                                               ALU.mult, ALU.add))
                wS, wV = sC.n, vC.n
                TE.wait_ge(sC.h, wS)
                for k in range(8):
                    ks = k % 2
                    TE.wait_ge(vC.h, tp_evac[ks])
                    tC.inc(TE.matmul(ptp[ks][:, 0:128], xn16[bi][:, k * 128:(k + 1) * 128],
                                     cx.P_idt16[:], is_transpose=True, start=True, stop=True))
                    VE.wait_ge(tC.h, tC.n)
                    vC.inc(VE.tensor_copy(dmb[bi][:, k, :], ptp[ks][:, 0:128]))
                    tp_evac[ks] = vC.n
                SY.wait_ge(sC.h, wS)
                dS.inc(SY.dma_start(out=cx.xn_tm[t * 128:(t + 1) * 128, :], in_=xn16[bi][:]), 16)
                SY.wait_ge(vC.h, wV)
                dS.inc(SY.dma_start(out=cx.xn_tm32[t * 128:(t + 1) * 128, :], in_=xn32[bi][:]), 16)
                SY.wait_ge(vC.h, vC.n)
                dS.inc(SY.dma_start(
                    out=cx.xn_dm.rearrange("(k p) T2 -> p k T2", p=128)[:, :, t * 128:(t + 1) * 128],
                    in_=dmb[bi][:]), 16)
                iter_v.append(vC.n); iter_s.append(sC.n)
            SY.wait_ge(dS.h, dS.n)
            ies.close()
        blk.sync(emit)


def _s1(cx):
    nc = cx.nc
    SY, TE, VE, SC, GP = cx.SY, cx.TE, cx.VE, cx.SC, cx.GP
    with nc.Block("s1") as blk:
        def emit(_):
            ies = ExitStack()
            dW = CT(nc, ies, "s1_dW"); dL = CT(nc, ies, "s1_dL"); dS = CT(nc, ies, "s1_dS")
            tC = CT(nc, ies, "s1_tC"); vC = CT(nc, ies, "s1_vC"); sC = CT(nc, ies, "s1_sC")
            wz_sb = ies.enter_context(nc.sbuf_tensor("s1_wz", [128, 8, DI], BF16))
            wx_sb = ies.enter_context(nc.sbuf_tensor("s1_wx", [128, 8, DI], BF16))
            wb_sb = ies.enter_context(nc.sbuf_tensor("s1_wb", [128, 8, 160], BF16))
            xb = [ies.enter_context(nc.sbuf_tensor(f"s1_x{i}", [128, 8, 128], BF16)) for i in range(2)]
            zb = [ies.enter_context(nc.sbuf_tensor(f"s1_z{i}", [128, DI], BF16)) for i in range(2)]
            sxb = [ies.enter_context(nc.sbuf_tensor(f"s1_s{i}", [128, DI], BF16)) for i in range(2)]
            bcb = [ies.enter_context(nc.sbuf_tensor(f"s1_b{i}", [128, 128], BF16)) for i in range(2)]
            dtb = [ies.enter_context(nc.sbuf_tensor(f"s1_t{i}", [128, 32], F32)) for i in range(2)]
            pz = [ies.enter_context(nc.psum_tensor(f"s1_p{i}", [128, 512], F32)) for i in range(8)]
            dW.inc(GP.dma_start(out=wz_sb[:], in_=cx.w_z.rearrange("(k p) n -> p k n", p=128)), 16)
            dW.inc(GP.dma_start(out=wx_sb[:], in_=cx.w_x.rearrange("(k p) n -> p k n", p=128)), 16)
            dW.inc(GP.dma_start(out=wb_sb[:], in_=cx.w_bcd.rearrange("(k p) n -> p k n", p=128)), 16)
            iter_v, iter_s = [], []
            for t in range(NT):
                bi = t % 2
                if t >= 2:
                    SY.wait_ge(vC.h, iter_v[t - 2])
                    SY.wait_ge(sC.h, iter_s[t - 2])
                dL.inc(SY.dma_start(
                    out=xb[bi][:],
                    in_=cx.xn_dm.rearrange("(k p) T2 -> p k T2", p=128)[:, :, t * 128:(t + 1) * 128]), 16)
                wL = dL.n
                TE.wait_ge(dL.h, wL)
                if t == 0:
                    TE.wait_ge(dW.h, dW.n)
                if t >= 2:
                    TE.wait_ge(sC.h, iter_s[t - 2])
                    TE.wait_ge(vC.h, iter_v[t - 2])
                for k in range(8):
                    for n in range(8):
                        src = wz_sb[:, k, (n % 4) * 512:(n % 4 + 1) * 512] if n < 4 else \
                              wx_sb[:, k, (n - 4) * 512:(n - 3) * 512]
                        tC.inc(TE.matmul(pz[n][:], xb[bi][:, k, :], src,
                                         start=(k == 0), stop=(k == 7)))
                wT = tC.n
                SC.wait_ge(tC.h, wT); VE.wait_ge(tC.h, wT)
                for n in range(4):
                    sC.inc(SC.activation(zb[bi][:, n * 512:(n + 1) * 512], pz[n][:], AF.Silu))
                for n in range(4):
                    vC.inc(VE.tensor_copy(sxb[bi][:, n * 512:(n + 1) * 512], pz[4 + n][:]))
                wE_s, wE_v = sC.n, vC.n
                TE.wait_ge(sC.h, wE_s)
                for k in range(8):
                    tC.inc(TE.matmul(pz[0][:, 0:160], xb[bi][:, k, :], wb_sb[:, k, :],
                                     start=(k == 0), stop=(k == 7)))
                VE.wait_ge(tC.h, tC.n)
                vC.inc(VE.tensor_copy(bcb[bi][:], pz[0][:, 0:128]))
                vC.inc(VE.tensor_copy(dtb[bi][:], pz[0][:, 128:160]))
                wB = vC.n
                SY.wait_ge(sC.h, wE_s)
                dS.inc(SY.dma_start(out=cx.sz_tm[t * 128:(t + 1) * 128, :], in_=zb[bi][:]), 16)
                SY.wait_ge(vC.h, wE_v)
                dS.inc(SY.dma_start(out=cx.xs_tm[t * 128:(t + 1) * 128, :], in_=sxb[bi][:]), 16)
                SY.wait_ge(vC.h, wB)
                dS.inc(SY.dma_start(out=cx.bc_tm[t * 128:(t + 1) * 128, :], in_=bcb[bi][:]), 16)
                dS.inc(SY.dma_start(out=cx.dtr_tm[t * 128:(t + 1) * 128, :], in_=dtb[bi][:]), 16)
                iter_v.append(vC.n); iter_s.append(sC.n)
            SY.wait_ge(dS.h, dS.n)
            ies.close()
        blk.sync(emit)


S2DBG = False
S3ABL = ("h", "tp", "v", "qk")

def _s2(cx):
    nc = cx.nc
    SY, TE, VE, SC, GP = cx.SY, cx.TE, cx.VE, cx.SC, cx.GP
    dbgt = {}
    if S2DBG:
        dbgt["dbg_y"] = nc.dram_tensor("dbg_y", [128, GRP * DI], BF16,
                                       kind="ExternalOutput").ap()
        for nm, sh_, dt_ in [("dbg_Wt", [128, 128], BF16), ("dbg_E", [128, 128], F32),
                             ("dbg_Gm", [128, 128], BF16), ("dbg_bnm", [64, 128], BF16),
                             ("dbg_cnm2", [64, 128], BF16), ("dbg_rot", [128, GRP * 128], BF16)]:
            dbgt[nm] = nc.dram_tensor(nm, sh_, dt_, kind="ExternalOutput").ap()
    with nc.Block("s2") as blk:
        def emit(_):
            ies = ExitStack()
            dW = CT(nc, ies, "s2_dW"); dL = CT(nc, ies, "s2_dL"); dS = CT(nc, ies, "s2_dS")
            tC = CT(nc, ies, "s2_tC"); vC = CT(nc, ies, "s2_vC"); sC = CT(nc, ies, "s2_sC")
            sbt = lambda n_, s, d: ies.enter_context(nc.sbuf_tensor(n_, list(s), d))
            wout_sb = sbt("s2_wo", [128, 16, D], BF16)
            dtr4 = sbt("s2_dtr", [128, GRP, 32], F32)
            bc4 = sbt("s2_bc", [128, GRP, 128], BF16)
            xs4 = sbt("s2_xs", [128, GRP, DI], BF16)
            sz4 = sbt("s2_sz", [128, GRP, DI], BF16)
            dt4 = sbt("s2_dt", [128, GRP, 32], F32)
            tmpa = sbt("s2_tmpa", [128, GRP, 32], F32)
            cumT = sbt("s2_cumT", [128, GRP * 32], F32)
            gcum = sbt("s2_gcum", [128, GRP * 32], F32)
            biasT = sbt("s2_biasT", [128, GRP * 32], F32)
            lamT = sbt("s2_lamT", [128, GRP * 32], F32)
            gdtT = sbt("s2_gdtT", [128, GRP * 32], F32)
            lndt = sbt("s2_lndt", [128, GRP * 32], F32)
            qa = sbt("s2_qa", [128, 2 * 512], F32)
            qi = sbt("s2_qi", [128, 2 * 512], I32)
            qf = sbt("s2_qf", [128, 2 * 512], F32)
            fr = sbt("s2_fr", [128, 2 * 512], F32)
            sdup = sbt("s2_sd", [128, GRP * 1024], BF16)
            cdup = sbt("s2_cd", [128, GRP * 1024], BF16)
            cnm = sbt("s2_cnm", [32, GRP * 128], F32)
            csrow = sbt("s2_cs", [1, 128], F32)
            offr = sbt("s2_off", [1, 160], F32)
            acdr = sbt("s2_acd", [1, 128], F32)
            decr = sbt("s2_dec", [1, 128], F32)
            dec_bc = sbt("s2_decb", [64, 128], F32)
            acd_bc = sbt("s2_acdb", [128, 128], F32)
            H = [sbt(f"s2_H{i}", [64, DI], BF16) for i in range(2)]
            rot = [sbt(f"s2_rot{i}", [128, GRP, 128], BF16) for i in range(2)]
            rtmp = [sbt(f"s2_rtmp{i}", [128, GRP, 32], F32) for i in range(2)]
            bgdt = [sbt(f"s2_bg{i}", [128, GRP, 64], BF16) for i in range(2)]
            cltm = sbt("s2_cl", [128, GRP, 128], BF16)
            bnm = [sbt(f"s2_bn{i}", [64, 128], BF16) for i in range(4)]
            cnm2 = [sbt(f"s2_cm{i}", [64, 128], BF16) for i in range(4)]
            clnm = [sbt(f"s2_cn{i}", [64, 128], BF16) for i in range(8)]
            Etile = [sbt(f"s2_E{i}", [128, 128], F32) for i in range(4)]
            Gm = [sbt(f"s2_Gm{i}", [128, 128], BF16) for i in range(4)]
            Wt = [sbt(f"s2_W{i}", [128, 128], BF16) for i in range(4)]
            ybuf = sbt("s2_y", [128, GRP, DI], BF16)
            yst = sbt("s2_yst", [128, GRP, 4], F32)
            ydmb = sbt("s2_ydm", [128, 16, 128], BF16)
            mo = [sbt(f"s2_mo{i}", [128, D], F32) for i in range(2)]
            pst = lambda n_, s, d=F32: ies.enter_context(nc.psum_tensor(n_, list(s), d))
            p_tp = [pst(f"s2_ptp{i}", [128, 1024], BF16) for i in range(2)]
            p_R = [pst(f"s2_pr{i}", [128, 512]) for i in range(2)]
            p_G = [pst(f"s2_pg{i}", [128, 512]) for i in range(2)]
            p_Y = pst("s2_py", [128, 512])
            p_Sc = pst("s2_psc", [64, 512])
            p_sm = p_G[0]
            p_W = p_R
            dW.inc(GP.dma_start(out=wout_sb[:], in_=cx.w_out.rearrange("(k p) n -> p k n", p=128)), 16)
            VE.wait_ge(dW.h, dW.n)
            for k in range(16):
                vC.inc(VE.tensor_scalar(wout_sb[:, k, :], wout_sb[:, k, :],
                                        cx.P_nwc[:, k:k + 1], None, ALU.mult))
            wout_ready = vC.n
            mk = {"tp": [0, 0], "R": [0, 0], "G": [[0, 0], [0, 0]], "Y": 0, "Sc": 0,
                  "rot": [0, 0], "nm": [[0, 0], [0, 0]], "W": [[0, 0], [0, 0]],
                  "E": [[0, 0], [0, 0]], "mo": [0, 0], "Wr": [[0, 0], [0, 0]],
                  "W2": [0, 0], "cn": 0, "yev": [0, 0], "Elast": [0, 0], "Glast": [0, 0]}

            def phaseA(b, g, hinit):
                g0 = b * L + g * GRP * Q
                SY.wait_ge(vC.h, vC.n)
                SY.wait_ge(sC.h, sC.n)
                SY.wait_ge(tC.h, tC.n)
                dL.inc(SY.dma_start(out=dtr4[:], in_=cx.dtr_tm[g0:g0 + 512, :]
                                    .rearrange("(c p) h -> p c h", p=128)), 16)
                dL.inc(SY.dma_start(out=bc4[:], in_=cx.bc_tm[g0:g0 + 512, :]
                                    .rearrange("(c p) n -> p c n", p=128)), 16)
                dL.inc(SY.dma_start(out=xs4[:], in_=cx.xs_tm[g0:g0 + 512, :]
                                    .rearrange("(c p) n -> p c n", p=128)), 16)
                dL.inc(SY.dma_start(out=sz4[:], in_=cx.sz_tm[g0:g0 + 512, :]
                                    .rearrange("(c p) n -> p c n", p=128)), 16)
                wL = dL.n
                VE.wait_ge(dL.h, wL); SC.wait_ge(dL.h, wL)
                VE.wait_ge(vC.h, hinit)
                vC.inc(VE.tensor_tensor(dt4[:], dtr4[:],
                                        cx.P_dtb[:].unsqueeze(1).broadcast_to([128, GRP, 32]),
                                        ALU.add))
                VE.wait_ge(vC.h, vC.n); SC.wait_ge(vC.h, vC.n)
                sC.inc(SC.activation(tmpa[:], dt4[:], AF.Abs))
                SC.wait_ge(sC.h, sC.n)
                sC.inc(SC.activation(tmpa[:], tmpa[:], AF.Exp, scale=-1.0))
                SC.wait_ge(sC.h, sC.n)
                sC.inc(SC.activation(tmpa[:], tmpa[:], AF.Ln, bias=1.0))
                VE.wait_ge(sC.h, sC.n)
                vC.inc(VE.tensor_scalar(dt4[:], dt4[:], 0.0, None, ALU.max))
                VE.wait_ge(vC.h, vC.n)
                vC.inc(VE.tensor_tensor(dt4[:], dt4[:], tmpa[:], ALU.add))
                VE.wait_ge(vC.h, vC.n)
                wDT = vC.n
                SC.wait_ge(vC.h, wDT)
                sC.inc(SC.activation(lndt[:].rearrange("p (c h) -> p c h", h=32),
                                     dt4[:], AF.Ln))
                TE.wait_ge(vC.h, max(wDT, mk["Glast"][0], mk["W2"][0], mk["W2"][1]))
                for c in range(GRP):
                    tC.inc(TE.matmul(p_sm[:, c * 32:(c + 1) * 32], cx.P_cum[:],
                                     dt4[:, c, :], start=(c == 0), stop=(c == GRP - 1)))
                VE.wait_ge(tC.h, tC.n)
                vC.inc(VE.tensor_copy(cumT[:], p_sm[:, 0:128]))
                VE.wait_ge(vC.h, vC.n)
                TE.wait_ge(vC.h, vC.n)
                for c in range(GRP):
                    tC.inc(TE.matmul(p_sm[0:1, c * 32:(c + 1) * 32],
                                     cx.P_onescol[:], dt4[:, c, :],
                                     start=(c == 0), stop=(c == GRP - 1)))
                VE.wait_ge(tC.h, tC.n)
                vC.inc(VE.tensor_copy(csrow[:], p_sm[0:1, 0:128]))
                VE.wait_ge(vC.h, vC.n)
                vC.inc(VE.tensor_copy(offr[0:1, 0:32], offr[0:1, 128:160]))
                for c in range(1, GRP):
                    VE.wait_ge(vC.h, vC.n)
                    vC.inc(VE.tensor_tensor(offr[0:1, c * 32:(c + 1) * 32],
                                            offr[0:1, (c - 1) * 32:c * 32],
                                            csrow[0:1, (c - 1) * 32:c * 32], ALU.add))
                VE.wait_ge(vC.h, vC.n)
                vC.inc(VE.tensor_tensor(offr[0:1, 128:160], offr[0:1, 96:128],
                                        csrow[0:1, 96:128], ALU.add))
                vC.inc(VE.tensor_tensor(acdr[:].rearrange("o (c h) -> o c h", h=32),
                                        csrow[:].rearrange("o (c h) -> o c h", h=32),
                                        cx.P_rowA[0:1, 0:32].unsqueeze(1)
                                        .broadcast_to([1, GRP, 32]), ALU.mult))
                VE.wait_ge(vC.h, vC.n)
                wOFF = vC.n
                SC.wait_ge(vC.h, wOFF)
                sC.inc(SC.activation(decr[:], acdr[:], AF.Exp, scale=-1.0))
                TE.wait_ge(vC.h, wOFF)
                tC.inc(TE.matmul(p_sm[:, 0:128], cx.P_ones_f[:], offr[0:1, 0:128],
                                 start=True, stop=True))
                VE.wait_ge(tC.h, tC.n)
                vC.inc(VE.tensor_tensor(gcum[:], cumT[:], p_sm[:, 0:128], ALU.add))
                VE.wait_ge(vC.h, vC.n)
                TE.wait_ge(vC.h, vC.n)
                tC.inc(TE.matmul(p_sm[:, 0:128], cx.P_ones_f[:], acdr[:],
                                 start=True, stop=True))
                VE.wait_ge(tC.h, tC.n)
                vC.inc(VE.tensor_copy(acd_bc[:], p_sm[:, 0:128]))
                VE.wait_ge(vC.h, vC.n)
                TE.wait_ge(vC.h, vC.n); TE.wait_ge(sC.h, sC.n)
                tC.inc(TE.matmul(p_sm[0:64, 0:128], cx.P_ones_f[0:1, 0:64], decr[:],
                                 start=True, stop=True))
                VE.wait_ge(tC.h, tC.n)
                vC.inc(VE.tensor_copy(dec_bc[:], p_sm[0:64, 0:128]))
                VE.wait_ge(vC.h, vC.n)
                vC.inc(VE.tensor_tensor(biasT[:].rearrange("p (c h) -> p c h", h=32),
                                        cumT[:].rearrange("p (c h) -> p c h", h=32),
                                        cx.P_A[:, 0:32].unsqueeze(1)
                                        .broadcast_to([128, GRP, 32]), ALU.mult))
                VE.wait_ge(vC.h, vC.n); SC.wait_ge(vC.h, vC.n)
                sC.inc(SC.activation(lamT[:], biasT[:], AF.Exp, scale=-1.0))
                VE.wait_ge(sC.h, sC.n)
                vC.inc(VE.tensor_tensor(biasT[:], biasT[:], lndt[:], ALU.add))
                VE.wait_ge(vC.h, vC.n)
                vC.inc(VE.tensor_tensor(gdtT[:], biasT[:], acd_bc[:], ALU.subtract))
                VE.wait_ge(vC.h, vC.n); SC.wait_ge(vC.h, vC.n)
                sC.inc(SC.activation(gdtT[:], gdtT[:], AF.Exp))
                for half in range(2):
                    cc = (2 * half, 2 * half + 1)
                    SC_mark = sC.n
                    VE.wait_ge(sC.h, SC_mark)
                    for ci, c in enumerate(cc):
                        vC.inc(VE.tensor_tensor(
                            qa[:, ci * 512:(ci + 1) * 512].rearrange("p (h j) -> p h j", j=16),
                            gcum[:, c * 32:(c + 1) * 32].unsqueeze(2).broadcast_to([128, 32, 16]),
                            cx.P_if2pi[:].rearrange("p (h j) -> p h j", j=16), ALU.mult))
                    VE.wait_ge(vC.h, vC.n)
                    vC.inc(VE.tensor_copy(qi[:], qa[:]))
                    VE.wait_ge(vC.h, vC.n)
                    vC.inc(VE.tensor_copy(qf[:], qi[:]))
                    VE.wait_ge(vC.h, vC.n)
                    vC.inc(VE.tensor_tensor(fr[:], qa[:], qf[:], ALU.subtract))
                    VE.wait_ge(vC.h, vC.n)
                    SC.wait_ge(vC.h, vC.n)
                    for ci, c in enumerate(cc):
                        for dup in range(2):
                            sC.inc(SC.activation(
                                sdup[:, c * 1024:(c + 1) * 1024]
                                    .rearrange("p (h d j) -> p h d j", d=2, j=16)[:, :, dup, :],
                                fr[:, ci * 512:(ci + 1) * 512].rearrange("p (h j) -> p h j", j=16),
                                AF.Sin, scale=TWO_PI))
                    vC.inc(VE.tensor_scalar(qa[:], qa[:], 0.25, None, ALU.add))
                    VE.wait_ge(vC.h, vC.n)
                    vC.inc(VE.tensor_copy(qi[:], qa[:]))
                    VE.wait_ge(vC.h, vC.n)
                    vC.inc(VE.tensor_copy(qf[:], qi[:]))
                    VE.wait_ge(vC.h, vC.n)
                    VE.wait_ge(sC.h, sC.n)
                    vC.inc(VE.tensor_tensor(fr[:], qa[:], qf[:], ALU.subtract))
                    VE.wait_ge(vC.h, vC.n)
                    SC.wait_ge(vC.h, vC.n)
                    for ci, c in enumerate(cc):
                        for dup in range(2):
                            sC.inc(SC.activation(
                                cdup[:, c * 1024:(c + 1) * 1024]
                                    .rearrange("p (h d j) -> p h d j", d=2, j=16)[:, :, dup, :],
                                fr[:, ci * 512:(ci + 1) * 512].rearrange("p (h j) -> p h j", j=16),
                                AF.Sin, scale=TWO_PI))
                TE.wait_ge(vC.h, max(vC.n, mk["W2"][0], mk["W2"][1]))
                TE.wait_ge(sC.h, max(mk["E"][0][0], mk["E"][0][1]))
                for c in range(GRP):
                    tC.inc(TE.matmul(p_R[0][0:32, c * 128:(c + 1) * 128],
                                     cumT[:, c * 32:(c + 1) * 32], cx.P_idt32[:],
                                     is_transpose=True, start=(c == 0), stop=(c == GRP - 1)))
                VE.wait_ge(tC.h, tC.n)
                vC.inc(VE.tensor_copy(cnm[:], p_R[0][0:32, 0:512]))
                mk["cn"] = vC.n
                mk["R"] = [vC.n, mk["R"][1]]
                return vC.n, sC.n

            def rot_head(h, wA_v, wA_s):
                hb = h % 2
                VE.wait_ge(tC.h, mk["rot"][hb])
                VE.wait_ge(sC.h, wA_s)
                co2 = cdup[:].rearrange("p (c h2 x) -> p c h2 x", h2=32, x=32)[:, :, h, :] \
                    .rearrange("p c (g j) -> p c g j", g=2)
                si2 = sdup[:].rearrange("p (c h2 x) -> p c h2 x", h2=32, x=32)[:, :, h, :] \
                    .rearrange("p c (g j) -> p c g j", g=2)
                bview = bc4[:].rearrange("p c (g n) -> p c g n", g=2)
                rview = rot[hb][:].rearrange("p c (g n) -> p c g n", g=2)
                bev, bod = bview[:, :, :, 0:32:2], bview[:, :, :, 1:32:2]
                rev, rod = rview[:, :, :, 0:32:2], rview[:, :, :, 1:32:2]
                rt = rtmp[hb][:].rearrange("p c (g j) -> p c g j", g=2)
                vC.inc(VE.tensor_copy(rview[:, :, :, 32:64], bview[:, :, :, 32:64]))
                vC.inc(VE.tensor_tensor(rev, bev, co2, ALU.mult))
                vC.inc(VE.tensor_tensor(rt, bod, si2, ALU.mult))
                VE.wait_ge(vC.h, vC.n)
                vC.inc(VE.tensor_tensor(rev, rev, rt, ALU.add))
                vC.inc(VE.tensor_tensor(rod, bod, co2, ALU.mult))
                vC.inc(VE.tensor_tensor(rt, bev, si2, ALU.mult))
                VE.wait_ge(vC.h, vC.n)
                vC.inc(VE.tensor_tensor(rod, rod, rt, ALU.subtract))
                VE.wait_ge(vC.h, vC.n)
                vC.inc(VE.tensor_tensor(
                    bgdt[hb][:], rot[hb][:, :, 0:64],
                    gdtT[:].rearrange("p (c h2) -> p c h2", h2=32)[:, :, h:h + 1]
                        .broadcast_to([128, GRP, 64]), ALU.mult))
                vC.inc(VE.tensor_tensor(
                    cltm[:, :, hb * 64:(hb + 1) * 64], rot[hb][:, :, 64:128],
                    lamT[:].rearrange("p (c h2) -> p c h2", h2=32)[:, :, h:h + 1]
                        .broadcast_to([128, GRP, 64]), ALU.mult))
                VE.wait_ge(vC.h, vC.n)
                return vC.n

            def pair_chunks(h0, b, wrot, wA_v):
                rdh = lambda c: (c % 2) ^ 1
                def stage_T(c):
                    cs = c % 2
                    for hb in range(2):
                        ix = hb * 2 + cs
                        TE.wait_ge(vC.h, max(mk["nm"][hb][cs], mk["tp"][cs], wrot[hb]))
                        tC.inc(TE.matmul(p_tp[cs][:, 0:128], rot[hb][:, c, :], cx.P_idt16[:],
                                         is_transpose=True, start=True, stop=True))
                        wTP = tC.n
                        VE.wait_ge(tC.h, wTP)
                        vC.inc(VE.tensor_copy(bnm[ix][:], p_tp[cs][0:64, 0:128]))
                        vC.inc(VE.tensor_copy(cnm2[ix][:], p_tp[cs][64:128, 0:128]))
                        mk["tp"][cs] = vC.n
                        mk["nm"][hb][cs] = vC.n
                    TE.wait_ge(vC.h, mk["tp"][cs])
                    tC.inc(TE.matmul(p_tp[cs][:, 0:128], cltm[:, c, :], cx.P_idt16[:],
                                     is_transpose=True, start=True, stop=True))
                    wTP = tC.n
                    VE.wait_ge(tC.h, wTP)
                    vC.inc(VE.tensor_copy(clnm[0 * 4 + c][:], p_tp[cs][0:64, 0:128]))
                    vC.inc(VE.tensor_copy(clnm[1 * 4 + c][:], p_tp[cs][64:128, 0:128]))
                    mk["tp"][cs] = vC.n

                def stage_RE(c):
                    cs = c % 2
                    for hb in range(2):
                        h = h0 + hb
                        TE.wait_ge(sC.h, mk["Elast"][hb])
                        TE.wait_ge(vC.h, max(mk["R"][hb], mk["cn"] if hb else wA_v))
                        tC.inc(TE.matmul(p_R[hb][:, 0:128],
                                         cx.P_onehot[:, h * 128:(h + 1) * 128],
                                         cnm[:, c * 128:(c + 1) * 128],
                                         start=True, stop=True))
                        wR = tC.n
                        SC.wait_ge(tC.h, wR)
                        SC.wait_ge(vC.h, mk["W"][hb][cs])
                        sC.inc(SC.activation(Etile[hb * 2 + cs][:], p_R[hb][:, 0:128], AF.Exp,
                                             scale=cx.P_negA[:, h:h + 1],
                                             bias=biasT[:, c * 32 + h:c * 32 + h + 1]))
                        mk["E"][hb][cs] = sC.n
                        mk["Elast"][hb] = sC.n

                def stage_GW(c):
                    cs = c % 2
                    for hb in range(2):
                        ix = hb * 2 + cs
                        TE.wait_ge(vC.h, max(mk["nm"][hb][cs], mk["Glast"][hb]))
                        tC.inc(TE.matmul(p_G[hb][:, 0:128], bnm[ix][:],
                                         cnm2[ix][:], start=True, stop=True))
                        wG = tC.n
                        VE.wait_ge(tC.h, max(wG, mk["Wr"][hb][cs]))
                        vC.inc(VE.scalar_tensor_tensor(Gm[ix][:], p_G[hb][:, 0:128], 1.0,
                                                       cx.P_wmask[:], ALU.mult, ALU.mult))
                        mk["G"][hb][cs] = vC.n
                        mk["Glast"][hb] = vC.n
                        VE.wait_ge(vC.h, vC.n); VE.wait_ge(sC.h, mk["E"][hb][cs])
                        vC.inc(VE.scalar_tensor_tensor(Wt[ix][:], Etile[ix][:], BIG,
                                                       Gm[ix][:], ALU.min, ALU.mult))
                        mk["W"][hb][cs] = vC.n

                def stage_Y(c):
                    cs = c % 2
                    rdH, wrH = (c % 2) ^ 1, c % 2
                    for hb in range(2):
                        h = h0 + hb
                        ix = hb * 2 + cs
                        TE.wait_ge(vC.h, mk["W"][hb][cs])
                        if c == 0 and hb == 0:
                            TE.wait_ge(vC.h, mk["Y"])
                        base = (hb * 4 + c) * 64
                        tC.inc(TE.matmul(p_Y[:, base:base + 64], Wt[ix][:],
                                         xs4[:, c, h * 64:(h + 1) * 64],
                                         start=(c == 0 and hb == 0), stop=False))
                        tC.inc(TE.matmul(p_Y[:, base:base + 64], clnm[hb * 4 + c][:],
                                         H[rdH][:, h * 64:(h + 1) * 64],
                                         start=False, stop=(c == GRP - 1 and hb == 1)))
                        mk["Wr"][hb][cs] = tC.n
                        TE.wait_ge(vC.h, mk["Sc"])
                        tC.inc(TE.matmul(p_Sc[:, 0:64], bgdt[hb][:, c, :],
                                         xs4[:, c, h * 64:(h + 1) * 64],
                                         start=True, stop=True))
                        wYS = tC.n
                        VE.wait_ge(tC.h, wYS)
                        vC.inc(VE.scalar_tensor_tensor(
                            H[wrH][:, h * 64:(h + 1) * 64],
                            H[rdH][:, h * 64:(h + 1) * 64],
                            dec_bc[:, c * 32 + h:c * 32 + h + 1],
                            p_Sc[:, 0:64], ALU.mult, ALU.add))
                        mk["Sc"] = vC.n

                stage_T(0)
                stage_T(1)
                stage_RE(0)
                for c in range(GRP):
                    stage_GW(c)
                    if c + 2 < GRP:
                        stage_T(c + 2)
                    if c + 1 < GRP:
                        stage_RE(c + 1)
                    stage_Y(c)
                VE.wait_ge(tC.h, tC.n)
                for hb in range(2):
                    h = h0 + hb
                    vC.inc(VE.scalar_tensor_tensor(
                        ybuf[:, :, h * 64:(h + 1) * 64],
                        xs4[:, :, h * 64:(h + 1) * 64],
                        cx.P_dsk[:, h:h + 1],
                        p_Y[:, hb * 256:hb * 256 + 256].rearrange("p (c x) -> p c x", x=64),
                        ALU.mult, ALU.add))
                mk["Y"] = vC.n
                mk["rot"][0] = tC.n
                mk["rot"][1] = tC.n

            def phaseC(b, g, first):
                g0 = b * L + g * GRP * Q
                if S2DBG and b == 0 and g == 0:
                    SY.wait_ge(vC.h, vC.n)
                    SY.wait_ge(tC.h, tC.n)
                    dS.inc(SY.dma_start(out=dbgt["dbg_y"][:],
                                        in_=ybuf[:].rearrange("p a b2 -> p (a b2)")), 16)
                    VE.wait_ge(dS.h, dS.n)
                VE.wait_ge(vC.h, vC.n)
                vC.inc(VE.tensor_tensor(ybuf[:], ybuf[:], sz4[:], ALU.mult))
                VE.wait_ge(vC.h, vC.n)
                SC.wait_ge(vC.h, vC.n)
                for c in range(GRP):
                    jdst = sdup if c < 2 else cdup
                    sC.inc(SC.activation(jdst[:, (c % 2) * 2048:(c % 2) * 2048 + 2048],
                                         ybuf[:, c, :], AF.Square,
                                         accum_out=yst[:, c, 0:1]))
                SC.wait_ge(sC.h, sC.n)
                for c in range(GRP):
                    sC.inc(SC.activation(yst[:, c, 1:2], yst[:, c, 0:1], AF.Ln,
                                         scale=1.0 / DI, bias=EPS_LN))
                SC.wait_ge(sC.h, sC.n)
                for c in range(GRP):
                    sC.inc(SC.activation(yst[:, c, 2:3], yst[:, c, 1:2], AF.Exp, scale=-0.5))
                VE.wait_ge(sC.h, sC.n)
                vC.inc(VE.tensor_tensor(ybuf[:], ybuf[:],
                                        yst[:, :, 2:3].broadcast_to([128, GRP, DI]),
                                        ALU.mult))
                VE.wait_ge(vC.h, vC.n)
                wYGN = vC.n
                if first:
                    TE.wait_ge(vC.h, wout_ready)
                for c in range(GRP):
                    TE.wait_ge(vC.h, wYGN)
                    for k in range(16):
                        ks = k % 2
                        TE.wait_ge(vC.h, mk["tp"][ks])
                        tC.inc(TE.matmul(p_tp[ks][:, 0:128], ybuf[:, c, k * 128:(k + 1) * 128],
                                         cx.P_idt16[:], is_transpose=True,
                                         start=True, stop=True))
                        VE.wait_ge(tC.h, tC.n)
                        vC.inc(VE.tensor_copy(ydmb[:, k, :], p_tp[ks][:, 0:128]))
                        mk["tp"][ks] = vC.n
                    wYD = vC.n
                    TE.wait_ge(vC.h, wYD)
                    TE.wait_ge(vC.h, max(mk["W2"][0], mk["W2"][1]))
                    TE.wait_ge(sC.h, max(mk["E"][0][0], mk["E"][0][1], mk["E"][1][0], mk["E"][1][1]))
                    for k in range(16):
                        for n in range(2):
                            tC.inc(TE.matmul(p_W[n][:, 0:512], ydmb[:, k, :],
                                             wout_sb[:, k, n * 512:(n + 1) * 512],
                                             start=(k == 0), stop=(k == 15)))
                    VE.wait_ge(tC.h, tC.n)
                    VE.wait_ge(dS.h, mk["mo"][c % 2])
                    vC.inc(VE.tensor_copy(mo[c % 2][:, 0:512], p_W[0][:, 0:512]))
                    vC.inc(VE.tensor_copy(mo[c % 2][:, 512:1024], p_W[1][:, 0:512]))
                    mk["W2"] = [vC.n, vC.n]
                    mk["R"] = [vC.n, vC.n]
                    r0 = g0 + c * 128
                    SY.wait_ge(vC.h, vC.n)
                    dS.inc(SY.dma_start(out=cx.mamba_tm[r0:r0 + 128, :], in_=mo[c % 2][:]), 16)
                    mk["mo"][c % 2] = dS.n

            dbg_g = [0]

            for b in range(BPC):
                VE.wait_ge(tC.h, tC.n)
                vC.inc(VE.memset(H[0][:], 0.0))
                vC.inc(VE.memset(H[1][:], 0.0))
                vC.inc(VE.memset(offr[:], 0.0))
                hinit = vC.n
                for g in range(NG):
                    dbg_g[0] = g
                    wA_v, wA_s = phaseA(b, g, hinit)
                    for h0 in range(0, NH, 2):
                        w0 = rot_head(h0, wA_v, wA_s)
                        w1 = rot_head(h0 + 1, wA_v, wA_s)
                        pair_chunks(h0, b, (w0, w1), wA_v)
                    phaseC(b, g, b == 0 and g == 0)
            SY.wait_ge(dS.h, dS.n)
            ies.close()
        blk.sync(emit)


def _s3(cx):
    nc = cx.nc
    SY, TE, VE, SC, GP = cx.SY, cx.TE, cx.VE, cx.SC, cx.GP
    with nc.Block("s3") as blk:
        def emit(_):
            ies = ExitStack()
            dW = CT(nc, ies, "s3_dW"); dL = CT(nc, ies, "s3_dL"); dS = CT(nc, ies, "s3_dS")
            tC = CT(nc, ies, "s3_tC"); vC = CT(nc, ies, "s3_vC"); sC = CT(nc, ies, "s3_sC")
            wq_sb = ies.enter_context(nc.sbuf_tensor("s3_wq", [128, 8, D], BF16))
            wk_sb = ies.enter_context(nc.sbuf_tensor("s3_wk", [128, 8, D], BF16))
            wv_sb = ies.enter_context(nc.sbuf_tensor("s3_wv", [128, 8, D], BF16))
            xb = [ies.enter_context(nc.sbuf_tensor(f"s3_x{i}", [128, D], BF16)) for i in range(2)]
            st = ies.enter_context(nc.sbuf_tensor("s3_st", [128, 40], F32))
            h16 = [ies.enter_context(nc.sbuf_tensor(f"s3_h{i}", [128, D], BF16)) for i in range(2)]
            hdm = [ies.enter_context(nc.sbuf_tensor(f"s3_hd{i}", [128, 8, 128], BF16)) for i in range(2)]
            sq = ies.enter_context(nc.sbuf_tensor("s3_sq", [128, D], F32))
            junk = ies.enter_context(nc.sbuf_tensor("s3_j", [128, D], F32))
            qtm = ies.enter_context(nc.sbuf_tensor("s3_qtm", [128, D], BF16))
            qdmb = ies.enter_context(nc.sbuf_tensor("s3_qd", [128, 8, 128], BF16))
            vtm = [ies.enter_context(nc.sbuf_tensor(f"s3_v{i}", [128, D], BF16)) for i in range(2)]
            ptp = [ies.enter_context(nc.psum_tensor(f"s3_p{i}", [128, 1024], BF16)) for i in range(2)]
            pmm = [ies.enter_context(nc.psum_tensor(f"s3_m{i}", [128, 512], F32)) for i in range(2)]
            dW.inc(GP.dma_start(out=wq_sb[:], in_=cx.wq_t.rearrange("(k p) n -> p k n", p=128)), 16)
            dW.inc(GP.dma_start(out=wk_sb[:], in_=cx.wk_t.rearrange("(k p) n -> p k n", p=128)), 16)
            dW.inc(GP.dma_start(out=wv_sb[:], in_=cx.wv_t.rearrange("(k p) n -> p k n", p=128)), 16)
            iter_v, iter_s = [], []
            mk_tp = [0, 0]
            for t in range(NT):
                bi = t % 2
                if t >= 2:
                    SY.wait_ge(vC.h, iter_v[t - 2])
                    SY.wait_ge(sC.h, iter_s[t - 2])
                dL.inc(SY.dma_start(out=xb[bi][:], in_=cx.xn_tm[t * 128:(t + 1) * 128, :]), 16)
                wL = dL.n
                if "h" not in S3ABL:
                    iter_v.append(vC.n); iter_s.append(sC.n)
                    continue
                SC.wait_ge(dL.h, wL)
                sC.inc(SC.activation(junk[:], xb[bi][:], AF.Square, accum_out=st[:, 0:1]))
                SC.wait_ge(sC.h, sC.n)
                sC.inc(SC.activation(st[:, 1:2], st[:, 0:1], AF.Ln, scale=1.0 / D, bias=EPS_RMS))
                SC.wait_ge(sC.h, sC.n)
                sC.inc(SC.activation(st[:, 2:3], st[:, 1:2], AF.Exp, scale=-0.5))
                VE.wait_ge(sC.h, sC.n); VE.wait_ge(dL.h, wL)
                vC.inc(VE.tensor_scalar(h16[bi][:], xb[bi][:], st[:, 2:3], None, ALU.mult))
                VE.wait_ge(vC.h, vC.n)
                vC.inc(VE.tensor_tensor(h16[bi][:], h16[bi][:], cx.P_n1t[:], ALU.mult))
                VE.wait_ge(vC.h, vC.n)
                wH = vC.n
                if "tp" not in S3ABL:
                    iter_v.append(vC.n); iter_s.append(sC.n)
                    continue
                TE.wait_ge(vC.h, wH)
                if t == 0:
                    TE.wait_ge(dW.h, dW.n)
                for k in range(8):
                    ks = k % 2
                    TE.wait_ge(vC.h, mk_tp[ks])
                    tC.inc(TE.matmul(ptp[ks][:, 0:128], h16[bi][:, k * 128:(k + 1) * 128],
                                     cx.P_idt16[:], is_transpose=True, start=True, stop=True))
                    VE.wait_ge(tC.h, tC.n)
                    vC.inc(VE.tensor_copy(hdm[bi][:, k, :], ptp[ks][:, 0:128]))
                    mk_tp[ks] = vC.n
                wHD = vC.n
                if "v" not in S3ABL:
                    iter_v.append(vC.n); iter_s.append(sC.n)
                    continue
                TE.wait_ge(vC.h, wHD)
                if t >= 2:
                    TE.wait_ge(vC.h, iter_v[t - 2])
                for n in range(2):
                    for k in range(8):
                        tC.inc(TE.matmul(pmm[n][:], hdm[bi][:, k, :],
                                         wv_sb[:, k, n * 512:(n + 1) * 512],
                                         start=(k == 0), stop=(k == 7)))
                VE.wait_ge(tC.h, tC.n)
                vC.inc(VE.tensor_copy(vtm[bi][:, 0:512], pmm[0][:]))
                vC.inc(VE.tensor_copy(vtm[bi][:, 512:1024], pmm[1][:]))
                SY.wait_ge(vC.h, vC.n)
                dS.inc(SY.dma_start(out=cx.v_tm[t * 128:(t + 1) * 128, :], in_=vtm[bi][:]), 16)
                if "qk" not in S3ABL:
                    iter_v.append(vC.n); iter_s.append(sC.n)
                    continue
                for wsb, ntile, dst in ((wq_sb, cx.P_qnt, cx.q_dm), (wk_sb, cx.P_knt, cx.k_dm)):
                    TE.wait_ge(vC.h, vC.n)
                    for n in range(2):
                        for k in range(8):
                            tC.inc(TE.matmul(pmm[n][:], hdm[bi][:, k, :],
                                             wsb[:, k, n * 512:(n + 1) * 512],
                                             start=(k == 0), stop=(k == 7)))
                    wQ = tC.n
                    SC.wait_ge(tC.h, wQ)
                    sC.inc(SC.activation(sq[:, 0:512], pmm[0][:], AF.Square))
                    sC.inc(SC.activation(sq[:, 512:1024], pmm[1][:], AF.Square))
                    VE.wait_ge(sC.h, sC.n)
                    for hh in range(16):
                        vC.inc(VE.reduce_sum(st[:, 8 + hh:9 + hh],
                                             sq[:, hh * 64:(hh + 1) * 64], axis=AX.X))
                    VE.wait_ge(vC.h, vC.n)
                    SC.wait_ge(vC.h, vC.n)
                    sC.inc(SC.activation(st[:, 24:40], st[:, 8:24], AF.Ln,
                                         scale=1.0 / HD, bias=EPS_RMS))
                    SC.wait_ge(sC.h, sC.n)
                    sC.inc(SC.activation(st[:, 24:40], st[:, 24:40], AF.Exp, scale=-0.5))
                    VE.wait_ge(sC.h, sC.n)
                    vC.inc(VE.tensor_tensor(
                        qtm[:, 0:512].rearrange("p (h x) -> p h x", x=64),
                        pmm[0][:].rearrange("p (h x) -> p h x", x=64),
                        st[:, 24:32].unsqueeze(2).broadcast_to([128, 8, 64]), ALU.mult))
                    vC.inc(VE.tensor_tensor(
                        qtm[:, 512:1024].rearrange("p (h x) -> p h x", x=64),
                        pmm[1][:].rearrange("p (h x) -> p h x", x=64),
                        st[:, 32:40].unsqueeze(2).broadcast_to([128, 8, 64]), ALU.mult))
                    VE.wait_ge(vC.h, vC.n)
                    vC.inc(VE.tensor_tensor(qtm[:], qtm[:], ntile[:], ALU.mult))
                    VE.wait_ge(vC.h, vC.n)
                    wQT = vC.n
                    TE.wait_ge(vC.h, wQT)
                    for k in range(8):
                        ks = k % 2
                        TE.wait_ge(vC.h, mk_tp[ks])
                        tC.inc(TE.matmul(ptp[ks][:, 0:128], qtm[:, k * 128:(k + 1) * 128],
                                         cx.P_idt16[:], is_transpose=True, start=True, stop=True))
                        VE.wait_ge(tC.h, tC.n)
                        vC.inc(VE.tensor_copy(qdmb[:, k, :], ptp[ks][:, 0:128]))
                        mk_tp[ks] = vC.n
                    SY.wait_ge(vC.h, vC.n)
                    dS.inc(SY.dma_start(
                        out=dst.rearrange("(k p) T2 -> p k T2", p=128)[:, :, t * 128:(t + 1) * 128],
                        in_=qdmb[:]), 16)
                    SY.wait_ge(dS.h, dS.n)
                iter_v.append(vC.n); iter_s.append(sC.n)
            SY.wait_ge(dS.h, dS.n)
            ies.close()
        blk.sync(emit)


def _s4(cx):
    nc = cx.nc
    SY, TE, VE, SC = cx.SY, cx.TE, cx.VE, cx.SC
    with nc.Block("s4") as blk:
        def emit(_):
            ies = ExitStack()
            dL = CT(nc, ies, "s4_dL"); dS = CT(nc, ies, "s4_dS")
            tC = CT(nc, ies, "s4_tC"); vC = CT(nc, ies, "s4_vC"); sC = CT(nc, ies, "s4_sC")
            NS = 4
            qb_ = [ies.enter_context(nc.sbuf_tensor(f"s4_q{i}", [64, L], BF16)) for i in range(2)]
            kb_ = [ies.enter_context(nc.sbuf_tensor(f"s4_k{i}", [64, L], BF16)) for i in range(2)]
            vb_ = [ies.enter_context(nc.sbuf_tensor(f"s4_v{i}", [128, 16, 65], BF16)) for i in range(2)]
            pb_ = [ies.enter_context(nc.sbuf_tensor(f"s4_pb{i}", [128, 512], BF16)) for i in range(NS)]
            o_un = [ies.enter_context(nc.sbuf_tensor(f"s4_ou{i}", [65, 4 * 512], F32)) for i in range(2)]
            rcp = [ies.enter_context(nc.sbuf_tensor(f"s4_rc{i}", [1, 4 * 512], F32)) for i in range(2)]
            rbc = [ies.enter_context(nc.sbuf_tensor(f"s4_rb{i}", [64, 512], F32)) for i in range(2)]
            ob = [ies.enter_context(nc.sbuf_tensor(f"s4_ob{i}", [64, 512], BF16)) for i in range(2)]
            pS = [ies.enter_context(nc.psum_tensor(f"s4_s{i}", [128, 512], F32)) for i in range(NS)]
            pO = [ies.enter_context(nc.psum_tensor(f"s4_o{i}", [65, 512], F32)) for i in range(2)]
            pB = ies.enter_context(nc.psum_tensor("s4_b", [64, 512], F32))
            vC.inc(VE.memset(vb_[0][:, :, 64:65], 1.0))
            vC.inc(VE.memset(vb_[1][:, :, 64:65], 1.0))
            sh = {"ks": 0, "mk_S": [0] * NS, "mk_B": 0, "mk_pb": [0] * NS,
                  "mk_O": [0, 0], "mk_ou": [0, 0], "mk_ob": [0, 0], "done": {}}

            def head_job(it):
                b, h = divmod(it, HEADS)
                bi = it % 2
                if it >= 2:
                    SY.wait_ge(tC.h, sh["done"].get(it - 2, 0))
                dL.inc(SY.dma_start(out=qb_[bi][:],
                                    in_=cx.q_dm[h * 64:(h + 1) * 64, b * L:(b + 1) * L]), 16)
                dL.inc(SY.dma_start(out=kb_[bi][:],
                                    in_=cx.k_dm[h * 64:(h + 1) * 64, b * L:(b + 1) * L]), 16)
                dL.inc(SY.dma_start(
                    out=vb_[bi][:, :, 0:64],
                    in_=cx.v_tm[b * L:(b + 1) * L, :]
                        .rearrange("(kt p) (h2 j) -> p kt h2 j", p=128, j=64)[:, :, h, :]), 16)
                wL = dL.n
                first = True
                kslot = [0]
                for qblk in range(4):
                    nkt = 4 * qblk + 4
                    exp_mark = {}
                    s_mark = {}
                    s_slot = {}
                    def emit_S(kt):
                        ks = 2 * bi + (kslot[0] % 2)
                        kslot[0] += 1
                        s_slot[kt] = ks
                        TE.wait_ge(sC.h, sh["mk_S"][ks])
                        if first:
                            TE.wait_ge(dL.h, wL)
                        tC.inc(TE.matmul(pS[ks][:], kb_[bi][:, kt * 128:(kt + 1) * 128],
                                         qb_[bi][:, qblk * 512:(qblk + 1) * 512],
                                         start=True, stop=True))
                        s_mark[kt] = tC.n
                    def emit_exp(kt):
                        ks = s_slot[kt]
                        wS = s_mark[kt]
                        dd = kt - 4 * qblk
                        if dd >= 0:
                            VE.wait_ge(tC.h, wS)
                            vC.inc(VE.tensor_tensor(pS[ks][:], pS[ks][:],
                                                    cx.P_amask[:, dd * 512:(dd + 1) * 512],
                                                    ALU.add))
                            SC.wait_ge(vC.h, vC.n)
                        SC.wait_ge(tC.h, max(wS, sh["mk_pb"][ks]))
                        sC.inc(SC.activation(pb_[ks][:], pS[ks][:], AF.Exp,
                                             scale=0.125, bias=cx.P_negc[:, 0:1]))
                        sh["mk_S"][ks] = sC.n
                        exp_mark[kt] = sC.n
                    emit_S(0)
                    if nkt > 1:
                        emit_S(1)
                    emit_exp(0)
                    first = False
                    for kt in range(nkt):
                        if kt + 2 < nkt:
                            emit_S(kt + 2)
                        if kt + 1 < nkt:
                            emit_exp(kt + 1)
                        TE.wait_ge(sC.h, exp_mark[kt])
                        if kt == 0:
                            TE.wait_ge(vC.h, sh["mk_O"][bi])
                        tC.inc(TE.matmul(pO[bi][:], vb_[bi][:, kt, :], pb_[s_slot[kt]][:],
                                         start=(kt == 0), stop=(kt == nkt - 1)))
                        sh["mk_pb"][s_slot[kt]] = tC.n
                        if kt % 4 == 3:
                            yield
                    wPV = tC.n
                    VE.wait_ge(tC.h, wPV)
                    if qblk == 0:
                        VE.wait_ge(vC.h, sh["mk_ou"][bi])
                    vC.inc(VE.tensor_copy(o_un[bi][:, qblk * 512:(qblk + 1) * 512], pO[bi][:]))
                    sh["mk_O"][bi] = vC.n
                    yield
                SC.wait_ge(vC.h, sh["mk_O"][bi])
                sC.inc(SC.activation(rcp[bi][0:1, :], o_un[bi][64:65, :], AF.Ln))
                SC.wait_ge(sC.h, sC.n)
                sC.inc(SC.activation(rcp[bi][0:1, :], rcp[bi][0:1, :], AF.Exp, scale=-1.0))
                wRC = sC.n
                for qblk in range(4):
                    TE.wait_ge(sC.h, wRC)
                    TE.wait_ge(vC.h, sh["mk_B"])
                    tC.inc(TE.matmul(pB[:], cx.P_ones_f[0:1, 0:64],
                                     rcp[bi][0:1, qblk * 512:(qblk + 1) * 512],
                                     start=True, stop=True))
                    VE.wait_ge(tC.h, tC.n)
                    vC.inc(VE.tensor_copy(rbc[bi][:], pB[:]))
                    sh["mk_B"] = vC.n
                    VE.wait_ge(vC.h, vC.n)
                    VE.wait_ge(dS.h, sh["mk_ob"][bi])
                    vC.inc(VE.scalar_tensor_tensor(ob[bi][:],
                                                   o_un[bi][0:64, qblk * 512:(qblk + 1) * 512],
                                                   1.0, rbc[bi][:], ALU.mult, ALU.mult))
                    SY.wait_ge(vC.h, vC.n)
                    dS.inc(SY.dma_start(
                        out=cx.ao_dm[h * 64:(h + 1) * 64,
                                     b * L + qblk * 512:b * L + (qblk + 1) * 512],
                        in_=ob[bi][:]), 16)
                    sh["mk_ob"][bi] = dS.n
                    yield
                sh["mk_ou"][bi] = vC.n
                sh["done"][it] = tC.n

            for it in range(0, BPC * HEADS, 2):
                g1, g2 = head_job(it), head_job(it + 1)
                alive = [g1, g2]
                while alive:
                    for gjob in list(alive):
                        try:
                            next(gjob)
                        except StopIteration:
                            alive.remove(gjob)
            SY.wait_ge(dS.h, dS.n)
            ies.close()
        blk.sync(emit)


def _s4b(cx):
    nc = cx.nc
    SY, TE, VE, GP = cx.SY, cx.TE, cx.VE, cx.GP
    with nc.Block("s4b") as blk:
        def emit(_):
            ies = ExitStack()
            dW = CT(nc, ies, "s4b_dW"); dL = CT(nc, ies, "s4b_dL"); dS = CT(nc, ies, "s4b_dS")
            tC = CT(nc, ies, "s4b_tC"); vC = CT(nc, ies, "s4b_vC")
            wo_sb = ies.enter_context(nc.sbuf_tensor("s4b_wo", [128, 8, D - MZ], BF16))
            ab = [ies.enter_context(nc.sbuf_tensor(f"s4b_a{i}", [128, 8, 128], BF16)) for i in range(2)]
            ob = [ies.enter_context(nc.sbuf_tensor(f"s4b_o{i}", [128, D - MZ], F32)) for i in range(2)]
            pm = [ies.enter_context(nc.psum_tensor(f"s4b_p{i}", [128, 512], F32)) for i in range(2)]
            dW.inc(GP.dma_start(out=wo_sb[:],
                                in_=cx.wo_t[:, MZ:].rearrange("(k p) n -> p k n", p=128)), 16)
            iter_v = []
            for t in range(NT):
                bi = t % 2
                if t >= 2:
                    SY.wait_ge(vC.h, iter_v[t - 2])
                dL.inc(SY.dma_start(
                    out=ab[bi][:],
                    in_=cx.ao_dm.rearrange("(k p) T2 -> p k T2", p=128)[:, :, t * 128:(t + 1) * 128]), 16)
                wL = dL.n
                TE.wait_ge(dL.h, wL)
                if t == 0:
                    TE.wait_ge(dW.h, dW.n)
                if t >= 2:
                    TE.wait_ge(vC.h, iter_v[t - 2])
                for n in range(2):
                    w0, w1 = n * 512, min((n + 1) * 512, D - MZ)
                    for k in range(8):
                        tC.inc(TE.matmul(pm[n][:, 0:w1 - w0], ab[bi][:, k, :],
                                         wo_sb[:, k, w0:w1], start=(k == 0), stop=(k == 7)))
                VE.wait_ge(tC.h, tC.n)
                vC.inc(VE.tensor_copy(ob[bi][:, 0:512], pm[0][:]))
                vC.inc(VE.tensor_copy(ob[bi][:, 512:768], pm[1][:, 0:256]))
                SY.wait_ge(vC.h, vC.n)
                dS.inc(SY.dma_start(out=cx.acc_tm[t * 128:(t + 1) * 128, :], in_=ob[bi][:]), 16)
                iter_v.append(vC.n)
            SY.wait_ge(dS.h, dS.n)
            ies.close()
        blk.sync(emit)


def _s5(cx):
    nc = cx.nc
    SY, TE, VE, SC, GP = cx.SY, cx.TE, cx.VE, cx.SC, cx.GP
    with nc.Block("s5") as blk:
        def emit(_):
            ies = ExitStack()
            dW = CT(nc, ies, "s5_dW"); dL = CT(nc, ies, "s5_dL"); dS = CT(nc, ies, "s5_dS")
            tC = CT(nc, ies, "s5_tC"); vC = CT(nc, ies, "s5_vC"); sC = CT(nc, ies, "s5_sC")
            gw_sb = ies.enter_context(nc.sbuf_tensor("s5_gw", [128, 8, 4], BF16))
            xnb = [ies.enter_context(nc.sbuf_tensor(f"s5_x{i}", [128, D], F32)) for i in range(2)]
            acb = [ies.enter_context(nc.sbuf_tensor(f"s5_a{i}", [128, D - MZ], F32)) for i in range(2)]
            x2 = ies.enter_context(nc.sbuf_tensor("s5_x2", [128, D], F32))
            st = ies.enter_context(nc.sbuf_tensor("s5_st", [128, 4], F32))
            junk = ies.enter_context(nc.sbuf_tensor("s5_j", [128, D], F32))
            xf16 = ies.enter_context(nc.sbuf_tensor("s5_xf", [128, D], BF16))
            xfd = [ies.enter_context(nc.sbuf_tensor(f"s5_d{i}", [128, 8, 128], BF16)) for i in range(2)]
            gate = ies.enter_context(nc.sbuf_tensor("s5_g", [128, 24], F32))
            gm = ies.enter_context(nc.sbuf_tensor("s5_gm", [128, 32], F32))
            wt4 = [ies.enter_context(nc.sbuf_tensor(f"s5_w{i}", [128, 4], F32)) for i in range(2)]
            ptp = [ies.enter_context(nc.psum_tensor(f"s5_p{i}", [128, 1024], BF16)) for i in range(2)]
            pg = ies.enter_context(nc.psum_tensor("s5_pg", [128, 512], F32))
            dW.inc(GP.dma_start(out=gw_sb[:], in_=cx.gw_t.rearrange("(k p) n -> p k n", p=128)), 16)
            iter_v, iter_s = [], []
            mk_tp = [0, 0]
            mk_pg = 0
            for t in range(NT):
                bi = t % 2
                if t >= 2:
                    SY.wait_ge(vC.h, iter_v[t - 2])
                    SY.wait_ge(sC.h, iter_s[t - 2])
                dL.inc(SY.dma_start(out=xnb[bi][:], in_=cx.xn_tm32[t * 128:(t + 1) * 128, :]), 16)
                dL.inc(SY.dma_start(out=acb[bi][:], in_=cx.acc_tm[t * 128:(t + 1) * 128, :]), 16)
                wL = dL.n
                VE.wait_ge(dL.h, wL)
                vC.inc(VE.tensor_copy(x2[:, 0:MZ], xnb[bi][:, 0:MZ]))
                vC.inc(VE.tensor_tensor(x2[:, MZ:], xnb[bi][:, MZ:], acb[bi][:], ALU.add))
                VE.wait_ge(vC.h, vC.n)
                SC.wait_ge(vC.h, vC.n)
                sC.inc(SC.activation(junk[:], x2[:], AF.Square, accum_out=st[:, 0:1]))
                SC.wait_ge(sC.h, sC.n)
                sC.inc(SC.activation(st[:, 1:2], st[:, 0:1], AF.Ln, scale=1.0 / D, bias=EPS_RMS))
                SC.wait_ge(sC.h, sC.n)
                sC.inc(SC.activation(st[:, 2:3], st[:, 1:2], AF.Exp, scale=-0.5))
                VE.wait_ge(sC.h, sC.n)
                vC.inc(VE.tensor_scalar(xf16[:], x2[:], st[:, 2:3], None, ALU.mult))
                VE.wait_ge(vC.h, vC.n)
                vC.inc(VE.tensor_tensor(xf16[:], xf16[:], cx.P_n2t[:], ALU.mult))
                VE.wait_ge(vC.h, vC.n)
                wXF = vC.n
                TE.wait_ge(vC.h, wXF)
                if t == 0:
                    TE.wait_ge(dW.h, dW.n)
                for k in range(8):
                    ks = k % 2
                    TE.wait_ge(vC.h, mk_tp[ks])
                    tC.inc(TE.matmul(ptp[ks][:, 0:128], xf16[:, k * 128:(k + 1) * 128],
                                     cx.P_idt16[:], is_transpose=True, start=True, stop=True))
                    VE.wait_ge(tC.h, tC.n)
                    vC.inc(VE.tensor_copy(xfd[bi][:, k, :], ptp[ks][:, 0:128]))
                    mk_tp[ks] = vC.n
                wXD = vC.n
                SY.wait_ge(vC.h, wXD)
                dS.inc(SY.dma_start(
                    out=cx.xf_dm.rearrange("(k p) T2 -> p k T2", p=128)[:, :, t * 128:(t + 1) * 128],
                    in_=xfd[bi][:]), 16)
                TE.wait_ge(vC.h, wXD)
                TE.wait_ge(vC.h, mk_pg)
                for k in range(8):
                    tC.inc(TE.matmul(pg[:, 0:4], xfd[bi][:, k, :], gw_sb[:, k, :],
                                     start=(k == 0), stop=(k == 7)))
                wG = tC.n
                VE.wait_ge(tC.h, wG)
                vC.inc(VE.reduce_max(st[:, 3:4], pg[:, 0:4], axis=AX.X, negate=True))
                VE.wait_ge(vC.h, vC.n)
                SC.wait_ge(vC.h, vC.n)
                sC.inc(SC.activation(gate[:, 0:4], pg[:, 0:4], AF.Exp,
                                     bias=st[:, 3:4], accum_out=gate[:, 4:5]))
                mk_pg = sC.n
                VE.wait_ge(sC.h, sC.n)
                vC.inc(VE.reciprocal(gate[:, 5:6], gate[:, 4:5]))
                VE.wait_ge(vC.h, vC.n)
                vC.inc(VE.tensor_scalar(gate[:, 0:4], gate[:, 0:4], gate[:, 5:6], None, ALU.mult))
                VE.wait_ge(vC.h, vC.n)
                vC.inc(VE.tensor_tensor(gm[:, 0:16].rearrange("p (e j) -> p e j", j=4),
                                        gate[:, 0:4].unsqueeze(1).broadcast_to([128, 4, 4]),
                                        gate[:, 0:4].unsqueeze(2).broadcast_to([128, 4, 4]),
                                        ALU.is_gt))
                vC.inc(VE.tensor_tensor(gm[:, 16:32].rearrange("p (e j) -> p e j", j=4),
                                        gate[:, 0:4].unsqueeze(1).broadcast_to([128, 4, 4]),
                                        gate[:, 0:4].unsqueeze(2).broadcast_to([128, 4, 4]),
                                        ALU.is_equal))
                VE.wait_ge(vC.h, vC.n)
                vC.inc(VE.tensor_tensor(gm[:, 16:32], gm[:, 16:32], cx.P_ltm[:], ALU.mult))
                VE.wait_ge(vC.h, vC.n)
                vC.inc(VE.tensor_tensor(gm[:, 0:16], gm[:, 0:16], gm[:, 16:32], ALU.add))
                VE.wait_ge(vC.h, vC.n)
                vC.inc(VE.tensor_reduce(gate[:, 8:12].unsqueeze(2),
                                        gm[:, 0:16].rearrange("p (e j) -> p e j", j=4),
                                        AX.X, ALU.add))
                VE.wait_ge(vC.h, vC.n)
                vC.inc(VE.tensor_scalar(gate[:, 12:16], gate[:, 8:12], 1.5, None, ALU.is_le))
                VE.wait_ge(vC.h, vC.n)
                vC.inc(VE.tensor_tensor(gate[:, 16:20], gate[:, 0:4], gate[:, 12:16], ALU.mult))
                VE.wait_ge(vC.h, vC.n)
                vC.inc(VE.reduce_sum(gate[:, 20:21], gate[:, 16:20], axis=AX.X))
                VE.wait_ge(vC.h, vC.n)
                vC.inc(VE.tensor_scalar(gate[:, 21:22], gate[:, 20:21], 1e-8, None, ALU.add))
                VE.wait_ge(vC.h, vC.n)
                vC.inc(VE.reciprocal(gate[:, 22:23], gate[:, 21:22]))
                VE.wait_ge(vC.h, vC.n)
                vC.inc(VE.tensor_scalar(wt4[bi][:], gate[:, 16:20], gate[:, 22:23], None, ALU.mult))
                SY.wait_ge(vC.h, vC.n)
                dS.inc(SY.dma_start(out=cx.wts_tm[t * 128:(t + 1) * 128, :], in_=wt4[bi][:]), 16)
                iter_v.append(vC.n); iter_s.append(sC.n)
            SY.wait_ge(dS.h, dS.n)
            ies.close()
        blk.sync(emit)


def _s6(cx, name, kind):
    nc = cx.nc
    SY, TE, VE, SC, GP = cx.SY, cx.TE, cx.VE, cx.SC, cx.GP
    with nc.Block(name) as blk:
        def emit(_):
            ies = ExitStack()
            dW = CT(nc, ies, f"{name}_dW"); dL = CT(nc, ies, f"{name}_dL")
            dS = CT(nc, ies, f"{name}_dS")
            tC = CT(nc, ies, f"{name}_tC"); vC = CT(nc, ies, f"{name}_vC")
            sC = CT(nc, ies, f"{name}_sC")
            sbt = lambda n_, s, d: ies.enter_context(nc.sbuf_tensor(name + n_, list(s), d))
            pst = lambda n_, s: ies.enter_context(nc.psum_tensor(name + n_, list(s), F32))
            eidx = {"e0": 0, "e1": 1, "e2b": 2, "e3": 3}.get(kind)
            rr = lambda ap_: ap_.rearrange("(k p) n -> p k n", p=128)
            if kind == "e0":
                wup = sbt("w_up", [128, 8, HID], BF16)
                wdn = sbt("w_dn", [128, 16, D - MZ], BF16)
                dW.inc(GP.dma_start(out=wup[:], in_=rr(cx.e0_up_t)), 16)
                dW.inc(GP.dma_start(out=wdn[:], in_=rr(cx.e0_dn_t[:, MZ:])), 16)
            elif kind == "e1":
                wup = sbt("w_u1", [128, 8, HID], BF16)
                wu2 = sbt("w_u2", [128, 8, HID], BF16)
                wdn = sbt("w_dn", [128, 16, D - MZ], BF16)
                dW.inc(GP.dma_start(out=wup[:], in_=rr(cx.e1_w1_t)), 16)
                dW.inc(GP.dma_start(out=wu2[:], in_=rr(cx.e1_w2_t)), 16)
                dW.inc(GP.dma_start(out=wdn[:], in_=rr(cx.e1_dn_t[:, MZ:])), 16)
            elif kind == "e3":
                wd0 = sbt("w_d0", [128, 8, D], BF16)
                wup = sbt("w_up", [128, 8, HID], BF16)
                wdn = sbt("w_dn", [128, 16, D - MZ], BF16)
                dW.inc(GP.dma_start(out=wd0[:], in_=rr(cx.e3_dn_t)), 16)
                dW.inc(GP.dma_start(out=wup[:], in_=rr(cx.e3_up_t)), 16)
                dW.inc(GP.dma_start(out=wdn[:], in_=rr(cx.e3_out_t[:, MZ:])), 16)
            elif kind == "e2a":
                wl1 = sbt("w_l1", [128, 8, HID], BF16)
                wl2 = sbt("w_l2", [128, 16, HID], BF16)
                dW.inc(GP.dma_start(out=wl1[:], in_=rr(cx.e2_l1_t)), 16)
                dW.inc(GP.dma_start(out=wl2[:], in_=rr(cx.e2_l2_t)), 16)
            else:
                wl3 = sbt("w_l3", [128, 16, HID], BF16)
                wdn = sbt("w_dn", [128, 16, D - MZ], BF16)
                dW.inc(GP.dma_start(out=wl3[:], in_=rr(cx.e2_l3_t)), 16)
                dW.inc(GP.dma_start(out=wdn[:], in_=rr(cx.e2_l4_t[:, MZ:])), 16)
            xfb = [sbt(f"xf{i}", [128, 8, 512], BF16) for i in range(2)]
            h2in = [sbt(f"h2i{i}", [128, 16, 512], BF16) for i in range(2)] if kind == "e2b" else None
            h1 = sbt("h1", [128, 16, 512], BF16)
            h1b = sbt("h1b", [128, 16, 512], BF16) if kind in ("e1", "e2a") else None
            h0 = sbt("h0", [128, 8, 512], BF16) if kind == "e3" else None
            wtb = [sbt(f"wt{i}", [128, 4, 4], F32) for i in range(2)]
            acb = sbt("ac", [128, D - MZ], F32)
            pu = [pst(f"pu{i}", [128, 512]) for i in range(2)]
            pd = [pst(f"pd{i}", [128, 512]) for i in range(2)]
            NGR = T // 512
            iter_end = []
            for gidx in range(NGR):
                bi = gidx % 2
                r0 = gidx * 512
                if gidx >= 2:
                    SY.wait_ge(tC.h, iter_end[gidx - 2][0])
                    SY.wait_ge(vC.h, iter_end[gidx - 2][1])
                if kind == "e2b":
                    dL.inc(SY.dma_start(
                        out=h2in[bi][:],
                        in_=cx.h2_dm.rearrange("(k p) T2 -> p k T2", p=128)[:, :, r0:r0 + 512]), 16)
                else:
                    dL.inc(SY.dma_start(
                        out=xfb[bi][:],
                        in_=cx.xf_dm.rearrange("(k p) T2 -> p k T2", p=128)[:, :, r0:r0 + 512]), 16)
                if kind != "e2a":
                    dL.inc(SY.dma_start(
                        out=wtb[bi][:],
                        in_=cx.wts_tm[r0:r0 + 512, :].rearrange("(s p) e -> p s e", p=128)), 16)
                wL = dL.n
                TE.wait_ge(dL.h, wL)
                if gidx == 0:
                    TE.wait_ge(dW.h, dW.n)
                def up_chain(wsb, nkb, src_idx, dst_sb, act):
                    for m in range(16):
                        ms = m % 2
                        TE.wait_ge(sC.h, sC.n); TE.wait_ge(vC.h, vC.n)
                        for k in range(nkb):
                            tC.inc(TE.matmul(pu[ms][:], wsb[:, k, m * 128:(m + 1) * 128],
                                             src_idx(k), start=(k == 0), stop=(k == nkb - 1)))
                        if act is not None:
                            SC.wait_ge(tC.h, tC.n)
                            sC.inc(SC.activation(dst_sb[:, m, :], pu[ms][:], act))
                        else:
                            VE.wait_ge(tC.h, tC.n)
                            vC.inc(VE.tensor_copy(dst_sb[:, m, :], pu[ms][:]))
                if kind == "e0":
                    up_chain(wup, 8, lambda k: xfb[bi][:, k, :], h1, AF.Gelu)
                elif kind == "e1":
                    up_chain(wup, 8, lambda k: xfb[bi][:, k, :], h1, AF.Silu)
                    up_chain(wu2, 8, lambda k: xfb[bi][:, k, :], h1b, None)
                    VE.wait_ge(sC.h, sC.n); VE.wait_ge(vC.h, vC.n)
                    vC.inc(VE.tensor_tensor(h1[:], h1[:], h1b[:], ALU.mult))
                    VE.wait_ge(vC.h, vC.n)
                elif kind == "e3":
                    for m in range(8):
                        ms = m % 2
                        TE.wait_ge(sC.h, sC.n)
                        for k in range(8):
                            tC.inc(TE.matmul(pu[ms][:], wd0[:, k, m * 128:(m + 1) * 128],
                                             xfb[bi][:, k, :], start=(k == 0), stop=(k == 7)))
                        SC.wait_ge(tC.h, tC.n)
                        sC.inc(SC.activation(h0[:, m, :], pu[ms][:], AF.Gelu))
                    TE.wait_ge(sC.h, sC.n)
                    up_chain(wup, 8, lambda k: h0[:, k, :], h1, AF.Gelu)
                elif kind == "e2a":
                    up_chain(wl1, 8, lambda k: xfb[bi][:, k, :], h1b, AF.Gelu)
                    TE.wait_ge(sC.h, sC.n)
                    up_chain(wl2, 16, lambda k: h1b[:, k, :], h1, AF.Gelu)
                    SY.wait_ge(sC.h, sC.n)
                    dS.inc(SY.dma_start(
                        out=cx.h2_dm.rearrange("(k p) T2 -> p k T2", p=128)[:, :, r0:r0 + 512],
                        in_=h1[:]), 16)
                    iter_end.append((tC.n, vC.n))
                    continue
                else:
                    up_chain(wl3, 16, lambda k: h2in[bi][:, k, :], h1, AF.Gelu)
                TE.wait_ge(sC.h, sC.n); TE.wait_ge(vC.h, vC.n)
                for msub in range(4):
                    SY.wait_ge(dS.h, dS.n)
                    dL.inc(SY.dma_start(out=acb[:],
                                        in_=cx.acc_tm[r0 + msub * 128:r0 + (msub + 1) * 128, :]), 16)
                    wA = dL.n
                    if msub >= 1:
                        TE.wait_ge(vC.h, vC.n)
                    for n in range(2):
                        w0, w1 = n * 512, min((n + 1) * 512, D - MZ)
                        for k in range(16):
                            tC.inc(TE.matmul(pd[n][:, 0:w1 - w0],
                                             h1[:, k, msub * 128:(msub + 1) * 128],
                                             wdn[:, k, w0:w1], start=(k == 0), stop=(k == 15)))
                    wD = tC.n
                    VE.wait_ge(tC.h, wD); VE.wait_ge(dL.h, wA)
                    vC.inc(VE.scalar_tensor_tensor(acb[:, 0:512], pd[0][:],
                                                   wtb[bi][:, msub, eidx:eidx + 1],
                                                   acb[:, 0:512], ALU.mult, ALU.add))
                    vC.inc(VE.scalar_tensor_tensor(acb[:, 512:768], pd[1][:, 0:256],
                                                   wtb[bi][:, msub, eidx:eidx + 1],
                                                   acb[:, 512:768], ALU.mult, ALU.add))
                    SY.wait_ge(vC.h, vC.n)
                    dS.inc(SY.dma_start(out=cx.acc_tm[r0 + msub * 128:r0 + (msub + 1) * 128, :],
                                        in_=acb[:]), 16)
                iter_end.append((tC.n, vC.n))
            SY.wait_ge(dS.h, dS.n)
            ies.close()
        blk.sync(emit)


def _s7(cx):
    nc = cx.nc
    SY, VE = cx.SY, cx.VE
    with nc.Block("s7") as blk:
        def emit(_):
            ies = ExitStack()
            dL = CT(nc, ies, "s7_dL"); dS = CT(nc, ies, "s7_dS"); vC = CT(nc, ies, "s7_vC")
            xb = [ies.enter_context(nc.sbuf_tensor(f"s7_x{i}", [128, D], F32)) for i in range(2)]
            mb = [ies.enter_context(nc.sbuf_tensor(f"s7_m{i}", [128, D], F32)) for i in range(2)]
            nb = [ies.enter_context(nc.sbuf_tensor(f"s7_n{i}", [128, D], F32)) for i in range(2)]
            ab = [ies.enter_context(nc.sbuf_tensor(f"s7_a{i}", [128, D - MZ], F32)) for i in range(2)]
            ob = [ies.enter_context(nc.sbuf_tensor(f"s7_o{i}", [128, D], F32)) for i in range(2)]
            iter_v = []
            for t in range(NT):
                bi = t % 2
                if t >= 2:
                    SY.wait_ge(vC.h, iter_v[t - 2])
                dL.inc(SY.dma_start(out=xb[bi][:], in_=cx.x[t * 128:(t + 1) * 128, :]), 16)
                dL.inc(SY.dma_start(out=mb[bi][:], in_=cx.mamba_tm[t * 128:(t + 1) * 128, :]), 16)
                dL.inc(SY.dma_start(out=nb[bi][:], in_=cx.xn_tm32[t * 128:(t + 1) * 128, :]), 16)
                dL.inc(SY.dma_start(out=ab[bi][:], in_=cx.acc_tm[t * 128:(t + 1) * 128, :]), 16)
                wL = dL.n
                VE.wait_ge(dL.h, wL)
                vC.inc(VE.tensor_tensor(ob[bi][:], xb[bi][:], mb[bi][:], ALU.add))
                VE.wait_ge(vC.h, vC.n)
                vC.inc(VE.tensor_tensor(ob[bi][:], ob[bi][:], nb[bi][:], ALU.add))
                VE.wait_ge(vC.h, vC.n)
                vC.inc(VE.tensor_tensor(ob[bi][:, MZ:], ob[bi][:, MZ:], ab[bi][:], ALU.add))
                SY.wait_ge(vC.h, vC.n)
                dS.inc(SY.dma_start(out=cx.out[t * 128:(t + 1) * 128, :], in_=ob[bi][:]), 16)
                iter_v.append(vC.n)
            SY.wait_ge(dS.h, dS.n)
            ies.close()
        blk.sync(emit)


def build(dbg=False, stages=("s0", "s1", "s2", "s3", "s4", "s5", "s6", "s7")):
    nc = bass.Bass("TRN2", target_bir_lowering=False)
    cx = _declare(nc, dbg)
    _init_stage(cx)
    if "s0" in stages: _s0(cx)
    if "s1" in stages: _s1(cx)
    if "s2" in stages: _s2(cx)
    if "s3" in stages: _s3(cx)
    if "s4" in stages: _s4(cx)
    if "s4" in stages: _s4b(cx)
    if "s5" in stages: _s5(cx)
    if "s6" in stages:
        _s6(cx, "s6e1", "e1")
        _s6(cx, "s6e0", "e0")
        _s6(cx, "s6e3", "e3")
        _s6(cx, "s6e2a", "e2a")
        _s6(cx, "s6e2b", "e2b")
    if "s7" in stages: _s7(cx)
    cx.es.close()
    return nc


_CACHE = {}

def kernel(**inputs):
    consts = _host_consts()
    if "nc" not in _CACHE:
        _CACHE["nc"] = build(dbg=False)
    nc = _CACHE["nc"]
    in_maps = []
    for core in range(NCORES):
        m = dict(_host_layout(inputs, core))
        m.update(consts)
        in_maps.append(m)
    res = run_bass_kernel_spmd(nc, in_maps, core_ids=list(range(NCORES)))
    outs = [np.asarray(r["out"], np.float32).reshape(BPC, L, D) for r in res.results]
    return np.concatenate(outs, axis=0)


# version 06
# speedup vs baseline: 1.2352x; 1.0601x over previous
import numpy as np
import ml_dtypes
import concourse.bass as bass
import concourse.mybir as mybir
from concourse.bass_utils import run_bass_kernel_spmd
from contextlib import ExitStack

AF = mybir.ActivationFunctionType
ALU = mybir.AluOpType
AX = mybir.AxisListType
F32, BF16, I32 = mybir.dt.float32, mybir.dt.bfloat16, mybir.dt.int32

NCORES = 8
BPC = 4
L = 2048
T = BPC * L
D = 1024
DI = 2048
NH = 32
Q = 128
NCH = L // Q
GRP = 4
NG = NCH // GRP
HEADS = 16
HD = 64
MZ = 256
HID = 2048
NT = T // 128
EPS_RMS = 1.1920929e-07
EPS_LN = 1e-5
TWO_PI = float(2.0 * np.pi)
BIG = 1e30


class CT:
    def __init__(self, nc, es, name):
        self.h = es.enter_context(nc.semaphore(name))
        self.n = 0
    def inc(self, inst, k=1):
        inst.then_inc(self.h, k)
        self.n += k
        return self.n


def _host_consts():
    c = {}
    le = np.where(np.arange(128)[:, None] <= np.arange(128)[None, :], 1.0, 0.0)
    c["c_cum"] = le.astype(np.float32)
    c["c_wmask"] = le.astype(ml_dtypes.bfloat16)
    c["c_idt16"] = np.eye(128, dtype=np.float32).astype(ml_dtypes.bfloat16)
    c["c_idt32"] = np.eye(128, dtype=np.float32)
    oh = np.zeros((32, 32 * 128), np.float32)
    for h in range(32):
        oh[h, h * 128:(h + 1) * 128] = 1.0
    c["c_onehot"] = oh
    inv2pi = (1.0 / 10000.0 ** (np.arange(16, dtype=np.float64) / 16.0)) / (2 * np.pi)
    c["c_if2pi"] = np.broadcast_to(inv2pi.astype(np.float32), (128, 32, 16)).reshape(128, 512).copy()
    am = np.zeros((4, 128, 512), np.float32)
    for i, dd in enumerate((0, 128, 256, 384)):
        am[i] = np.where(np.arange(128)[:, None] + dd <= np.arange(512)[None, :], 0.0, -30000.0)
    c["c_amask"] = np.ascontiguousarray(am.reshape(4 * 128, 512))
    ltm = np.zeros((4, 4), np.float32)
    for e in range(4):
        ltm[e, :e] = 1.0
    c["c_ltm"] = np.broadcast_to(ltm.reshape(1, 16), (128, 16)).copy()
    c["c_ones_f"] = np.ones((1, 128), np.float32)
    c["c_onescol"] = np.ones((128, 1), np.float32)
    return c


def _host_layout(inputs, core):
    g = lambda k: np.asarray(inputs[k])
    tr = lambda a: np.ascontiguousarray(np.asarray(a).T)
    d = {}
    d["x"] = np.ascontiguousarray(g("L0_attn_x")[core * BPC:(core + 1) * BPC].reshape(T, D))
    W_in = g("L0_attn_W_in")
    d["w_z"] = np.ascontiguousarray(W_in[:, :DI])
    d["w_x"] = np.ascontiguousarray(W_in[:, DI:2 * DI])
    d["w_bcd"] = np.ascontiguousarray(W_in[:, 2 * DI:])
    d["w_out"] = np.ascontiguousarray(g("L0_attn_W_out"))
    d["wq_t"] = tr(g("L0_ffn_wq")); d["wk_t"] = tr(g("L0_ffn_wk"))
    d["wv_t"] = tr(g("L0_ffn_wv")); d["wo_t"] = tr(g("L0_ffn_wo"))
    d["gw_t"] = tr(g("L0_ffn_gate_w"))
    d["e0_up_t"] = tr(g("L0_ffn_e0_up")); d["e0_dn_t"] = tr(g("L0_ffn_e0_down"))
    d["e1_w1_t"] = tr(g("L0_ffn_e1_w1")); d["e1_w2_t"] = tr(g("L0_ffn_e1_w2"))
    d["e1_dn_t"] = tr(g("L0_ffn_e1_down"))
    d["e2_l1_t"] = tr(g("L0_ffn_e2_l1")); d["e2_l2_t"] = tr(g("L0_ffn_e2_l2"))
    d["e2_l3_t"] = tr(g("L0_ffn_e2_l3")); d["e2_l4_t"] = tr(g("L0_ffn_e2_l4"))
    d["e3_dn_t"] = tr(g("L0_ffn_e3_down")); d["e3_up_t"] = tr(g("L0_ffn_e3_up"))
    d["e3_out_t"] = tr(g("L0_ffn_e3_out"))
    d["a_log_r"] = np.ascontiguousarray(g("L0_attn_A_log").reshape(1, NH))
    d["qn_r"] = np.ascontiguousarray(g("L0_ffn_qn_w").reshape(1, HD))
    d["kn_r"] = np.ascontiguousarray(g("L0_ffn_kn_w").reshape(1, HD))
    d["dtb_bc"] = np.ascontiguousarray(np.broadcast_to(g("L0_attn_dt_bias"), (128, NH)))
    d["dsk_bc"] = np.ascontiguousarray(np.broadcast_to(g("L0_attn_D_skip"), (128, NH)))
    d["n1t"] = np.ascontiguousarray(np.broadcast_to(g("L0_ffn_n1_w"), (128, D)))
    d["n2t"] = np.ascontiguousarray(np.broadcast_to(g("L0_ffn_n2_w"), (128, D)))
    d["qnt"] = np.ascontiguousarray(np.broadcast_to(np.tile(g("L0_ffn_qn_w"), HEADS), (128, D)))
    d["knt"] = np.ascontiguousarray(np.broadcast_to(np.tile(g("L0_ffn_kn_w"), HEADS), (128, D)))
    d["nw_c"] = np.ascontiguousarray(g("L0_attn_norm_weight").reshape(16, 128).T)
    return d


class Ctx:
    pass


def _declare(nc, dbg):
    cx = Ctx()
    cx.nc = nc
    cx.es = ExitStack()
    cx.SY, cx.TE, cx.VE, cx.SC, cx.GP = nc.sync, nc.tensor, nc.vector, nc.scalar, nc.gpsimd
    def din(name, shape, dt=F32):
        ap = nc.dram_tensor(name, list(shape), dt, kind="ExternalInput").ap()
        setattr(cx, name, ap)
    din("x", [T, D])
    din("w_z", [D, DI]); din("w_x", [D, DI]); din("w_bcd", [D, 160])
    din("w_out", [DI, D])
    din("wq_t", [D, D]); din("wk_t", [D, D]); din("wv_t", [D, D])
    din("wo_t", [D, D]); din("gw_t", [D, 4])
    din("e0_up_t", [D, HID]); din("e0_dn_t", [HID, D])
    din("e1_w1_t", [D, HID]); din("e1_w2_t", [D, HID]); din("e1_dn_t", [HID, D])
    din("e2_l1_t", [D, HID]); din("e2_l2_t", [HID, HID])
    din("e2_l3_t", [HID, HID]); din("e2_l4_t", [HID, D])
    din("e3_dn_t", [D, D]); din("e3_up_t", [D, HID]); din("e3_out_t", [HID, D])
    din("a_log_r", [1, NH]); din("qn_r", [1, HD]); din("kn_r", [1, HD])
    din("dtb_bc", [128, NH]); din("dsk_bc", [128, NH])
    din("n1t", [128, D]); din("n2t", [128, D]); din("qnt", [128, D]); din("knt", [128, D])
    din("nw_c", [128, 16])
    din("c_cum", [128, 128]); din("c_wmask", [128, 128], BF16)
    din("c_idt16", [128, 128], BF16); din("c_idt32", [128, 128])
    din("c_onehot", [32, 32 * 128]); din("c_if2pi", [128, 512])
    din("c_amask", [4 * 128, 512]); din("c_ltm", [128, 16])
    din("c_ones_f", [1, 128]); din("c_onescol", [128, 1])
    skind = "ExternalOutput" if dbg else "Internal"
    def dscr(name, shape, dt):
        ap = nc.dram_tensor(name, list(shape), dt, kind=skind).ap()
        setattr(cx, name, ap)
    dscr("xn_tm", [T, D], BF16)
    dscr("xn_tm32", [T, D], F32)
    dscr("xn_dm", [D, T], BF16)
    dscr("sz_tm", [T, DI], BF16)
    dscr("xs_tm", [T, DI], BF16)
    dscr("bc_tm", [T, 128], BF16)
    dscr("dtr_tm", [T, NH], F32)
    dscr("mamba_tm", [T, D], F32)
    dscr("q_dm", [D, T], BF16)
    dscr("k_dm", [D, T], BF16)
    dscr("v_tm", [T, D], BF16)
    dscr("ao_dm", [D, T], BF16)
    dscr("acc_tm", [T, D - MZ], F32)
    dscr("xf_dm", [D, T], BF16)
    dscr("wts_tm", [T, 4], F32)
    dscr("h2_dm", [HID, T], BF16)
    cx.out = nc.dram_tensor("out", [T, D], F32, kind="ExternalOutput").ap()
    def psb(name, shape, dt):
        setattr(cx, name, cx.es.enter_context(nc.sbuf_tensor(name, list(shape), dt)))
    psb("P_idt16", [128, 128], BF16)
    psb("P_idt32", [128, 128], F32)
    psb("P_cum", [128, 128], F32)
    psb("P_wmask", [128, 128], BF16)
    psb("P_onehot", [32, 32 * 128], F32)
    psb("P_if2pi", [128, 512], F32)
    psb("P_amask", [128, 4 * 512], F32)
    psb("P_ltm", [128, 16], F32)
    psb("P_ones_f", [1, 128], F32)
    psb("P_onescol", [128, 1], F32)
    psb("P_dtb", [128, NH], F32)
    psb("P_dsk", [128, NH], F32)
    psb("P_n1t", [128, D], BF16)
    psb("P_n2t", [128, D], BF16)
    psb("P_qnt", [128, D], BF16)
    psb("P_knt", [128, D], BF16)
    psb("P_nwc", [128, 16], F32)
    psb("P_A", [128, NH], F32)
    psb("P_negA", [128, NH], F32)
    psb("P_negc", [128, 1], F32)
    psb("P_tiny", [1, 512], F32)
    psb("P_rowA", [1, 128], F32)
    psb("P_rowNegA", [1, 128], F32)
    psb("P_rowNegc", [1, 128], F32)
    psb("P_const", [128, 2], F32)
    nc.const_aps.register(F32, EPS_LN, cx.P_const[:, 0:1])
    nc.const_aps.register(F32, EPS_RMS, cx.P_const[:, 1:2])
    return cx


def _init_stage(cx):
    nc = cx.nc
    SY, TE, VE, SC, GP = cx.SY, cx.TE, cx.VE, cx.SC, cx.GP
    with nc.Block("init") as blk:
        def emit(_):
            ies = ExitStack()
            dL = CT(nc, ies, "i_dL"); vC = CT(nc, ies, "i_vC")
            sC = CT(nc, ies, "i_sC"); tC = CT(nc, ies, "i_tC")
            pb = ies.enter_context(nc.psum_tensor("i_pb", [128, 512], F32))
            for dst, src in [(cx.P_cum, cx.c_cum), (cx.P_wmask, cx.c_wmask),
                             (cx.P_idt16, cx.c_idt16), (cx.P_idt32, cx.c_idt32),
                             (cx.P_onehot, cx.c_onehot), (cx.P_if2pi, cx.c_if2pi),
                             (cx.P_ltm, cx.c_ltm), (cx.P_ones_f, cx.c_ones_f),
                             (cx.P_onescol, cx.c_onescol), (cx.P_dtb, cx.dtb_bc),
                             (cx.P_dsk, cx.dsk_bc), (cx.P_nwc, cx.nw_c)]:
                dL.inc(SY.dma_start(out=dst[:], in_=src[:]), 16)
            dL.inc(SY.dma_start(out=cx.P_amask[:].rearrange("p (d n) -> p d n", n=512),
                                in_=cx.c_amask.rearrange("(d p) n -> p d n", p=128)), 16)
            for dst, src in [(cx.P_n1t, cx.n1t), (cx.P_n2t, cx.n2t),
                             (cx.P_qnt, cx.qnt), (cx.P_knt, cx.knt)]:
                dL.inc(GP.dma_start(out=dst[:], in_=src[:]), 16)
            vC.inc(VE.memset(cx.P_const[:, 0:1], EPS_LN))
            vC.inc(VE.memset(cx.P_const[:, 1:2], EPS_RMS))
            dL.inc(SY.dma_start(out=cx.P_tiny[0:1, 0:NH], in_=cx.a_log_r[:]), 16)
            dL.inc(SY.dma_start(out=cx.P_tiny[0:1, 32:96], in_=cx.qn_r[:]), 16)
            dL.inc(SY.dma_start(out=cx.P_tiny[0:1, 96:160], in_=cx.kn_r[:]), 16)
            VE.wait_ge(dL.h, dL.n); SC.wait_ge(dL.h, dL.n)
            sC.inc(SC.activation(cx.P_rowA[0:1, 0:NH], cx.P_tiny[0:1, 0:NH], AF.Exp))
            vC.inc(VE.reduce_max(cx.P_tiny[0:1, 160:161], cx.P_tiny[0:1, 32:96], axis=AX.X,
                                 apply_absolute_value=True))
            vC.inc(VE.reduce_max(cx.P_tiny[0:1, 161:162], cx.P_tiny[0:1, 96:160], axis=AX.X,
                                 apply_absolute_value=True))
            VE.wait_ge(vC.h, vC.n)
            vC.inc(VE.tensor_tensor(cx.P_tiny[0:1, 162:163], cx.P_tiny[0:1, 160:161],
                                    cx.P_tiny[0:1, 161:162], ALU.mult))
            VE.wait_ge(vC.h, vC.n)
            vC.inc(VE.tensor_scalar(cx.P_rowNegc[0:1, 0:1], cx.P_tiny[0:1, 162:163],
                                    -8.0, None, ALU.mult))
            SC.wait_ge(sC.h, sC.n)
            sC.inc(SC.mul(cx.P_rowNegA[0:1, 0:NH], cx.P_rowA[0:1, 0:NH], -1.0))
            TE.wait_ge(sC.h, sC.n); TE.wait_ge(vC.h, vC.n)
            tC.inc(TE.matmul(pb[:, 0:NH], cx.P_ones_f[:], cx.P_rowA[0:1, 0:NH],
                             start=True, stop=True))
            VE.wait_ge(tC.h, tC.n)
            vC.inc(VE.tensor_copy(cx.P_A[:], pb[:, 0:NH]))
            VE.wait_ge(vC.h, vC.n)
            TE.wait_ge(vC.h, vC.n)
            tC.inc(TE.matmul(pb[:, 0:NH], cx.P_ones_f[:], cx.P_rowNegA[0:1, 0:NH],
                             start=True, stop=True))
            VE.wait_ge(tC.h, tC.n)
            vC.inc(VE.tensor_copy(cx.P_negA[:], pb[:, 0:NH]))
            VE.wait_ge(vC.h, vC.n)
            TE.wait_ge(vC.h, vC.n)
            tC.inc(TE.matmul(pb[:, 0:1], cx.P_ones_f[:], cx.P_rowNegc[0:1, 0:1],
                             start=True, stop=True))
            VE.wait_ge(tC.h, tC.n)
            vC.inc(VE.tensor_copy(cx.P_negc[:], pb[:, 0:1]))
            SY.wait_ge(vC.h, vC.n)
            ies.close()
        blk.sync(emit)


def _s0(cx):
    nc = cx.nc
    SY, TE, VE, SC = cx.SY, cx.TE, cx.VE, cx.SC
    with nc.Block("s0") as blk:
        def emit(_):
            ies = ExitStack()
            dL = CT(nc, ies, "s0_dL"); dS = CT(nc, ies, "s0_dS")
            tC = CT(nc, ies, "s0_tC"); vC = CT(nc, ies, "s0_vC"); sC = CT(nc, ies, "s0_sC")
            xbuf = [ies.enter_context(nc.sbuf_tensor(f"s0_x{i}", [128, D], F32)) for i in range(2)]
            junk = ies.enter_context(nc.sbuf_tensor("s0_junk", [128, D], F32))
            st = ies.enter_context(nc.sbuf_tensor("s0_st", [128, 8], F32))
            xn16 = [ies.enter_context(nc.sbuf_tensor(f"s0_a{i}", [128, D], BF16)) for i in range(2)]
            xn32 = [ies.enter_context(nc.sbuf_tensor(f"s0_b{i}", [128, D], F32)) for i in range(2)]
            dmb = [ies.enter_context(nc.sbuf_tensor(f"s0_d{i}", [128, 8, 128], BF16)) for i in range(2)]
            ptp = [ies.enter_context(nc.psum_tensor(f"s0_p{i}", [128, 1024], BF16)) for i in range(2)]
            iter_v, iter_s = [], []
            tp_evac = [0, 0]
            for t in range(NT):
                bi = t % 2
                if t >= 2:
                    SY.wait_ge(vC.h, iter_v[t - 2])
                    SY.wait_ge(sC.h, iter_s[t - 2])
                dL.inc(SY.dma_start(out=xbuf[bi][:], in_=cx.x[t * 128:(t + 1) * 128, :]), 16)
                wL = dL.n
                VE.wait_ge(dL.h, wL); SC.wait_ge(dL.h, wL)
                vC.inc(VE.reduce_sum(st[:, 0:1], xbuf[bi][:], axis=AX.X))
                sC.inc(SC.activation(junk[:], xbuf[bi][:], AF.Square, accum_out=st[:, 1:2]))
                VE.wait_ge(sC.h, sC.n); VE.wait_ge(vC.h, vC.n)
                vC.inc(VE.tensor_scalar(st[:, 2:3], st[:, 0:1], 1.0 / D, None, ALU.mult))
                VE.wait_ge(vC.h, vC.n)
                vC.inc(VE.tensor_tensor(st[:, 3:4], st[:, 2:3], st[:, 2:3], ALU.mult))
                VE.wait_ge(vC.h, vC.n)
                vC.inc(VE.tensor_scalar(st[:, 4:5], st[:, 1:2], 1.0 / D, st[:, 3:4],
                                        ALU.mult, ALU.subtract))
                SC.wait_ge(vC.h, vC.n)
                sC.inc(SC.activation(st[:, 5:6], st[:, 4:5], AF.Ln, bias=EPS_LN))
                SC.wait_ge(sC.h, sC.n)
                sC.inc(SC.activation(st[:, 6:7], st[:, 5:6], AF.Exp, scale=-0.5))
                VE.wait_ge(sC.h, sC.n)
                vC.inc(VE.scalar_tensor_tensor(st[:, 7:8], st[:, 2:3], -1.0,
                                               st[:, 6:7], ALU.mult, ALU.mult))
                VE.wait_ge(vC.h, vC.n); SC.wait_ge(vC.h, vC.n)
                sC.inc(SC.activation(xn16[bi][:], xbuf[bi][:], AF.Identity,
                                     bias=st[:, 7:8], scale=st[:, 6:7]))
                vC.inc(VE.scalar_tensor_tensor(xn32[bi][:], xbuf[bi][:], st[:, 6:7],
                                               st[:, 7:8].broadcast_to([128, D]),
                                               ALU.mult, ALU.add))
                wS, wV = sC.n, vC.n
                TE.wait_ge(sC.h, wS)
                for k in range(8):
                    ks = k % 2
                    TE.wait_ge(vC.h, tp_evac[ks])
                    tC.inc(TE.matmul(ptp[ks][:, 0:128], xn16[bi][:, k * 128:(k + 1) * 128],
                                     cx.P_idt16[:], is_transpose=True, start=True, stop=True))
                    VE.wait_ge(tC.h, tC.n)
                    vC.inc(VE.tensor_copy(dmb[bi][:, k, :], ptp[ks][:, 0:128]))
                    tp_evac[ks] = vC.n
                SY.wait_ge(sC.h, wS)
                dS.inc(SY.dma_start(out=cx.xn_tm[t * 128:(t + 1) * 128, :], in_=xn16[bi][:]), 16)
                SY.wait_ge(vC.h, wV)
                dS.inc(SY.dma_start(out=cx.xn_tm32[t * 128:(t + 1) * 128, :], in_=xn32[bi][:]), 16)
                SY.wait_ge(vC.h, vC.n)
                dS.inc(SY.dma_start(
                    out=cx.xn_dm.rearrange("(k p) T2 -> p k T2", p=128)[:, :, t * 128:(t + 1) * 128],
                    in_=dmb[bi][:]), 16)
                iter_v.append(vC.n); iter_s.append(sC.n)
            SY.wait_ge(dS.h, dS.n)
            ies.close()
        blk.sync(emit)


def _s1(cx):
    nc = cx.nc
    SY, TE, VE, SC, GP = cx.SY, cx.TE, cx.VE, cx.SC, cx.GP
    with nc.Block("s1") as blk:
        def emit(_):
            ies = ExitStack()
            dW = CT(nc, ies, "s1_dW"); dL = CT(nc, ies, "s1_dL"); dS = CT(nc, ies, "s1_dS")
            tC = CT(nc, ies, "s1_tC"); vC = CT(nc, ies, "s1_vC"); sC = CT(nc, ies, "s1_sC")
            wz_sb = ies.enter_context(nc.sbuf_tensor("s1_wz", [128, 8, DI], BF16))
            wx_sb = ies.enter_context(nc.sbuf_tensor("s1_wx", [128, 8, DI], BF16))
            wb_sb = ies.enter_context(nc.sbuf_tensor("s1_wb", [128, 8, 160], BF16))
            xb = [ies.enter_context(nc.sbuf_tensor(f"s1_x{i}", [128, 8, 128], BF16)) for i in range(2)]
            zb = [ies.enter_context(nc.sbuf_tensor(f"s1_z{i}", [128, DI], BF16)) for i in range(2)]
            sxb = [ies.enter_context(nc.sbuf_tensor(f"s1_s{i}", [128, DI], BF16)) for i in range(2)]
            bcb = [ies.enter_context(nc.sbuf_tensor(f"s1_b{i}", [128, 128], BF16)) for i in range(2)]
            dtb = [ies.enter_context(nc.sbuf_tensor(f"s1_t{i}", [128, 32], F32)) for i in range(2)]
            pz = [ies.enter_context(nc.psum_tensor(f"s1_p{i}", [128, 512], F32)) for i in range(8)]
            dW.inc(GP.dma_start(out=wz_sb[:], in_=cx.w_z.rearrange("(k p) n -> p k n", p=128)), 16)
            dW.inc(GP.dma_start(out=wx_sb[:], in_=cx.w_x.rearrange("(k p) n -> p k n", p=128)), 16)
            dW.inc(GP.dma_start(out=wb_sb[:], in_=cx.w_bcd.rearrange("(k p) n -> p k n", p=128)), 16)
            iter_v, iter_s = [], []
            for t in range(NT):
                bi = t % 2
                if t >= 2:
                    SY.wait_ge(vC.h, iter_v[t - 2])
                    SY.wait_ge(sC.h, iter_s[t - 2])
                dL.inc(SY.dma_start(
                    out=xb[bi][:],
                    in_=cx.xn_dm.rearrange("(k p) T2 -> p k T2", p=128)[:, :, t * 128:(t + 1) * 128]), 16)
                wL = dL.n
                TE.wait_ge(dL.h, wL)
                if t == 0:
                    TE.wait_ge(dW.h, dW.n)
                if t >= 2:
                    TE.wait_ge(sC.h, iter_s[t - 2])
                    TE.wait_ge(vC.h, iter_v[t - 2])
                for k in range(8):
                    for n in range(8):
                        src = wz_sb[:, k, (n % 4) * 512:(n % 4 + 1) * 512] if n < 4 else \
                              wx_sb[:, k, (n - 4) * 512:(n - 3) * 512]
                        tC.inc(TE.matmul(pz[n][:], xb[bi][:, k, :], src,
                                         start=(k == 0), stop=(k == 7)))
                wT = tC.n
                SC.wait_ge(tC.h, wT); VE.wait_ge(tC.h, wT)
                for n in range(4):
                    sC.inc(SC.activation(zb[bi][:, n * 512:(n + 1) * 512], pz[n][:], AF.Silu))
                for n in range(4):
                    vC.inc(VE.tensor_copy(sxb[bi][:, n * 512:(n + 1) * 512], pz[4 + n][:]))
                wE_s, wE_v = sC.n, vC.n
                TE.wait_ge(sC.h, wE_s)
                for k in range(8):
                    tC.inc(TE.matmul(pz[0][:, 0:160], xb[bi][:, k, :], wb_sb[:, k, :],
                                     start=(k == 0), stop=(k == 7)))
                VE.wait_ge(tC.h, tC.n)
                vC.inc(VE.tensor_copy(bcb[bi][:], pz[0][:, 0:128]))
                vC.inc(VE.tensor_copy(dtb[bi][:], pz[0][:, 128:160]))
                wB = vC.n
                SY.wait_ge(sC.h, wE_s)
                dS.inc(SY.dma_start(out=cx.sz_tm[t * 128:(t + 1) * 128, :], in_=zb[bi][:]), 16)
                SY.wait_ge(vC.h, wE_v)
                dS.inc(SY.dma_start(out=cx.xs_tm[t * 128:(t + 1) * 128, :], in_=sxb[bi][:]), 16)
                SY.wait_ge(vC.h, wB)
                dS.inc(SY.dma_start(out=cx.bc_tm[t * 128:(t + 1) * 128, :], in_=bcb[bi][:]), 16)
                dS.inc(SY.dma_start(out=cx.dtr_tm[t * 128:(t + 1) * 128, :], in_=dtb[bi][:]), 16)
                iter_v.append(vC.n); iter_s.append(sC.n)
            SY.wait_ge(dS.h, dS.n)
            ies.close()
        blk.sync(emit)


S2DBG = False
S3ABL = ("h", "tp", "v", "qk")

def _s2(cx):
    nc = cx.nc
    SY, TE, VE, SC, GP = cx.SY, cx.TE, cx.VE, cx.SC, cx.GP
    dbgt = {}
    if S2DBG:
        dbgt["dbg_y"] = nc.dram_tensor("dbg_y", [128, GRP * DI], BF16,
                                       kind="ExternalOutput").ap()
        for nm, sh_, dt_ in [("dbg_Wt", [128, 128], BF16), ("dbg_E", [128, 128], F32),
                             ("dbg_Gm", [128, 128], BF16), ("dbg_bnm", [64, 128], BF16),
                             ("dbg_cnm2", [64, 128], BF16), ("dbg_rot", [128, GRP * 128], BF16)]:
            dbgt[nm] = nc.dram_tensor(nm, sh_, dt_, kind="ExternalOutput").ap()
    with nc.Block("s2") as blk:
        def emit(_):
            ies = ExitStack()
            dW = CT(nc, ies, "s2_dW"); dL = CT(nc, ies, "s2_dL"); dS = CT(nc, ies, "s2_dS")
            tC = CT(nc, ies, "s2_tC"); vC = CT(nc, ies, "s2_vC"); sC = CT(nc, ies, "s2_sC")
            sbt = lambda n_, s, d: ies.enter_context(nc.sbuf_tensor(n_, list(s), d))
            wout_sb = sbt("s2_wo", [128, 16, D], BF16)
            dtr4 = sbt("s2_dtr", [128, GRP, 32], F32)
            bc4 = sbt("s2_bc", [128, GRP, 128], BF16)
            xs4 = sbt("s2_xs", [128, GRP, DI], BF16)
            sz4 = sbt("s2_sz", [128, GRP, DI], BF16)
            dt4 = sbt("s2_dt", [128, GRP, 32], F32)
            tmpa = sbt("s2_tmpa", [128, GRP, 32], F32)
            cumT = sbt("s2_cumT", [128, GRP * 32], F32)
            gcum = sbt("s2_gcum", [128, GRP * 32], F32)
            biasT = sbt("s2_biasT", [128, GRP * 32], F32)
            lamT = sbt("s2_lamT", [128, GRP * 32], F32)
            gdtT = sbt("s2_gdtT", [128, GRP * 32], F32)
            lndt = sbt("s2_lndt", [128, GRP * 32], F32)
            qa = sbt("s2_qa", [128, 2 * 512], F32)
            qi = sbt("s2_qi", [128, 2 * 512], I32)
            qf = sbt("s2_qf", [128, 2 * 512], F32)
            fr = sbt("s2_fr", [128, 2 * 512], F32)
            sdup = sbt("s2_sd", [128, GRP * 1024], BF16)
            cdup = sbt("s2_cd", [128, GRP * 1024], BF16)
            cnm = sbt("s2_cnm", [32, GRP * 128], F32)
            csrow = sbt("s2_cs", [1, 128], F32)
            offr = sbt("s2_off", [1, 160], F32)
            acdr = sbt("s2_acd", [1, 128], F32)
            decr = sbt("s2_dec", [1, 128], F32)
            dec_bc = sbt("s2_decb", [64, 128], F32)
            acd_bc = sbt("s2_acdb", [128, 128], F32)
            H = [sbt(f"s2_H{i}", [64, DI], BF16) for i in range(2)]
            rot = [sbt(f"s2_rot{i}", [128, GRP, 128], BF16) for i in range(2)]
            rtmp = [sbt(f"s2_rtmp{i}", [128, GRP, 32], F32) for i in range(2)]
            bgdt = [sbt(f"s2_bg{i}", [128, GRP, 64], BF16) for i in range(2)]
            cltm = sbt("s2_cl", [128, GRP, 128], BF16)
            bnm = [sbt(f"s2_bn{i}", [64, 128], BF16) for i in range(4)]
            cnm2 = [sbt(f"s2_cm{i}", [64, 128], BF16) for i in range(4)]
            clnm = [sbt(f"s2_cn{i}", [64, 128], BF16) for i in range(8)]
            Etile = [sbt(f"s2_E{i}", [128, 128], F32) for i in range(4)]
            Gm = [sbt(f"s2_Gm{i}", [128, 128], BF16) for i in range(4)]
            Wt = [sbt(f"s2_W{i}", [128, 128], BF16) for i in range(4)]
            ybuf = sbt("s2_y", [128, GRP, DI], BF16)
            yst = sbt("s2_yst", [128, GRP, 4], F32)
            ydmb = sbt("s2_ydm", [128, 16, 128], BF16)
            mo = [sbt(f"s2_mo{i}", [128, D], F32) for i in range(2)]
            pst = lambda n_, s, d=F32: ies.enter_context(nc.psum_tensor(n_, list(s), d))
            p_tp = [pst(f"s2_ptp{i}", [128, 1024], BF16) for i in range(2)]
            p_R = [pst(f"s2_pr{i}", [128, 512]) for i in range(2)]
            p_G = [pst(f"s2_pg{i}", [128, 512]) for i in range(2)]
            p_Y = pst("s2_py", [128, 512])
            p_Sc = pst("s2_psc", [64, 512])
            p_sm = p_G[0]
            p_W = p_R
            dW.inc(GP.dma_start(out=wout_sb[:], in_=cx.w_out.rearrange("(k p) n -> p k n", p=128)), 16)
            VE.wait_ge(dW.h, dW.n)
            for k in range(16):
                vC.inc(VE.tensor_scalar(wout_sb[:, k, :], wout_sb[:, k, :],
                                        cx.P_nwc[:, k:k + 1], None, ALU.mult))
            wout_ready = vC.n
            mk = {"tp": [0, 0], "R": [0, 0], "G": [[0, 0], [0, 0]], "Y": 0, "Sc": 0,
                  "rot": [0, 0], "nm": [[0, 0], [0, 0]], "W": [[0, 0], [0, 0]],
                  "E": [[0, 0], [0, 0]], "mo": [0, 0], "Wr": [[0, 0], [0, 0]],
                  "W2": [0, 0], "cn": 0, "yev": [0, 0], "Elast": [0, 0], "Glast": [0, 0]}

            def phaseA(b, g, hinit):
                g0 = b * L + g * GRP * Q
                SY.wait_ge(vC.h, vC.n)
                SY.wait_ge(sC.h, sC.n)
                SY.wait_ge(tC.h, tC.n)
                dL.inc(SY.dma_start(out=dtr4[:], in_=cx.dtr_tm[g0:g0 + 512, :]
                                    .rearrange("(c p) h -> p c h", p=128)), 16)
                dL.inc(SY.dma_start(out=bc4[:], in_=cx.bc_tm[g0:g0 + 512, :]
                                    .rearrange("(c p) n -> p c n", p=128)), 16)
                dL.inc(SY.dma_start(out=xs4[:], in_=cx.xs_tm[g0:g0 + 512, :]
                                    .rearrange("(c p) n -> p c n", p=128)), 16)
                dL.inc(SY.dma_start(out=sz4[:], in_=cx.sz_tm[g0:g0 + 512, :]
                                    .rearrange("(c p) n -> p c n", p=128)), 16)
                wL = dL.n
                VE.wait_ge(dL.h, wL); SC.wait_ge(dL.h, wL)
                VE.wait_ge(vC.h, hinit)
                vC.inc(VE.tensor_tensor(dt4[:], dtr4[:],
                                        cx.P_dtb[:].unsqueeze(1).broadcast_to([128, GRP, 32]),
                                        ALU.add))
                VE.wait_ge(vC.h, vC.n); SC.wait_ge(vC.h, vC.n)
                sC.inc(SC.activation(tmpa[:], dt4[:], AF.Abs))
                SC.wait_ge(sC.h, sC.n)
                sC.inc(SC.activation(tmpa[:], tmpa[:], AF.Exp, scale=-1.0))
                SC.wait_ge(sC.h, sC.n)
                sC.inc(SC.activation(tmpa[:], tmpa[:], AF.Ln, bias=1.0))
                VE.wait_ge(sC.h, sC.n)
                vC.inc(VE.tensor_scalar(dt4[:], dt4[:], 0.0, None, ALU.max))
                VE.wait_ge(vC.h, vC.n)
                vC.inc(VE.tensor_tensor(dt4[:], dt4[:], tmpa[:], ALU.add))
                VE.wait_ge(vC.h, vC.n)
                wDT = vC.n
                SC.wait_ge(vC.h, wDT)
                sC.inc(SC.activation(lndt[:].rearrange("p (c h) -> p c h", h=32),
                                     dt4[:], AF.Ln))
                TE.wait_ge(vC.h, max(wDT, mk["Glast"][0], mk["W2"][0], mk["W2"][1]))
                for c in range(GRP):
                    tC.inc(TE.matmul(p_sm[:, c * 32:(c + 1) * 32], cx.P_cum[:],
                                     dt4[:, c, :], start=(c == 0), stop=(c == GRP - 1)))
                VE.wait_ge(tC.h, tC.n)
                vC.inc(VE.tensor_copy(cumT[:], p_sm[:, 0:128]))
                VE.wait_ge(vC.h, vC.n)
                TE.wait_ge(vC.h, vC.n)
                for c in range(GRP):
                    tC.inc(TE.matmul(p_sm[0:1, c * 32:(c + 1) * 32],
                                     cx.P_onescol[:], dt4[:, c, :],
                                     start=(c == 0), stop=(c == GRP - 1)))
                VE.wait_ge(tC.h, tC.n)
                vC.inc(VE.tensor_copy(csrow[:], p_sm[0:1, 0:128]))
                VE.wait_ge(vC.h, vC.n)
                vC.inc(VE.tensor_copy(offr[0:1, 0:32], offr[0:1, 128:160]))
                for c in range(1, GRP):
                    VE.wait_ge(vC.h, vC.n)
                    vC.inc(VE.tensor_tensor(offr[0:1, c * 32:(c + 1) * 32],
                                            offr[0:1, (c - 1) * 32:c * 32],
                                            csrow[0:1, (c - 1) * 32:c * 32], ALU.add))
                VE.wait_ge(vC.h, vC.n)
                vC.inc(VE.tensor_tensor(offr[0:1, 128:160], offr[0:1, 96:128],
                                        csrow[0:1, 96:128], ALU.add))
                vC.inc(VE.tensor_tensor(acdr[:].rearrange("o (c h) -> o c h", h=32),
                                        csrow[:].rearrange("o (c h) -> o c h", h=32),
                                        cx.P_rowA[0:1, 0:32].unsqueeze(1)
                                        .broadcast_to([1, GRP, 32]), ALU.mult))
                VE.wait_ge(vC.h, vC.n)
                wOFF = vC.n
                SC.wait_ge(vC.h, wOFF)
                sC.inc(SC.activation(decr[:], acdr[:], AF.Exp, scale=-1.0))
                TE.wait_ge(vC.h, wOFF)
                tC.inc(TE.matmul(p_sm[:, 0:128], cx.P_ones_f[:], offr[0:1, 0:128],
                                 start=True, stop=True))
                VE.wait_ge(tC.h, tC.n)
                vC.inc(VE.tensor_tensor(gcum[:], cumT[:], p_sm[:, 0:128], ALU.add))
                VE.wait_ge(vC.h, vC.n)
                TE.wait_ge(vC.h, vC.n)
                tC.inc(TE.matmul(p_sm[:, 0:128], cx.P_ones_f[:], acdr[:],
                                 start=True, stop=True))
                VE.wait_ge(tC.h, tC.n)
                vC.inc(VE.tensor_copy(acd_bc[:], p_sm[:, 0:128]))
                VE.wait_ge(vC.h, vC.n)
                TE.wait_ge(vC.h, vC.n); TE.wait_ge(sC.h, sC.n)
                tC.inc(TE.matmul(p_sm[0:64, 0:128], cx.P_ones_f[0:1, 0:64], decr[:],
                                 start=True, stop=True))
                VE.wait_ge(tC.h, tC.n)
                vC.inc(VE.tensor_copy(dec_bc[:], p_sm[0:64, 0:128]))
                VE.wait_ge(vC.h, vC.n)
                vC.inc(VE.tensor_tensor(biasT[:].rearrange("p (c h) -> p c h", h=32),
                                        cumT[:].rearrange("p (c h) -> p c h", h=32),
                                        cx.P_A[:, 0:32].unsqueeze(1)
                                        .broadcast_to([128, GRP, 32]), ALU.mult))
                VE.wait_ge(vC.h, vC.n); SC.wait_ge(vC.h, vC.n)
                sC.inc(SC.activation(lamT[:], biasT[:], AF.Exp, scale=-1.0))
                VE.wait_ge(sC.h, sC.n)
                vC.inc(VE.tensor_tensor(biasT[:], biasT[:], lndt[:], ALU.add))
                VE.wait_ge(vC.h, vC.n)
                vC.inc(VE.tensor_tensor(gdtT[:], biasT[:], acd_bc[:], ALU.subtract))
                VE.wait_ge(vC.h, vC.n); SC.wait_ge(vC.h, vC.n)
                sC.inc(SC.activation(gdtT[:], gdtT[:], AF.Exp))
                for half in range(2):
                    cc = (2 * half, 2 * half + 1)
                    SC_mark = sC.n
                    VE.wait_ge(sC.h, SC_mark)
                    for ci, c in enumerate(cc):
                        vC.inc(VE.tensor_tensor(
                            qa[:, ci * 512:(ci + 1) * 512].rearrange("p (h j) -> p h j", j=16),
                            gcum[:, c * 32:(c + 1) * 32].unsqueeze(2).broadcast_to([128, 32, 16]),
                            cx.P_if2pi[:].rearrange("p (h j) -> p h j", j=16), ALU.mult))
                    VE.wait_ge(vC.h, vC.n)
                    vC.inc(VE.tensor_copy(qi[:], qa[:]))
                    VE.wait_ge(vC.h, vC.n)
                    vC.inc(VE.tensor_copy(qf[:], qi[:]))
                    VE.wait_ge(vC.h, vC.n)
                    vC.inc(VE.tensor_tensor(fr[:], qa[:], qf[:], ALU.subtract))
                    VE.wait_ge(vC.h, vC.n)
                    SC.wait_ge(vC.h, vC.n)
                    for ci, c in enumerate(cc):
                        for dup in range(2):
                            sC.inc(SC.activation(
                                sdup[:, c * 1024:(c + 1) * 1024]
                                    .rearrange("p (h d j) -> p h d j", d=2, j=16)[:, :, dup, :],
                                fr[:, ci * 512:(ci + 1) * 512].rearrange("p (h j) -> p h j", j=16),
                                AF.Sin, scale=TWO_PI))
                    vC.inc(VE.tensor_scalar(qa[:], qa[:], 0.25, None, ALU.add))
                    VE.wait_ge(vC.h, vC.n)
                    vC.inc(VE.tensor_copy(qi[:], qa[:]))
                    VE.wait_ge(vC.h, vC.n)
                    vC.inc(VE.tensor_copy(qf[:], qi[:]))
                    VE.wait_ge(vC.h, vC.n)
                    VE.wait_ge(sC.h, sC.n)
                    vC.inc(VE.tensor_tensor(fr[:], qa[:], qf[:], ALU.subtract))
                    VE.wait_ge(vC.h, vC.n)
                    SC.wait_ge(vC.h, vC.n)
                    for ci, c in enumerate(cc):
                        for dup in range(2):
                            sC.inc(SC.activation(
                                cdup[:, c * 1024:(c + 1) * 1024]
                                    .rearrange("p (h d j) -> p h d j", d=2, j=16)[:, :, dup, :],
                                fr[:, ci * 512:(ci + 1) * 512].rearrange("p (h j) -> p h j", j=16),
                                AF.Sin, scale=TWO_PI))
                TE.wait_ge(vC.h, max(vC.n, mk["W2"][0], mk["W2"][1]))
                TE.wait_ge(sC.h, max(mk["E"][0][0], mk["E"][0][1]))
                for c in range(GRP):
                    tC.inc(TE.matmul(p_R[0][0:32, c * 128:(c + 1) * 128],
                                     cumT[:, c * 32:(c + 1) * 32], cx.P_idt32[:],
                                     is_transpose=True, start=(c == 0), stop=(c == GRP - 1)))
                VE.wait_ge(tC.h, tC.n)
                vC.inc(VE.tensor_copy(cnm[:], p_R[0][0:32, 0:512]))
                mk["cn"] = vC.n
                mk["R"] = [vC.n, mk["R"][1]]
                return vC.n, sC.n

            def rot_head(h, wA_v, wA_s):
                hb = h % 2
                VE.wait_ge(tC.h, mk["rot"][hb])
                VE.wait_ge(sC.h, wA_s)
                co2 = cdup[:].rearrange("p (c h2 x) -> p c h2 x", h2=32, x=32)[:, :, h, :] \
                    .rearrange("p c (g j) -> p c g j", g=2)
                si2 = sdup[:].rearrange("p (c h2 x) -> p c h2 x", h2=32, x=32)[:, :, h, :] \
                    .rearrange("p c (g j) -> p c g j", g=2)
                bview = bc4[:].rearrange("p c (g n) -> p c g n", g=2)
                rview = rot[hb][:].rearrange("p c (g n) -> p c g n", g=2)
                bev, bod = bview[:, :, :, 0:32:2], bview[:, :, :, 1:32:2]
                rev, rod = rview[:, :, :, 0:32:2], rview[:, :, :, 1:32:2]
                rt = rtmp[hb][:].rearrange("p c (g j) -> p c g j", g=2)
                vC.inc(VE.tensor_copy(rview[:, :, :, 32:64], bview[:, :, :, 32:64]))
                vC.inc(VE.tensor_tensor(rev, bev, co2, ALU.mult))
                vC.inc(VE.tensor_tensor(rt, bod, si2, ALU.mult))
                VE.wait_ge(vC.h, vC.n)
                vC.inc(VE.tensor_tensor(rev, rev, rt, ALU.add))
                vC.inc(VE.tensor_tensor(rod, bod, co2, ALU.mult))
                vC.inc(VE.tensor_tensor(rt, bev, si2, ALU.mult))
                VE.wait_ge(vC.h, vC.n)
                vC.inc(VE.tensor_tensor(rod, rod, rt, ALU.subtract))
                VE.wait_ge(vC.h, vC.n)
                vC.inc(VE.tensor_tensor(
                    bgdt[hb][:], rot[hb][:, :, 0:64],
                    gdtT[:].rearrange("p (c h2) -> p c h2", h2=32)[:, :, h:h + 1]
                        .broadcast_to([128, GRP, 64]), ALU.mult))
                vC.inc(VE.tensor_tensor(
                    cltm[:, :, hb * 64:(hb + 1) * 64], rot[hb][:, :, 64:128],
                    lamT[:].rearrange("p (c h2) -> p c h2", h2=32)[:, :, h:h + 1]
                        .broadcast_to([128, GRP, 64]), ALU.mult))
                VE.wait_ge(vC.h, vC.n)
                return vC.n

            def pair_chunks(h0, b, wrot, wA_v):
                rdh = lambda c: (c % 2) ^ 1
                def stage_T(c):
                    cs = c % 2
                    for hb in range(2):
                        ix = hb * 2 + cs
                        TE.wait_ge(vC.h, max(mk["nm"][hb][cs], mk["tp"][cs], wrot[hb]))
                        tC.inc(TE.matmul(p_tp[cs][:, 0:128], rot[hb][:, c, :], cx.P_idt16[:],
                                         is_transpose=True, start=True, stop=True))
                        wTP = tC.n
                        VE.wait_ge(tC.h, wTP)
                        vC.inc(VE.tensor_copy(bnm[ix][:], p_tp[cs][0:64, 0:128]))
                        vC.inc(VE.tensor_copy(cnm2[ix][:], p_tp[cs][64:128, 0:128]))
                        mk["tp"][cs] = vC.n
                        mk["nm"][hb][cs] = vC.n
                    TE.wait_ge(vC.h, mk["tp"][cs])
                    tC.inc(TE.matmul(p_tp[cs][:, 0:128], cltm[:, c, :], cx.P_idt16[:],
                                     is_transpose=True, start=True, stop=True))
                    wTP = tC.n
                    VE.wait_ge(tC.h, wTP)
                    vC.inc(VE.tensor_copy(clnm[0 * 4 + c][:], p_tp[cs][0:64, 0:128]))
                    vC.inc(VE.tensor_copy(clnm[1 * 4 + c][:], p_tp[cs][64:128, 0:128]))
                    mk["tp"][cs] = vC.n

                def stage_RE(c):
                    cs = c % 2
                    for hb in range(2):
                        h = h0 + hb
                        TE.wait_ge(sC.h, mk["Elast"][hb])
                        TE.wait_ge(vC.h, max(mk["R"][hb], mk["cn"] if hb else wA_v))
                        tC.inc(TE.matmul(p_R[hb][:, 0:128],
                                         cx.P_onehot[:, h * 128:(h + 1) * 128],
                                         cnm[:, c * 128:(c + 1) * 128],
                                         start=True, stop=True))
                        wR = tC.n
                        SC.wait_ge(tC.h, wR)
                        SC.wait_ge(vC.h, mk["W"][hb][cs])
                        sC.inc(SC.activation(Etile[hb * 2 + cs][:], p_R[hb][:, 0:128], AF.Exp,
                                             scale=cx.P_negA[:, h:h + 1],
                                             bias=biasT[:, c * 32 + h:c * 32 + h + 1]))
                        mk["E"][hb][cs] = sC.n
                        mk["Elast"][hb] = sC.n

                def stage_GW(c):
                    cs = c % 2
                    for hb in range(2):
                        ix = hb * 2 + cs
                        TE.wait_ge(vC.h, max(mk["nm"][hb][cs], mk["Glast"][hb]))
                        tC.inc(TE.matmul(p_G[hb][:, 0:128], bnm[ix][:],
                                         cnm2[ix][:], start=True, stop=True))
                        wG = tC.n
                        VE.wait_ge(tC.h, max(wG, mk["Wr"][hb][cs]))
                        vC.inc(VE.scalar_tensor_tensor(Gm[ix][:], p_G[hb][:, 0:128], 1.0,
                                                       cx.P_wmask[:], ALU.mult, ALU.mult))
                        mk["G"][hb][cs] = vC.n
                        mk["Glast"][hb] = vC.n
                        VE.wait_ge(vC.h, vC.n); VE.wait_ge(sC.h, mk["E"][hb][cs])
                        vC.inc(VE.scalar_tensor_tensor(Wt[ix][:], Etile[ix][:], BIG,
                                                       Gm[ix][:], ALU.min, ALU.mult))
                        mk["W"][hb][cs] = vC.n

                def stage_Y(c):
                    cs = c % 2
                    rdH, wrH = (c % 2) ^ 1, c % 2
                    for hb in range(2):
                        h = h0 + hb
                        ix = hb * 2 + cs
                        TE.wait_ge(vC.h, mk["W"][hb][cs])
                        if c == 0 and hb == 0:
                            TE.wait_ge(vC.h, mk["Y"])
                        base = (hb * 4 + c) * 64
                        tC.inc(TE.matmul(p_Y[:, base:base + 64], Wt[ix][:],
                                         xs4[:, c, h * 64:(h + 1) * 64],
                                         start=(c == 0 and hb == 0), stop=False))
                        tC.inc(TE.matmul(p_Y[:, base:base + 64], clnm[hb * 4 + c][:],
                                         H[rdH][:, h * 64:(h + 1) * 64],
                                         start=False, stop=(c == GRP - 1 and hb == 1)))
                        mk["Wr"][hb][cs] = tC.n
                        TE.wait_ge(vC.h, mk["Sc"])
                        tC.inc(TE.matmul(p_Sc[:, 0:64], bgdt[hb][:, c, :],
                                         xs4[:, c, h * 64:(h + 1) * 64],
                                         start=True, stop=True))
                        wYS = tC.n
                        VE.wait_ge(tC.h, wYS)
                        vC.inc(VE.scalar_tensor_tensor(
                            H[wrH][:, h * 64:(h + 1) * 64],
                            H[rdH][:, h * 64:(h + 1) * 64],
                            dec_bc[:, c * 32 + h:c * 32 + h + 1],
                            p_Sc[:, 0:64], ALU.mult, ALU.add))
                        mk["Sc"] = vC.n

                stage_T(0)
                stage_T(1)
                stage_RE(0)
                for c in range(GRP):
                    stage_GW(c)
                    if c + 2 < GRP:
                        stage_T(c + 2)
                    if c + 1 < GRP:
                        stage_RE(c + 1)
                    stage_Y(c)
                VE.wait_ge(tC.h, tC.n)
                for hb in range(2):
                    h = h0 + hb
                    vC.inc(VE.scalar_tensor_tensor(
                        ybuf[:, :, h * 64:(h + 1) * 64],
                        xs4[:, :, h * 64:(h + 1) * 64],
                        cx.P_dsk[:, h:h + 1],
                        p_Y[:, hb * 256:hb * 256 + 256].rearrange("p (c x) -> p c x", x=64),
                        ALU.mult, ALU.add))
                mk["Y"] = vC.n
                mk["rot"][0] = tC.n
                mk["rot"][1] = tC.n

            def phaseC(b, g, first):
                g0 = b * L + g * GRP * Q
                if S2DBG and b == 0 and g == 0:
                    SY.wait_ge(vC.h, vC.n)
                    SY.wait_ge(tC.h, tC.n)
                    dS.inc(SY.dma_start(out=dbgt["dbg_y"][:],
                                        in_=ybuf[:].rearrange("p a b2 -> p (a b2)")), 16)
                    VE.wait_ge(dS.h, dS.n)
                VE.wait_ge(vC.h, vC.n)
                vC.inc(VE.tensor_tensor(ybuf[:], ybuf[:], sz4[:], ALU.mult))
                VE.wait_ge(vC.h, vC.n)
                SC.wait_ge(vC.h, vC.n)
                for c in range(GRP):
                    jdst = sdup if c < 2 else cdup
                    sC.inc(SC.activation(jdst[:, (c % 2) * 2048:(c % 2) * 2048 + 2048],
                                         ybuf[:, c, :], AF.Square,
                                         accum_out=yst[:, c, 0:1]))
                SC.wait_ge(sC.h, sC.n)
                for c in range(GRP):
                    sC.inc(SC.activation(yst[:, c, 1:2], yst[:, c, 0:1], AF.Ln,
                                         scale=1.0 / DI, bias=EPS_LN))
                SC.wait_ge(sC.h, sC.n)
                for c in range(GRP):
                    sC.inc(SC.activation(yst[:, c, 2:3], yst[:, c, 1:2], AF.Exp, scale=-0.5))
                VE.wait_ge(sC.h, sC.n)
                vC.inc(VE.tensor_tensor(ybuf[:], ybuf[:],
                                        yst[:, :, 2:3].broadcast_to([128, GRP, DI]),
                                        ALU.mult))
                VE.wait_ge(vC.h, vC.n)
                wYGN = vC.n
                if first:
                    TE.wait_ge(vC.h, wout_ready)
                for c in range(GRP):
                    TE.wait_ge(vC.h, wYGN)
                    for k in range(16):
                        ks = k % 2
                        TE.wait_ge(vC.h, mk["tp"][ks])
                        tC.inc(TE.matmul(p_tp[ks][:, 0:128], ybuf[:, c, k * 128:(k + 1) * 128],
                                         cx.P_idt16[:], is_transpose=True,
                                         start=True, stop=True))
                        VE.wait_ge(tC.h, tC.n)
                        vC.inc(VE.tensor_copy(ydmb[:, k, :], p_tp[ks][:, 0:128]))
                        mk["tp"][ks] = vC.n
                    wYD = vC.n
                    TE.wait_ge(vC.h, wYD)
                    TE.wait_ge(vC.h, max(mk["W2"][0], mk["W2"][1]))
                    TE.wait_ge(sC.h, max(mk["E"][0][0], mk["E"][0][1], mk["E"][1][0], mk["E"][1][1]))
                    for k in range(16):
                        for n in range(2):
                            tC.inc(TE.matmul(p_W[n][:, 0:512], ydmb[:, k, :],
                                             wout_sb[:, k, n * 512:(n + 1) * 512],
                                             start=(k == 0), stop=(k == 15)))
                    VE.wait_ge(tC.h, tC.n)
                    VE.wait_ge(dS.h, mk["mo"][c % 2])
                    vC.inc(VE.tensor_copy(mo[c % 2][:, 0:512], p_W[0][:, 0:512]))
                    vC.inc(VE.tensor_copy(mo[c % 2][:, 512:1024], p_W[1][:, 0:512]))
                    mk["W2"] = [vC.n, vC.n]
                    mk["R"] = [vC.n, vC.n]
                    r0 = g0 + c * 128
                    SY.wait_ge(vC.h, vC.n)
                    dS.inc(SY.dma_start(out=cx.mamba_tm[r0:r0 + 128, :], in_=mo[c % 2][:]), 16)
                    mk["mo"][c % 2] = dS.n

            dbg_g = [0]

            for b in range(BPC):
                VE.wait_ge(tC.h, tC.n)
                vC.inc(VE.memset(H[0][:], 0.0))
                vC.inc(VE.memset(H[1][:], 0.0))
                vC.inc(VE.memset(offr[:], 0.0))
                hinit = vC.n
                for g in range(NG):
                    dbg_g[0] = g
                    wA_v, wA_s = phaseA(b, g, hinit)
                    for h0 in range(0, NH, 2):
                        w0 = rot_head(h0, wA_v, wA_s)
                        w1 = rot_head(h0 + 1, wA_v, wA_s)
                        pair_chunks(h0, b, (w0, w1), wA_v)
                    phaseC(b, g, b == 0 and g == 0)
            SY.wait_ge(dS.h, dS.n)
            ies.close()
        blk.sync(emit)


def _s3(cx):
    nc = cx.nc
    SY, TE, VE, SC, GP = cx.SY, cx.TE, cx.VE, cx.SC, cx.GP
    with nc.Block("s3") as blk:
        def emit(_):
            ies = ExitStack()
            dW = CT(nc, ies, "s3_dW"); dL = CT(nc, ies, "s3_dL"); dS = CT(nc, ies, "s3_dS")
            tC = CT(nc, ies, "s3_tC"); vC = CT(nc, ies, "s3_vC"); sC = CT(nc, ies, "s3_sC")
            wq_sb = ies.enter_context(nc.sbuf_tensor("s3_wq", [128, 8, D], BF16))
            wk_sb = ies.enter_context(nc.sbuf_tensor("s3_wk", [128, 8, D], BF16))
            wv_sb = ies.enter_context(nc.sbuf_tensor("s3_wv", [128, 8, D], BF16))
            xb = [ies.enter_context(nc.sbuf_tensor(f"s3_x{i}", [128, D], BF16)) for i in range(2)]
            st = ies.enter_context(nc.sbuf_tensor("s3_st", [128, 40], F32))
            h16 = [ies.enter_context(nc.sbuf_tensor(f"s3_h{i}", [128, D], BF16)) for i in range(2)]
            hdm = [ies.enter_context(nc.sbuf_tensor(f"s3_hd{i}", [128, 8, 128], BF16)) for i in range(2)]
            sq = ies.enter_context(nc.sbuf_tensor("s3_sq", [128, D], F32))
            junk = ies.enter_context(nc.sbuf_tensor("s3_j", [128, D], F32))
            qtm = ies.enter_context(nc.sbuf_tensor("s3_qtm", [128, D], BF16))
            qdmb = ies.enter_context(nc.sbuf_tensor("s3_qd", [128, 8, 128], BF16))
            vtm = [ies.enter_context(nc.sbuf_tensor(f"s3_v{i}", [128, D], BF16)) for i in range(2)]
            ptp = [ies.enter_context(nc.psum_tensor(f"s3_p{i}", [128, 1024], BF16)) for i in range(2)]
            pmm = [ies.enter_context(nc.psum_tensor(f"s3_m{i}", [128, 512], F32)) for i in range(2)]
            dW.inc(GP.dma_start(out=wq_sb[:], in_=cx.wq_t.rearrange("(k p) n -> p k n", p=128)), 16)
            dW.inc(GP.dma_start(out=wk_sb[:], in_=cx.wk_t.rearrange("(k p) n -> p k n", p=128)), 16)
            dW.inc(GP.dma_start(out=wv_sb[:], in_=cx.wv_t.rearrange("(k p) n -> p k n", p=128)), 16)
            iter_v, iter_s = [], []
            mk_tp = [0, 0]
            for t in range(NT):
                bi = t % 2
                if t >= 2:
                    SY.wait_ge(vC.h, iter_v[t - 2])
                    SY.wait_ge(sC.h, iter_s[t - 2])
                dL.inc(SY.dma_start(out=xb[bi][:], in_=cx.xn_tm[t * 128:(t + 1) * 128, :]), 16)
                wL = dL.n
                if "h" not in S3ABL:
                    iter_v.append(vC.n); iter_s.append(sC.n)
                    continue
                SC.wait_ge(dL.h, wL)
                sC.inc(SC.activation(junk[:], xb[bi][:], AF.Square, accum_out=st[:, 0:1]))
                SC.wait_ge(sC.h, sC.n)
                sC.inc(SC.activation(st[:, 1:2], st[:, 0:1], AF.Ln, scale=1.0 / D, bias=EPS_RMS))
                SC.wait_ge(sC.h, sC.n)
                sC.inc(SC.activation(st[:, 2:3], st[:, 1:2], AF.Exp, scale=-0.5))
                VE.wait_ge(sC.h, sC.n); VE.wait_ge(dL.h, wL)
                vC.inc(VE.tensor_scalar(h16[bi][:], xb[bi][:], st[:, 2:3], None, ALU.mult))
                VE.wait_ge(vC.h, vC.n)
                vC.inc(VE.tensor_tensor(h16[bi][:], h16[bi][:], cx.P_n1t[:], ALU.mult))
                VE.wait_ge(vC.h, vC.n)
                wH = vC.n
                if "tp" not in S3ABL:
                    iter_v.append(vC.n); iter_s.append(sC.n)
                    continue
                TE.wait_ge(vC.h, wH)
                if t == 0:
                    TE.wait_ge(dW.h, dW.n)
                for k in range(8):
                    ks = k % 2
                    TE.wait_ge(vC.h, mk_tp[ks])
                    tC.inc(TE.matmul(ptp[ks][:, 0:128], h16[bi][:, k * 128:(k + 1) * 128],
                                     cx.P_idt16[:], is_transpose=True, start=True, stop=True))
                    VE.wait_ge(tC.h, tC.n)
                    vC.inc(VE.tensor_copy(hdm[bi][:, k, :], ptp[ks][:, 0:128]))
                    mk_tp[ks] = vC.n
                wHD = vC.n
                if "v" not in S3ABL:
                    iter_v.append(vC.n); iter_s.append(sC.n)
                    continue
                TE.wait_ge(vC.h, wHD)
                if t >= 2:
                    TE.wait_ge(vC.h, iter_v[t - 2])
                for n in range(2):
                    for k in range(8):
                        tC.inc(TE.matmul(pmm[n][:], hdm[bi][:, k, :],
                                         wv_sb[:, k, n * 512:(n + 1) * 512],
                                         start=(k == 0), stop=(k == 7)))
                VE.wait_ge(tC.h, tC.n)
                vC.inc(VE.tensor_copy(vtm[bi][:, 0:512], pmm[0][:]))
                vC.inc(VE.tensor_copy(vtm[bi][:, 512:1024], pmm[1][:]))
                SY.wait_ge(vC.h, vC.n)
                dS.inc(SY.dma_start(out=cx.v_tm[t * 128:(t + 1) * 128, :], in_=vtm[bi][:]), 16)
                if "qk" not in S3ABL:
                    iter_v.append(vC.n); iter_s.append(sC.n)
                    continue
                for wsb, ntile, dst in ((wq_sb, cx.P_qnt, cx.q_dm), (wk_sb, cx.P_knt, cx.k_dm)):
                    TE.wait_ge(vC.h, vC.n)
                    for n in range(2):
                        for k in range(8):
                            tC.inc(TE.matmul(pmm[n][:], hdm[bi][:, k, :],
                                             wsb[:, k, n * 512:(n + 1) * 512],
                                             start=(k == 0), stop=(k == 7)))
                    wQ = tC.n
                    SC.wait_ge(tC.h, wQ)
                    sC.inc(SC.activation(sq[:, 0:512], pmm[0][:], AF.Square))
                    sC.inc(SC.activation(sq[:, 512:1024], pmm[1][:], AF.Square))
                    VE.wait_ge(sC.h, sC.n)
                    for hh in range(16):
                        vC.inc(VE.reduce_sum(st[:, 8 + hh:9 + hh],
                                             sq[:, hh * 64:(hh + 1) * 64], axis=AX.X))
                    VE.wait_ge(vC.h, vC.n)
                    SC.wait_ge(vC.h, vC.n)
                    sC.inc(SC.activation(st[:, 24:40], st[:, 8:24], AF.Ln,
                                         scale=1.0 / HD, bias=EPS_RMS))
                    SC.wait_ge(sC.h, sC.n)
                    sC.inc(SC.activation(st[:, 24:40], st[:, 24:40], AF.Exp, scale=-0.5))
                    VE.wait_ge(sC.h, sC.n)
                    vC.inc(VE.tensor_tensor(
                        qtm[:, 0:512].rearrange("p (h x) -> p h x", x=64),
                        pmm[0][:].rearrange("p (h x) -> p h x", x=64),
                        st[:, 24:32].unsqueeze(2).broadcast_to([128, 8, 64]), ALU.mult))
                    vC.inc(VE.tensor_tensor(
                        qtm[:, 512:1024].rearrange("p (h x) -> p h x", x=64),
                        pmm[1][:].rearrange("p (h x) -> p h x", x=64),
                        st[:, 32:40].unsqueeze(2).broadcast_to([128, 8, 64]), ALU.mult))
                    VE.wait_ge(vC.h, vC.n)
                    vC.inc(VE.tensor_tensor(qtm[:], qtm[:], ntile[:], ALU.mult))
                    VE.wait_ge(vC.h, vC.n)
                    wQT = vC.n
                    TE.wait_ge(vC.h, wQT)
                    for k in range(8):
                        ks = k % 2
                        TE.wait_ge(vC.h, mk_tp[ks])
                        tC.inc(TE.matmul(ptp[ks][:, 0:128], qtm[:, k * 128:(k + 1) * 128],
                                         cx.P_idt16[:], is_transpose=True, start=True, stop=True))
                        VE.wait_ge(tC.h, tC.n)
                        vC.inc(VE.tensor_copy(qdmb[:, k, :], ptp[ks][:, 0:128]))
                        mk_tp[ks] = vC.n
                    SY.wait_ge(vC.h, vC.n)
                    dS.inc(SY.dma_start(
                        out=dst.rearrange("(k p) T2 -> p k T2", p=128)[:, :, t * 128:(t + 1) * 128],
                        in_=qdmb[:]), 16)
                    SY.wait_ge(dS.h, dS.n)
                iter_v.append(vC.n); iter_s.append(sC.n)
            SY.wait_ge(dS.h, dS.n)
            ies.close()
        blk.sync(emit)


def _s4(cx):
    nc = cx.nc
    SY, TE, VE, SC = cx.SY, cx.TE, cx.VE, cx.SC
    with nc.Block("s4") as blk:
        def emit(_):
            ies = ExitStack()
            dL = CT(nc, ies, "s4_dL"); dS = CT(nc, ies, "s4_dS")
            tC = CT(nc, ies, "s4_tC"); vC = CT(nc, ies, "s4_vC"); sC = CT(nc, ies, "s4_sC")
            NS = 4
            qb_ = [ies.enter_context(nc.sbuf_tensor(f"s4_q{i}", [64, L], BF16)) for i in range(2)]
            kb_ = [ies.enter_context(nc.sbuf_tensor(f"s4_k{i}", [64, L], BF16)) for i in range(2)]
            vb_ = [ies.enter_context(nc.sbuf_tensor(f"s4_v{i}", [128, 16, 65], BF16)) for i in range(2)]
            pb_ = [ies.enter_context(nc.sbuf_tensor(f"s4_pb{i}", [128, 512], BF16)) for i in range(NS)]
            o_un = [ies.enter_context(nc.sbuf_tensor(f"s4_ou{i}", [65, 4 * 512], F32)) for i in range(2)]
            rcp = [ies.enter_context(nc.sbuf_tensor(f"s4_rc{i}", [1, 4 * 512], F32)) for i in range(2)]
            rbc = [ies.enter_context(nc.sbuf_tensor(f"s4_rb{i}", [64, 512], F32)) for i in range(2)]
            ob = [ies.enter_context(nc.sbuf_tensor(f"s4_ob{i}", [64, 512], BF16)) for i in range(2)]
            pS = [ies.enter_context(nc.psum_tensor(f"s4_s{i}", [128, 512], F32)) for i in range(NS)]
            pO = [ies.enter_context(nc.psum_tensor(f"s4_o{i}", [65, 512], F32)) for i in range(2)]
            pB = ies.enter_context(nc.psum_tensor("s4_b", [64, 512], F32))
            vC.inc(VE.memset(vb_[0][:, :, 64:65], 1.0))
            vC.inc(VE.memset(vb_[1][:, :, 64:65], 1.0))
            sh = {"ks": 0, "mk_S": [0] * NS, "mk_B": 0, "mk_pb": [0] * NS,
                  "mk_O": [0, 0], "mk_ou": [0, 0], "mk_ob": [0, 0], "done": {}}

            def head_job(it):
                b, h = divmod(it, HEADS)
                bi = it % 2
                if it >= 2:
                    SY.wait_ge(tC.h, sh["done"].get(it - 2, 0))
                dL.inc(SY.dma_start(out=qb_[bi][:],
                                    in_=cx.q_dm[h * 64:(h + 1) * 64, b * L:(b + 1) * L]), 16)
                dL.inc(SY.dma_start(out=kb_[bi][:],
                                    in_=cx.k_dm[h * 64:(h + 1) * 64, b * L:(b + 1) * L]), 16)
                dL.inc(SY.dma_start(
                    out=vb_[bi][:, :, 0:64],
                    in_=cx.v_tm[b * L:(b + 1) * L, :]
                        .rearrange("(kt p) (h2 j) -> p kt h2 j", p=128, j=64)[:, :, h, :]), 16)
                wL = dL.n
                first = True
                kslot = [0]
                for qblk in range(4):
                    nkt = 4 * qblk + 4
                    exp_mark = {}
                    s_mark = {}
                    s_slot = {}
                    def emit_S(kt):
                        ks = 2 * bi + (kslot[0] % 2)
                        kslot[0] += 1
                        s_slot[kt] = ks
                        TE.wait_ge(sC.h, sh["mk_S"][ks])
                        if first:
                            TE.wait_ge(dL.h, wL)
                        tC.inc(TE.matmul(pS[ks][:], kb_[bi][:, kt * 128:(kt + 1) * 128],
                                         qb_[bi][:, qblk * 512:(qblk + 1) * 512],
                                         start=True, stop=True))
                        s_mark[kt] = tC.n
                    def emit_exp(kt):
                        ks = s_slot[kt]
                        wS = s_mark[kt]
                        dd = kt - 4 * qblk
                        if dd >= 0:
                            VE.wait_ge(tC.h, wS)
                            vC.inc(VE.tensor_tensor(pS[ks][:], pS[ks][:],
                                                    cx.P_amask[:, dd * 512:(dd + 1) * 512],
                                                    ALU.add))
                            SC.wait_ge(vC.h, vC.n)
                        SC.wait_ge(tC.h, max(wS, sh["mk_pb"][ks]))
                        sC.inc(SC.activation(pb_[ks][:], pS[ks][:], AF.Exp,
                                             scale=0.125, bias=cx.P_negc[:, 0:1]))
                        sh["mk_S"][ks] = sC.n
                        exp_mark[kt] = sC.n
                    emit_S(0)
                    if nkt > 1:
                        emit_S(1)
                    emit_exp(0)
                    first = False
                    for kt in range(nkt):
                        if kt + 2 < nkt:
                            emit_S(kt + 2)
                        if kt + 1 < nkt:
                            emit_exp(kt + 1)
                        TE.wait_ge(sC.h, exp_mark[kt])
                        if kt == 0:
                            TE.wait_ge(vC.h, sh["mk_O"][bi])
                        tC.inc(TE.matmul(pO[bi][:], vb_[bi][:, kt, :], pb_[s_slot[kt]][:],
                                         start=(kt == 0), stop=(kt == nkt - 1)))
                        sh["mk_pb"][s_slot[kt]] = tC.n
                        if kt % 4 == 3:
                            yield
                    wPV = tC.n
                    VE.wait_ge(tC.h, wPV)
                    if qblk == 0:
                        VE.wait_ge(vC.h, sh["mk_ou"][bi])
                    vC.inc(VE.tensor_copy(o_un[bi][:, qblk * 512:(qblk + 1) * 512], pO[bi][:]))
                    sh["mk_O"][bi] = vC.n
                    yield
                SC.wait_ge(vC.h, sh["mk_O"][bi])
                sC.inc(SC.activation(rcp[bi][0:1, :], o_un[bi][64:65, :], AF.Ln))
                SC.wait_ge(sC.h, sC.n)
                sC.inc(SC.activation(rcp[bi][0:1, :], rcp[bi][0:1, :], AF.Exp, scale=-1.0))
                wRC = sC.n
                for qblk in range(4):
                    TE.wait_ge(sC.h, wRC)
                    TE.wait_ge(vC.h, sh["mk_B"])
                    tC.inc(TE.matmul(pB[:], cx.P_ones_f[0:1, 0:64],
                                     rcp[bi][0:1, qblk * 512:(qblk + 1) * 512],
                                     start=True, stop=True))
                    VE.wait_ge(tC.h, tC.n)
                    vC.inc(VE.tensor_copy(rbc[bi][:], pB[:]))
                    sh["mk_B"] = vC.n
                    VE.wait_ge(vC.h, vC.n)
                    VE.wait_ge(dS.h, sh["mk_ob"][bi])
                    vC.inc(VE.scalar_tensor_tensor(ob[bi][:],
                                                   o_un[bi][0:64, qblk * 512:(qblk + 1) * 512],
                                                   1.0, rbc[bi][:], ALU.mult, ALU.mult))
                    SY.wait_ge(vC.h, vC.n)
                    dS.inc(SY.dma_start(
                        out=cx.ao_dm[h * 64:(h + 1) * 64,
                                     b * L + qblk * 512:b * L + (qblk + 1) * 512],
                        in_=ob[bi][:]), 16)
                    sh["mk_ob"][bi] = dS.n
                    yield
                sh["mk_ou"][bi] = vC.n
                sh["done"][it] = tC.n

            for it in range(0, BPC * HEADS, 2):
                g1, g2 = head_job(it), head_job(it + 1)
                alive = [g1, g2]
                while alive:
                    for gjob in list(alive):
                        try:
                            next(gjob)
                        except StopIteration:
                            alive.remove(gjob)
            SY.wait_ge(dS.h, dS.n)
            ies.close()
        blk.sync(emit)


def _s4b(cx):
    nc = cx.nc
    SY, TE, VE, GP = cx.SY, cx.TE, cx.VE, cx.GP
    with nc.Block("s4b") as blk:
        def emit(_):
            ies = ExitStack()
            dW = CT(nc, ies, "s4b_dW"); dL = CT(nc, ies, "s4b_dL"); dS = CT(nc, ies, "s4b_dS")
            tC = CT(nc, ies, "s4b_tC"); vC = CT(nc, ies, "s4b_vC")
            wo_sb = ies.enter_context(nc.sbuf_tensor("s4b_wo", [128, 8, D - MZ], BF16))
            ab = [ies.enter_context(nc.sbuf_tensor(f"s4b_a{i}", [128, 8, 128], BF16)) for i in range(2)]
            ob = [ies.enter_context(nc.sbuf_tensor(f"s4b_o{i}", [128, D - MZ], F32)) for i in range(2)]
            pm = [ies.enter_context(nc.psum_tensor(f"s4b_p{i}", [128, 512], F32)) for i in range(2)]
            dW.inc(GP.dma_start(out=wo_sb[:],
                                in_=cx.wo_t[:, MZ:].rearrange("(k p) n -> p k n", p=128)), 16)
            iter_v = []
            for t in range(NT):
                bi = t % 2
                if t >= 2:
                    SY.wait_ge(vC.h, iter_v[t - 2])
                dL.inc(SY.dma_start(
                    out=ab[bi][:],
                    in_=cx.ao_dm.rearrange("(k p) T2 -> p k T2", p=128)[:, :, t * 128:(t + 1) * 128]), 16)
                wL = dL.n
                TE.wait_ge(dL.h, wL)
                if t == 0:
                    TE.wait_ge(dW.h, dW.n)
                if t >= 2:
                    TE.wait_ge(vC.h, iter_v[t - 2])
                for n in range(2):
                    w0, w1 = n * 512, min((n + 1) * 512, D - MZ)
                    for k in range(8):
                        tC.inc(TE.matmul(pm[n][:, 0:w1 - w0], ab[bi][:, k, :],
                                         wo_sb[:, k, w0:w1], start=(k == 0), stop=(k == 7)))
                VE.wait_ge(tC.h, tC.n)
                vC.inc(VE.tensor_copy(ob[bi][:, 0:512], pm[0][:]))
                vC.inc(VE.tensor_copy(ob[bi][:, 512:768], pm[1][:, 0:256]))
                SY.wait_ge(vC.h, vC.n)
                dS.inc(SY.dma_start(out=cx.acc_tm[t * 128:(t + 1) * 128, :], in_=ob[bi][:]), 16)
                iter_v.append(vC.n)
            SY.wait_ge(dS.h, dS.n)
            ies.close()
        blk.sync(emit)


def _s5(cx):
    nc = cx.nc
    SY, TE, VE, SC, GP = cx.SY, cx.TE, cx.VE, cx.SC, cx.GP
    with nc.Block("s5") as blk:
        def emit(_):
            ies = ExitStack()
            dW = CT(nc, ies, "s5_dW"); dL = CT(nc, ies, "s5_dL"); dS = CT(nc, ies, "s5_dS")
            tC = CT(nc, ies, "s5_tC"); vC = CT(nc, ies, "s5_vC"); sC = CT(nc, ies, "s5_sC")
            gw_sb = ies.enter_context(nc.sbuf_tensor("s5_gw", [128, 8, 4], BF16))
            xnb = [ies.enter_context(nc.sbuf_tensor(f"s5_x{i}", [128, D], F32)) for i in range(2)]
            acb = [ies.enter_context(nc.sbuf_tensor(f"s5_a{i}", [128, D - MZ], F32)) for i in range(2)]
            x2 = ies.enter_context(nc.sbuf_tensor("s5_x2", [128, D], F32))
            st = ies.enter_context(nc.sbuf_tensor("s5_st", [128, 4], F32))
            junk = ies.enter_context(nc.sbuf_tensor("s5_j", [128, D], F32))
            xf16 = ies.enter_context(nc.sbuf_tensor("s5_xf", [128, D], BF16))
            xfd = [ies.enter_context(nc.sbuf_tensor(f"s5_d{i}", [128, 8, 128], BF16)) for i in range(2)]
            gate = ies.enter_context(nc.sbuf_tensor("s5_g", [128, 24], F32))
            gm = ies.enter_context(nc.sbuf_tensor("s5_gm", [128, 32], F32))
            wt4 = [ies.enter_context(nc.sbuf_tensor(f"s5_w{i}", [128, 4], F32)) for i in range(2)]
            ptp = [ies.enter_context(nc.psum_tensor(f"s5_p{i}", [128, 1024], BF16)) for i in range(2)]
            pg = ies.enter_context(nc.psum_tensor("s5_pg", [128, 512], F32))
            dW.inc(GP.dma_start(out=gw_sb[:], in_=cx.gw_t.rearrange("(k p) n -> p k n", p=128)), 16)
            iter_v, iter_s = [], []
            mk_tp = [0, 0]
            mk_pg = 0
            for t in range(NT):
                bi = t % 2
                if t >= 2:
                    SY.wait_ge(vC.h, iter_v[t - 2])
                    SY.wait_ge(sC.h, iter_s[t - 2])
                dL.inc(SY.dma_start(out=xnb[bi][:], in_=cx.xn_tm32[t * 128:(t + 1) * 128, :]), 16)
                dL.inc(SY.dma_start(out=acb[bi][:], in_=cx.acc_tm[t * 128:(t + 1) * 128, :]), 16)
                wL = dL.n
                VE.wait_ge(dL.h, wL)
                vC.inc(VE.tensor_copy(x2[:, 0:MZ], xnb[bi][:, 0:MZ]))
                vC.inc(VE.tensor_tensor(x2[:, MZ:], xnb[bi][:, MZ:], acb[bi][:], ALU.add))
                VE.wait_ge(vC.h, vC.n)
                SC.wait_ge(vC.h, vC.n)
                sC.inc(SC.activation(junk[:], x2[:], AF.Square, accum_out=st[:, 0:1]))
                SC.wait_ge(sC.h, sC.n)
                sC.inc(SC.activation(st[:, 1:2], st[:, 0:1], AF.Ln, scale=1.0 / D, bias=EPS_RMS))
                SC.wait_ge(sC.h, sC.n)
                sC.inc(SC.activation(st[:, 2:3], st[:, 1:2], AF.Exp, scale=-0.5))
                VE.wait_ge(sC.h, sC.n)
                vC.inc(VE.tensor_scalar(xf16[:], x2[:], st[:, 2:3], None, ALU.mult))
                VE.wait_ge(vC.h, vC.n)
                vC.inc(VE.tensor_tensor(xf16[:], xf16[:], cx.P_n2t[:], ALU.mult))
                VE.wait_ge(vC.h, vC.n)
                wXF = vC.n
                TE.wait_ge(vC.h, wXF)
                if t == 0:
                    TE.wait_ge(dW.h, dW.n)
                for k in range(8):
                    ks = k % 2
                    TE.wait_ge(vC.h, mk_tp[ks])
                    tC.inc(TE.matmul(ptp[ks][:, 0:128], xf16[:, k * 128:(k + 1) * 128],
                                     cx.P_idt16[:], is_transpose=True, start=True, stop=True))
                    VE.wait_ge(tC.h, tC.n)
                    vC.inc(VE.tensor_copy(xfd[bi][:, k, :], ptp[ks][:, 0:128]))
                    mk_tp[ks] = vC.n
                wXD = vC.n
                SY.wait_ge(vC.h, wXD)
                dS.inc(SY.dma_start(
                    out=cx.xf_dm.rearrange("(k p) T2 -> p k T2", p=128)[:, :, t * 128:(t + 1) * 128],
                    in_=xfd[bi][:]), 16)
                TE.wait_ge(vC.h, wXD)
                TE.wait_ge(vC.h, mk_pg)
                for k in range(8):
                    tC.inc(TE.matmul(pg[:, 0:4], xfd[bi][:, k, :], gw_sb[:, k, :],
                                     start=(k == 0), stop=(k == 7)))
                wG = tC.n
                VE.wait_ge(tC.h, wG)
                vC.inc(VE.reduce_max(st[:, 3:4], pg[:, 0:4], axis=AX.X, negate=True))
                VE.wait_ge(vC.h, vC.n)
                SC.wait_ge(vC.h, vC.n)
                sC.inc(SC.activation(gate[:, 0:4], pg[:, 0:4], AF.Exp,
                                     bias=st[:, 3:4], accum_out=gate[:, 4:5]))
                mk_pg = sC.n
                VE.wait_ge(sC.h, sC.n)
                vC.inc(VE.reciprocal(gate[:, 5:6], gate[:, 4:5]))
                VE.wait_ge(vC.h, vC.n)
                vC.inc(VE.tensor_scalar(gate[:, 0:4], gate[:, 0:4], gate[:, 5:6], None, ALU.mult))
                VE.wait_ge(vC.h, vC.n)
                vC.inc(VE.tensor_tensor(gm[:, 0:16].rearrange("p (e j) -> p e j", j=4),
                                        gate[:, 0:4].unsqueeze(1).broadcast_to([128, 4, 4]),
                                        gate[:, 0:4].unsqueeze(2).broadcast_to([128, 4, 4]),
                                        ALU.is_gt))
                vC.inc(VE.tensor_tensor(gm[:, 16:32].rearrange("p (e j) -> p e j", j=4),
                                        gate[:, 0:4].unsqueeze(1).broadcast_to([128, 4, 4]),
                                        gate[:, 0:4].unsqueeze(2).broadcast_to([128, 4, 4]),
                                        ALU.is_equal))
                VE.wait_ge(vC.h, vC.n)
                vC.inc(VE.tensor_tensor(gm[:, 16:32], gm[:, 16:32], cx.P_ltm[:], ALU.mult))
                VE.wait_ge(vC.h, vC.n)
                vC.inc(VE.tensor_tensor(gm[:, 0:16], gm[:, 0:16], gm[:, 16:32], ALU.add))
                VE.wait_ge(vC.h, vC.n)
                vC.inc(VE.tensor_reduce(gate[:, 8:12].unsqueeze(2),
                                        gm[:, 0:16].rearrange("p (e j) -> p e j", j=4),
                                        AX.X, ALU.add))
                VE.wait_ge(vC.h, vC.n)
                vC.inc(VE.tensor_scalar(gate[:, 12:16], gate[:, 8:12], 1.5, None, ALU.is_le))
                VE.wait_ge(vC.h, vC.n)
                vC.inc(VE.tensor_tensor(gate[:, 16:20], gate[:, 0:4], gate[:, 12:16], ALU.mult))
                VE.wait_ge(vC.h, vC.n)
                vC.inc(VE.reduce_sum(gate[:, 20:21], gate[:, 16:20], axis=AX.X))
                VE.wait_ge(vC.h, vC.n)
                vC.inc(VE.tensor_scalar(gate[:, 21:22], gate[:, 20:21], 1e-8, None, ALU.add))
                VE.wait_ge(vC.h, vC.n)
                vC.inc(VE.reciprocal(gate[:, 22:23], gate[:, 21:22]))
                VE.wait_ge(vC.h, vC.n)
                vC.inc(VE.tensor_scalar(wt4[bi][:], gate[:, 16:20], gate[:, 22:23], None, ALU.mult))
                SY.wait_ge(vC.h, vC.n)
                dS.inc(SY.dma_start(out=cx.wts_tm[t * 128:(t + 1) * 128, :], in_=wt4[bi][:]), 16)
                iter_v.append(vC.n); iter_s.append(sC.n)
            SY.wait_ge(dS.h, dS.n)
            ies.close()
        blk.sync(emit)


def _s6(cx, name, kind):
    nc = cx.nc
    SY, TE, VE, SC, GP = cx.SY, cx.TE, cx.VE, cx.SC, cx.GP
    with nc.Block(name) as blk:
        def emit(_):
            ies = ExitStack()
            dW = CT(nc, ies, f"{name}_dW"); dL = CT(nc, ies, f"{name}_dL")
            dS = CT(nc, ies, f"{name}_dS")
            tC = CT(nc, ies, f"{name}_tC"); vC = CT(nc, ies, f"{name}_vC")
            sC = CT(nc, ies, f"{name}_sC")
            sbt = lambda n_, s, d: ies.enter_context(nc.sbuf_tensor(name + n_, list(s), d))
            pst = lambda n_, s: ies.enter_context(nc.psum_tensor(name + n_, list(s), F32))
            eidx = {"e0": 0, "e1": 1, "e2b": 2, "e3": 3}.get(kind)
            rr = lambda ap_: ap_.rearrange("(k p) n -> p k n", p=128)
            if kind == "e0":
                wup = sbt("w_up", [128, 8, HID], BF16)
                wdn = sbt("w_dn", [128, 16, D - MZ], BF16)
                dW.inc(GP.dma_start(out=wup[:], in_=rr(cx.e0_up_t)), 16)
                dW.inc(GP.dma_start(out=wdn[:], in_=rr(cx.e0_dn_t[:, MZ:])), 16)
            elif kind == "e1":
                wup = sbt("w_u1", [128, 8, HID], BF16)
                wu2 = sbt("w_u2", [128, 8, HID], BF16)
                wdn = sbt("w_dn", [128, 16, D - MZ], BF16)
                dW.inc(GP.dma_start(out=wup[:], in_=rr(cx.e1_w1_t)), 16)
                dW.inc(GP.dma_start(out=wu2[:], in_=rr(cx.e1_w2_t)), 16)
                dW.inc(GP.dma_start(out=wdn[:], in_=rr(cx.e1_dn_t[:, MZ:])), 16)
            elif kind == "e3":
                wd0 = sbt("w_d0", [128, 8, D], BF16)
                wup = sbt("w_up", [128, 8, HID], BF16)
                wdn = sbt("w_dn", [128, 16, D - MZ], BF16)
                dW.inc(GP.dma_start(out=wd0[:], in_=rr(cx.e3_dn_t)), 16)
                dW.inc(GP.dma_start(out=wup[:], in_=rr(cx.e3_up_t)), 16)
                dW.inc(GP.dma_start(out=wdn[:], in_=rr(cx.e3_out_t[:, MZ:])), 16)
            elif kind == "e2a":
                wl1 = sbt("w_l1", [128, 8, HID], BF16)
                wl2 = sbt("w_l2", [128, 16, HID], BF16)
                dW.inc(GP.dma_start(out=wl1[:], in_=rr(cx.e2_l1_t)), 16)
                dW.inc(GP.dma_start(out=wl2[:], in_=rr(cx.e2_l2_t)), 16)
            else:
                wl3 = sbt("w_l3", [128, 16, HID], BF16)
                wdn = sbt("w_dn", [128, 16, D - MZ], BF16)
                dW.inc(GP.dma_start(out=wl3[:], in_=rr(cx.e2_l3_t)), 16)
                dW.inc(GP.dma_start(out=wdn[:], in_=rr(cx.e2_l4_t[:, MZ:])), 16)
            xfb = [sbt(f"xf{i}", [128, 8, 512], BF16) for i in range(2)]
            h2in = [sbt(f"h2i{i}", [128, 16, 512], BF16) for i in range(2)] if kind == "e2b" else None
            h1 = sbt("h1", [128, 16, 512], BF16)
            h1b = sbt("h1b", [128, 16, 512], BF16) if kind == "e2a" else None
            h0 = sbt("h0", [128, 8, 512], BF16) if kind == "e3" else None
            wtb = [sbt(f"wt{i}", [128, 4, 4], F32) for i in range(2)] if kind != "e2a" else None
            nab = 1 if kind == "e2b" else 2
            acb = [sbt(f"ac{i}", [128, 4, D - MZ], F32) for i in range(nab)] if kind != "e2a" else None
            pu = [pst(f"pu{i}", [128, 512]) for i in range(2)]
            pd = [pst(f"pd{i}", [128, 512]) for i in range(2)]
            NGR = T // 512
            iter_end = []
            pum = [[0, 0], [0, 0]]
            pdm = [0, 0]
            ac_mark = [0, 0]
            for gidx in range(NGR):
                bi = gidx % 2
                r0 = gidx * 512
                if gidx >= 2:
                    SY.wait_ge(tC.h, iter_end[gidx - 2][0])
                    SY.wait_ge(vC.h, iter_end[gidx - 2][1])
                if kind == "e2b":
                    dL.inc(SY.dma_start(
                        out=h2in[bi][:],
                        in_=cx.h2_dm.rearrange("(k p) T2 -> p k T2", p=128)[:, :, r0:r0 + 512]), 16)
                else:
                    dL.inc(SY.dma_start(
                        out=xfb[bi][:],
                        in_=cx.xf_dm.rearrange("(k p) T2 -> p k T2", p=128)[:, :, r0:r0 + 512]), 16)
                if kind != "e2a":
                    dL.inc(SY.dma_start(
                        out=wtb[bi][:],
                        in_=cx.wts_tm[r0:r0 + 512, :].rearrange("(s p) e -> p s e", p=128)), 16)
                    SY.wait_ge(dS.h, ac_mark[bi % nab])
                    dL.inc(SY.dma_start(
                        out=acb[bi % nab][:],
                        in_=cx.acc_tm[r0:r0 + 512, :].rearrange("(s p) n -> p s n", p=128)), 16)
                wL = dL.n
                TE.wait_ge(dL.h, wL)
                if gidx == 0:
                    TE.wait_ge(dW.h, dW.n)
                def up_chain(wsb, nkb, src_idx, dst_sb, act, nm=16, mul_sb=None):
                    for m in range(nm):
                        ms = m % 2
                        TE.wait_ge(sC.h, pum[ms][0]); TE.wait_ge(vC.h, pum[ms][1])
                        for k in range(nkb):
                            tC.inc(TE.matmul(pu[ms][:], wsb[:, k, m * 128:(m + 1) * 128],
                                             src_idx(k), start=(k == 0), stop=(k == nkb - 1)))
                        if act is not None:
                            SC.wait_ge(tC.h, tC.n)
                            sC.inc(SC.activation(dst_sb[:, m, :], pu[ms][:], act))
                            pum[ms][0] = sC.n
                        elif mul_sb is not None:
                            VE.wait_ge(tC.h, tC.n)
                            vC.inc(VE.tensor_tensor(dst_sb[:, m, :], mul_sb[:, m, :],
                                                    pu[ms][:], ALU.mult))
                            pum[ms][1] = vC.n
                        else:
                            VE.wait_ge(tC.h, tC.n)
                            vC.inc(VE.tensor_copy(dst_sb[:, m, :], pu[ms][:]))
                            pum[ms][1] = vC.n
                if kind == "e0":
                    up_chain(wup, 8, lambda k: xfb[bi][:, k, :], h1, AF.Gelu)
                elif kind == "e1":
                    up_chain(wup, 8, lambda k: xfb[bi][:, k, :], h1, AF.Silu)
                    TE.wait_ge(sC.h, sC.n)
                    up_chain(wu2, 8, lambda k: xfb[bi][:, k, :], h1, None, mul_sb=h1)
                elif kind == "e3":
                    up_chain(wd0, 8, lambda k: xfb[bi][:, k, :], h0, AF.Gelu, nm=8)
                    TE.wait_ge(sC.h, sC.n)
                    up_chain(wup, 8, lambda k: h0[:, k, :], h1, AF.Gelu)
                elif kind == "e2a":
                    up_chain(wl1, 8, lambda k: xfb[bi][:, k, :], h1b, AF.Gelu)
                    TE.wait_ge(sC.h, sC.n)
                    up_chain(wl2, 16, lambda k: h1b[:, k, :], h1, AF.Gelu)
                    SY.wait_ge(sC.h, sC.n)
                    dS.inc(SY.dma_start(
                        out=cx.h2_dm.rearrange("(k p) T2 -> p k T2", p=128)[:, :, r0:r0 + 512],
                        in_=h1[:]), 16)
                    iter_end.append((tC.n, vC.n))
                    continue
                else:
                    up_chain(wl3, 16, lambda k: h2in[bi][:, k, :], h1, AF.Gelu)
                TE.wait_ge(sC.h, sC.n); TE.wait_ge(vC.h, vC.n)
                for msub in range(4):
                    for n in range(2):
                        w0, w1 = n * 512, min((n + 1) * 512, D - MZ)
                        TE.wait_ge(vC.h, pdm[n])
                        for k in range(16):
                            tC.inc(TE.matmul(pd[n][:, 0:w1 - w0],
                                             h1[:, k, msub * 128:(msub + 1) * 128],
                                             wdn[:, k, w0:w1], start=(k == 0), stop=(k == 15)))
                        VE.wait_ge(tC.h, tC.n)
                        if msub == 0 and n == 0:
                            VE.wait_ge(dL.h, wL)
                        vC.inc(VE.scalar_tensor_tensor(acb[bi % nab][:, msub, w0:w1],
                                                       pd[n][:, 0:w1 - w0],
                                                       wtb[bi][:, msub, eidx:eidx + 1],
                                                       acb[bi % nab][:, msub, w0:w1],
                                                       ALU.mult, ALU.add))
                        pdm[n] = vC.n
                SY.wait_ge(vC.h, vC.n)
                dS.inc(SY.dma_start(out=cx.acc_tm[r0:r0 + 512, :]
                                    .rearrange("(s p) n -> p s n", p=128),
                                    in_=acb[bi % nab][:]), 16)
                ac_mark[bi % nab] = dS.n
                iter_end.append((tC.n, vC.n))
            SY.wait_ge(dS.h, dS.n)
            ies.close()
        blk.sync(emit)


def _s7(cx):
    nc = cx.nc
    SY, VE = cx.SY, cx.VE
    with nc.Block("s7") as blk:
        def emit(_):
            ies = ExitStack()
            dL = CT(nc, ies, "s7_dL"); dS = CT(nc, ies, "s7_dS"); vC = CT(nc, ies, "s7_vC")
            xb = [ies.enter_context(nc.sbuf_tensor(f"s7_x{i}", [128, D], F32)) for i in range(2)]
            mb = [ies.enter_context(nc.sbuf_tensor(f"s7_m{i}", [128, D], F32)) for i in range(2)]
            nb = [ies.enter_context(nc.sbuf_tensor(f"s7_n{i}", [128, D], F32)) for i in range(2)]
            ab = [ies.enter_context(nc.sbuf_tensor(f"s7_a{i}", [128, D - MZ], F32)) for i in range(2)]
            ob = [ies.enter_context(nc.sbuf_tensor(f"s7_o{i}", [128, D], F32)) for i in range(2)]
            iter_v = []
            for t in range(NT):
                bi = t % 2
                if t >= 2:
                    SY.wait_ge(vC.h, iter_v[t - 2])
                dL.inc(SY.dma_start(out=xb[bi][:], in_=cx.x[t * 128:(t + 1) * 128, :]), 16)
                dL.inc(SY.dma_start(out=mb[bi][:], in_=cx.mamba_tm[t * 128:(t + 1) * 128, :]), 16)
                dL.inc(SY.dma_start(out=nb[bi][:], in_=cx.xn_tm32[t * 128:(t + 1) * 128, :]), 16)
                dL.inc(SY.dma_start(out=ab[bi][:], in_=cx.acc_tm[t * 128:(t + 1) * 128, :]), 16)
                wL = dL.n
                VE.wait_ge(dL.h, wL)
                vC.inc(VE.tensor_tensor(ob[bi][:], xb[bi][:], mb[bi][:], ALU.add))
                VE.wait_ge(vC.h, vC.n)
                vC.inc(VE.tensor_tensor(ob[bi][:], ob[bi][:], nb[bi][:], ALU.add))
                VE.wait_ge(vC.h, vC.n)
                vC.inc(VE.tensor_tensor(ob[bi][:, MZ:], ob[bi][:, MZ:], ab[bi][:], ALU.add))
                SY.wait_ge(vC.h, vC.n)
                dS.inc(SY.dma_start(out=cx.out[t * 128:(t + 1) * 128, :], in_=ob[bi][:]), 16)
                iter_v.append(vC.n)
            SY.wait_ge(dS.h, dS.n)
            ies.close()
        blk.sync(emit)


def build(dbg=False, stages=("s0", "s1", "s2", "s3", "s4", "s5", "s6", "s7")):
    nc = bass.Bass("TRN2", target_bir_lowering=False)
    cx = _declare(nc, dbg)
    _init_stage(cx)
    if "s0" in stages: _s0(cx)
    if "s1" in stages: _s1(cx)
    if "s2" in stages: _s2(cx)
    if "s3" in stages: _s3(cx)
    if "s4" in stages: _s4(cx)
    if "s4" in stages: _s4b(cx)
    if "s5" in stages: _s5(cx)
    if "s6" in stages:
        _s6(cx, "s6e1", "e1")
        _s6(cx, "s6e0", "e0")
        _s6(cx, "s6e3", "e3")
        _s6(cx, "s6e2a", "e2a")
        _s6(cx, "s6e2b", "e2b")
    if "s7" in stages: _s7(cx)
    cx.es.close()
    return nc


_CACHE = {}

def kernel(**inputs):
    consts = _host_consts()
    if "nc" not in _CACHE:
        _CACHE["nc"] = build(dbg=False)
    nc = _CACHE["nc"]
    in_maps = []
    for core in range(NCORES):
        m = dict(_host_layout(inputs, core))
        m.update(consts)
        in_maps.append(m)
    res = run_bass_kernel_spmd(nc, in_maps, core_ids=list(range(NCORES)))
    outs = [np.asarray(r["out"], np.float32).reshape(BPC, L, D) for r in res.results]
    return np.concatenate(outs, axis=0)
